# version 02
# baseline (speedup 1.0000x reference)
import numpy as np
import ml_dtypes
import contextlib
import concourse.bass as bass
import concourse.mybir as mybir
from concourse.bass_utils import run_bass_kernel_spmd

F32 = mybir.dt.float32
BF16 = mybir.dt.bfloat16
AF = mybir.ActivationFunctionType
ALU = mybir.AluOpType
AX = mybir.AxisListType
bfdt = ml_dtypes.bfloat16

B, T, C = 2, 8192, 1024
QT = 512
NT = 4
TOK = QT * NT
NH1, HD1, L1 = 16, 64, 32
NH2, HD2, HH2, L2 = 16, 64, 32, 256
ER = 7
SC1 = 1.0 / np.sqrt(np.float32(L1))
SC2s = 1.0 / np.sqrt(np.float32(HD2))

ENGINES = ("sync", "tensor", "scalar", "vector", "gpsimd")

KST0, VST0, PT0, MSK0, YH0 = 0, 4096, 8192, 10240, 18432
WOFF = dict(wkvd=0, wqd=2048, wku=4096, wropek=5120, wvu=9216)
WQQ0 = 2048
WRT0 = 7296

CO = dict(ident=0, ones1=128, esel1T=132, ones2=260, esel2T=262, onesc=390,
          l2oT=391, q2lT=455, kv2lT=487, oneh1=519, oneh2=535, onesrb=539,
          sele=668)
FO = dict(r1T=0, r2T=128, wqkT=256, identf=352, onesr=480, l2oT=608, q2lT=672,
          kv2lT=704)
BO = dict(bqkv=288, bq2l=312, bkv2l=313, bwqk=314, bl2o=315, bcproj=316,
          wln1=324, wln2=332, brout=340)
ACC0 = 1024


class Bld:
    def __init__(self):
        self.steps = {e: [] for e in ENGINES}
        self.cnt = {}
        self.sems = {}

    def emit(self, engn, fn):
        self.steps[engn].append(fn)

    def wait(self, engn, sem, val=None):
        v = self.cnt[sem] if val is None else val
        assert v <= self.cnt[sem], f"wait on future {sem} {v}>{self.cnt[sem]}"
        if v <= 0:
            return
        h = self.sems[sem]
        self.emit(engn, lambda eng, h=h, v=v: eng.wait_ge(h, v))

    def step(self, engn, fn, sem=None, amt=1):
        if sem is None:
            self.emit(engn, fn)
            return None
        h = self.sems[sem]
        self.cnt[sem] += amt
        self.emit(engn, lambda eng, fn=fn, h=h, amt=amt: fn(eng).then_inc(h, amt))
        return self.cnt[sem]


def bank(PS, i, n=512):
    return PS[:, 512 * i:512 * i + n]


def build_nc():
    nc = bass.Bass("TRN2", target_bir_lowering=False, debug=False)
    I = {}

    def inp(name, shape, dt=F32):
        I[name] = nc.dram_tensor(name, list(shape), dt, kind="ExternalInput").ap()

    inp("x", (C, TOK))
    inp("cos1", (128, TOK)); inp("sin1", (128, TOK))
    inp("cos2", (128, TOK)); inp("sin2", (128, TOK))
    inp("mask", (16 * 128, QT), BF16)
    inp("wqkv", (C, 3 * C)); inp("bqkv", (128, 24))
    inp("q2lT", (128, 32)); inp("kv2lT", (128, 32))
    inp("bq2l", (128, 1)); inp("bkv2l", (128, 1))
    inp("wqkT", (128, 32)); inp("bwqk", (128, 1))
    inp("l2oT", (128, 64)); inp("bl2o", (128, 1))
    inp("cprojT", (C, C)); inp("bcproj", (128, 8))
    inp("wln1", (128, 8)); inp("wln2", (128, 8))
    inp("wkvd", (C, 256)); inp("wqd", (C, 256))
    inp("wku", (256, 512)); inp("wqq", (256, 1024))
    inp("wropek", (C, 512)); inp("wvu", (256, 1024))
    inp("wo", (C, C))
    inp("wgs", (C, C)); inp("wus", (C, C)); inp("wds", (C, C))
    inp("wgr", (ER, C, C)); inp("wur", (ER, C, C)); inp("wdr", (ER, C, C))
    inp("wrout", (C, ER)); inp("brout", (ER, 1))
    inp("identb", (128, 128), BF16)
    inp("identf", (128, 128))
    inp("r1T", (128, 128)); inp("r2T", (128, 128))
    inp("ones1", (128, 4), BF16); inp("esel1T", (4, 128), BF16)
    inp("ones2", (128, 2), BF16); inp("esel2T", (2, 128), BF16)
    inp("onesc", (128, 1), BF16); inp("onesr", (1, 128))
    inp("oneh1", (128, 16), BF16); inp("oneh2", (128, 4), BF16)
    inp("onesrb", (1, 128), BF16); inp("sele", (ER, ER * 128), BF16)

    y_out = nc.dram_tensor("y", [C, TOK], F32, kind="ExternalOutput").ap()

    ag = {}
    ag["ag1k_i"] = nc.dram_tensor("ag1k_i", [512, TOK], BF16)
    ag["ag1k_o"] = nc.dram_tensor("ag1k_o", [4 * 512, TOK], BF16)
    ag["ag1v_i"] = nc.dram_tensor("ag1v_i", [TOK, 512], BF16)
    ag["ag1v_o"] = nc.dram_tensor("ag1v_o", [4 * TOK, 512], BF16)
    ag["ag2k_i"] = nc.dram_tensor("ag2k_i", [1024, TOK], BF16)
    ag["ag2k_o"] = nc.dram_tensor("ag2k_o", [4 * 1024, TOK], BF16)
    ag["ag2v_i"] = nc.dram_tensor("ag2v_i", [TOK, 1024], BF16)
    ag["ag2v_o"] = nc.dram_tensor("ag2v_o", [4 * TOK, 1024], BF16)
    ag["RG"] = [[0, 1, 2, 3], [4, 5, 6, 7]]

    bld = Bld()
    with contextlib.ExitStack() as ctx:
        E = ctx.enter_context
        U1 = E(nc.sbuf_tensor("U1", [128, 24576], BF16))
        XB = E(nc.sbuf_tensor("XB", [128, 8 * QT], BF16))
        QF = E(nc.sbuf_tensor("QF", [128, 8192], BF16))
        WR = E(nc.sbuf_tensor("WR", [128, 8192], BF16))
        SCb = E(nc.sbuf_tensor("SCb", [128, 3072], BF16))
        CONb = E(nc.sbuf_tensor("CONb", [128, 1664], BF16))
        XR = E(nc.sbuf_tensor("XR", [128, 8 * TOK], F32))
        TAB = E(nc.sbuf_tensor("TAB", [128, 5120], F32))
        ST = E(nc.sbuf_tensor("ST", [128, 2048], F32))
        SC2 = E(nc.sbuf_tensor("SC2", [128, 2048], F32))
        CONf = E(nc.sbuf_tensor("CONf", [128, 1024], F32))
        PS = E(nc.psum_tensor("PS", [128, 3584], F32))
        PSB = E(nc.psum_tensor("PSB", [128, 1024], BF16))

        for s in ("dx", "dt", "dm", "dk0", "dk1", "da", "do", "cc", "dwq",
                  "dwa0", "dwa1", "dwg", "dwd", "dwr", "pe", "act", "dve", "pool"):
            bld.sems[s] = E(nc.semaphore(s))
            bld.cnt[s] = 0

        M = dict(U1=U1, XB=XB, QF=QF, WR=WR, SCb=SCb, CONb=CONb, XR=XR, TAB=TAB,
                 ST=ST, SC2=SC2, CONf=CONf, PS=PS, PSB=PSB)
        build_program(nc, bld, I, y_out, M, ag)

        block = E(nc.Block())

        @block.sync
        def _(eng):
            for fn in bld.steps["sync"]:
                fn(eng)

        @block.tensor
        def _(eng):
            for fn in bld.steps["tensor"]:
                fn(eng)

        @block.scalar
        def _(eng):
            for fn in bld.steps["scalar"]:
                fn(eng)

        @block.vector
        def _(eng):
            for fn in bld.steps["vector"]:
                fn(eng)

        @block.gpsimd
        def _(eng):
            for fn in bld.steps["gpsimd"]:
                fn(eng)

    return nc


def emit_stats_norm(bld, M, t, wcol, mode):
    XR, XB, SCb, ST, SC2, CONb, CONf, PS = (M["XR"], M["XB"], M["SCb"], M["ST"],
                                            M["SC2"], M["CONb"], M["CONf"], M["PS"])
    st, wt = bld.step, bld.wait
    if mode != "rmsxb":
        wt("gpsimd", "dve", bld.cnt["dve"])
        wt("gpsimd", "pe", bld.cnt["pe"])
        for oc in range(8):
            st("gpsimd", lambda e, oc=oc, t=t: e.tensor_copy(
                out=XB[:, 512 * oc:512 * oc + 512],
                in_=XR[:, TOK * oc + QT * t:TOK * oc + QT * t + 512]), "pool", 1)
        wt("tensor", "pool", bld.cnt["pool"])
        wt("scalar", "pool", bld.cnt["pool"])
    else:
        wt("tensor", "dve", bld.cnt["dve"])
        wt("scalar", "dve", bld.cnt["dve"])
    wt("scalar", "pe", bld.cnt["pe"])
    if mode == "ln":
        for oc in range(8):
            st("tensor", lambda e, oc=oc: e.matmul(
                PS[0:1, 512 * 6:512 * 6 + 512], CONb[:, CO["onesc"]:CO["onesc"] + 1],
                XB[:, 512 * oc:512 * oc + 512], start=(oc == 0), stop=(oc == 7)), "pe", 1)
    for oc in range(8):
        if oc >= 2:
            wt("scalar", "pe", bld.cnt["pe"])
        st("scalar", lambda e, oc=oc: e.activation(
            out=SCb[:, 512 * (oc % 2):512 * (oc % 2) + 512],
            in_=XB[:, 512 * oc:512 * oc + 512], func=AF.Square), "act", 1)
        a_sq = bld.cnt["act"]
        wt("tensor", "act", a_sq)
        st("tensor", lambda e, oc=oc: e.matmul(
            PS[0:1, 512 * 5:512 * 5 + 512], CONb[:, CO["onesc"]:CO["onesc"] + 1],
            SCb[:, 512 * (oc % 2):512 * (oc % 2) + 512], start=(oc == 0), stop=(oc == 7)),
            "pe", 1)
    pe_ssq = bld.cnt["pe"]
    wt("scalar", "pe", pe_ssq)
    if mode == "ln":
        st("scalar", lambda e: e.activation(out=ST[0:1, 0:512],
                                            in_=PS[0:1, 512 * 6:512 * 6 + 512],
                                            func=AF.Copy, scale=1.0 / 1024.0), "act", 1)
        st("scalar", lambda e: e.activation(out=ST[0:1, 512:1024],
                                            in_=PS[0:1, 512 * 5:512 * 5 + 512],
                                            func=AF.Copy, scale=1.0 / 1024.0), "act", 1)
        act_ms = bld.cnt["act"]
        wt("vector", "act", act_ms)
        st("vector", lambda e: e.tensor_mul(out=ST[0:1, 1024:1536], in0=ST[0:1, 0:512],
                                            in1=ST[0:1, 0:512]), "dve", 1)
        wt("vector", "dve", bld.cnt["dve"])
        st("vector", lambda e: e.tensor_sub(out=ST[0:1, 512:1024], in0=ST[0:1, 512:1024],
                                            in1=ST[0:1, 1024:1536]), "dve", 1)
        wt("vector", "dve", bld.cnt["dve"])
        st("vector", lambda e: e.tensor_scalar_add(out=ST[0:1, 512:1024],
                                                   in0=ST[0:1, 512:1024], scalar1=1e-5),
           "dve", 1)
        dve_v = bld.cnt["dve"]
        wt("scalar", "dve", dve_v)
        st("scalar", lambda e: e.activation(out=ST[0:1, 512:1024], in_=ST[0:1, 512:1024],
                                            func=AF.Ln), "act", 1)
        wt("scalar", "act", bld.cnt["act"])
        st("scalar", lambda e: e.activation(out=ST[0:1, 512:1024], in_=ST[0:1, 512:1024],
                                            func=AF.Exp, scale=-0.5), "act", 1)
        act_rs = bld.cnt["act"]
        wt("vector", "act", act_rs)
        st("vector", lambda e: e.tensor_mul(out=ST[0:1, 0:512], in0=ST[0:1, 0:512],
                                            in1=ST[0:1, 512:1024]), "dve", 1)
        dve_mr = bld.cnt["dve"]
        wt("tensor", "dve", dve_mr)
        st("tensor", lambda e: e.matmul(PS[:, 512 * 6:512 * 6 + 512],
                                        CONf[0:1, FO["onesr"]:FO["onesr"] + 128],
                                        ST[0:1, 512:1024], start=True, stop=True), "pe", 1)
        st("tensor", lambda e: e.matmul(PS[:, 512 * 5:512 * 5 + 512],
                                        CONf[0:1, FO["onesr"]:FO["onesr"] + 128],
                                        ST[0:1, 0:512], start=True, stop=True), "pe", 1)
        pe_bc = bld.cnt["pe"]
        wt("vector", "pe", pe_bc)
        st("vector", lambda e: e.tensor_copy(out=SC2[:, 1024:1536],
                                             in_=PS[:, 512 * 6:512 * 6 + 512]), "dve", 1)
        st("vector", lambda e: e.tensor_copy(out=SC2[:, 1536:2048],
                                             in_=PS[:, 512 * 5:512 * 5 + 512]), "dve", 1)
        wt("vector", "dve", bld.cnt["dve"])
        for oc in range(8):
            st("vector", lambda e, oc=oc, t=t: e.tensor_mul(
                out=XR[:, TOK * oc + QT * t:TOK * oc + QT * t + 512],
                in0=XR[:, TOK * oc + QT * t:TOK * oc + QT * t + 512],
                in1=SC2[:, 1024:1536]), "dve", 1)
            wt("vector", "dve", bld.cnt["dve"])
            st("vector", lambda e, oc=oc, t=t: e.tensor_sub(
                out=XR[:, TOK * oc + QT * t:TOK * oc + QT * t + 512],
                in0=XR[:, TOK * oc + QT * t:TOK * oc + QT * t + 512],
                in1=SC2[:, 1536:2048]), "dve", 1)
    else:
        st("scalar", lambda e: e.activation(out=ST[0:1, 512:1024],
                                            in_=PS[0:1, 512 * 5:512 * 5 + 512],
                                            func=AF.Sqrt, scale=1.0 / 1024.0), "act", 1)
        act_rt = bld.cnt["act"]
        wt("vector", "act", act_rt)
        st("vector", lambda e: e.tensor_scalar_add(out=ST[0:1, 512:1024],
                                                   in0=ST[0:1, 512:1024], scalar1=1e-5),
           "dve", 1)
        dve_e = bld.cnt["dve"]
        wt("scalar", "dve", dve_e)
        st("scalar", lambda e: e.activation(out=ST[0:1, 512:1024], in_=ST[0:1, 512:1024],
                                            func=AF.Ln), "act", 1)
        wt("scalar", "act", bld.cnt["act"])
        st("scalar", lambda e: e.activation(out=ST[0:1, 512:1024], in_=ST[0:1, 512:1024],
                                            func=AF.Exp, scale=-1.0), "act", 1)
        act_rc = bld.cnt["act"]
        wt("tensor", "act", act_rc)
        st("tensor", lambda e: e.matmul(PS[:, 512 * 6:512 * 6 + 512],
                                        CONf[0:1, FO["onesr"]:FO["onesr"] + 128],
                                        ST[0:1, 512:1024], start=True, stop=True), "pe", 1)
        pe_bc = bld.cnt["pe"]
        wt("vector", "pe", pe_bc)
        st("vector", lambda e: e.tensor_copy(out=SC2[:, 1024:1536],
                                             in_=PS[:, 512 * 6:512 * 6 + 512]), "dve", 1)
        wt("vector", "dve", bld.cnt["dve"])
        for oc in range(8):
            if mode == "rms":
                st("vector", lambda e, oc=oc, t=t, wcol=wcol: e.scalar_tensor_tensor(
                    out=XB[:, 512 * oc:512 * oc + 512],
                    in0=XR[:, TOK * oc + QT * t:TOK * oc + QT * t + 512],
                    scalar=CONf[:, wcol + oc:wcol + oc + 1], in1=SC2[:, 1024:1536],
                    op0=ALU.mult, op1=ALU.mult), "dve", 1)
            else:
                st("vector", lambda e, oc=oc, wcol=wcol: e.scalar_tensor_tensor(
                    out=XB[:, 512 * oc:512 * oc + 512], in0=XB[:, 512 * oc:512 * oc + 512],
                    scalar=CONf[:, wcol + oc:wcol + oc + 1], in1=SC2[:, 1024:1536],
                    op0=ALU.mult, op1=ALU.mult), "dve", 1)


def build_program(nc, bld, I, y_out, M, ag):
    U1, XB, QF, WR = M["U1"], M["XB"], M["QF"], M["WR"]
    SCb, CONb = M["SCb"], M["CONb"]
    XR, TAB, ST, SC2, CONf = M["XR"], M["TAB"], M["ST"], M["SC2"], M["CONf"]
    PS, PSB = M["PS"], M["PSB"]
    st, wt = bld.step, bld.wait

    for cc in range(8):
        st("sync", lambda e, cc=cc: e.dma_start(
            out=XR[:, TOK * cc:TOK * (cc + 1)], in_=I["x"][128 * cc:128 * (cc + 1), :]),
            "dx", 16)
    for nm, sl in (("identb", (CO["ident"], 128, 128)), ("ones1", (CO["ones1"], 4, 128)),
                   ("ones2", (CO["ones2"], 2, 128)), ("onesc", (CO["onesc"], 1, 128)),
                   ("oneh1", (CO["oneh1"], 16, 128)), ("oneh2", (CO["oneh2"], 4, 128))):
        off, w, p = sl
        st("sync", lambda e, nm=nm, off=off, w=w: e.dma_start(
            out=CONb[:, off:off + w], in_=I[nm]), "dx", 16)
    st("sync", lambda e: e.dma_start(out=CONb[0:4, CO["esel1T"]:CO["esel1T"] + 128],
                                     in_=I["esel1T"]), "dx", 16)
    st("sync", lambda e: e.dma_start(out=CONb[0:2, CO["esel2T"]:CO["esel2T"] + 128],
                                     in_=I["esel2T"]), "dx", 16)
    st("sync", lambda e: e.dma_start(out=CONb[0:1, CO["onesrb"]:CO["onesrb"] + 128],
                                     in_=I["onesrb"]), "dx", 16)
    st("sync", lambda e: e.dma_start(out=CONb[0:ER, CO["sele"]:CO["sele"] + ER * 128],
                                     in_=I["sele"]), "dx", 16)
    for nm, off, w in (("r1T", FO["r1T"], 128), ("r2T", FO["r2T"], 128),
                       ("wqkT", FO["wqkT"], 32), ("identf", FO["identf"], 128),
                       ("l2oT", FO["l2oT"], 64), ("q2lT", FO["q2lT"], 32),
                       ("kv2lT", FO["kv2lT"], 32)):
        st("sync", lambda e, nm=nm, off=off, w=w: e.dma_start(
            out=CONf[:, off:off + w], in_=I[nm]), "dx", 16)
    st("sync", lambda e: e.dma_start(out=CONf[0:1, FO["onesr"]:FO["onesr"] + 128],
                                     in_=I["onesr"]), "dx", 16)
    for nm, off, w in (("bqkv", BO["bqkv"], 24), ("bq2l", BO["bq2l"], 1),
                       ("bkv2l", BO["bkv2l"], 1), ("bwqk", BO["bwqk"], 1),
                       ("bl2o", BO["bl2o"], 1), ("bcproj", BO["bcproj"], 8),
                       ("wln1", BO["wln1"], 8), ("wln2", BO["wln2"], 8)):
        st("sync", lambda e, nm=nm, off=off, w=w: e.dma_start(
            out=CONf[:, off:off + w], in_=I[nm]), "dx", 16)
    st("sync", lambda e: e.dma_start(out=CONf[0:ER, BO["brout"]:BO["brout"] + 1],
                                     in_=I["brout"]), "dx", 16)
    dx_pre = bld.cnt["dx"]

    wt("gpsimd", "dx", dx_pre)
    st("gpsimd", lambda e: e.tensor_copy(out=CONb[:, CO["l2oT"]:CO["l2oT"] + 64],
                                         in_=CONf[:, FO["l2oT"]:FO["l2oT"] + 64]), "pool", 1)
    st("gpsimd", lambda e: e.tensor_copy(out=CONb[:, CO["q2lT"]:CO["q2lT"] + 32],
                                         in_=CONf[:, FO["q2lT"]:FO["q2lT"] + 32]), "pool", 1)
    st("gpsimd", lambda e: e.tensor_copy(out=CONb[:, CO["kv2lT"]:CO["kv2lT"] + 32],
                                         in_=CONf[:, FO["kv2lT"]:FO["kv2lT"] + 32]), "pool", 1)
    pool_consts = bld.cnt["pool"]

    for oc in range(24):
        for kc in range(8):
            st("gpsimd", lambda e, oc=oc, kc=kc: e.dma_start(
                out=U1[:, 1024 * oc + 128 * kc:1024 * oc + 128 * kc + 128],
                in_=I["wqkv"][128 * kc:128 * kc + 128, 128 * oc:128 * oc + 128]), "dwq", 16)
    dw_qkv = bld.cnt["dwq"]

    for t in range(NT):
        wt("sync", "dve", bld.cnt["dve"])
        st("sync", lambda e, t=t: e.dma_start(out=TAB[:, 0:512],
                                              in_=I["cos1"][:, QT * t:QT * t + 512]), "dt", 16)
        st("sync", lambda e, t=t: e.dma_start(out=TAB[:, 512:1024],
                                              in_=I["sin1"][:, QT * t:QT * t + 512]), "dt", 16)
        dx_tab = bld.cnt["dt"]
        wt("gpsimd", "dx", dx_pre)
        wt("gpsimd", "pe", bld.cnt["pe"])
        for cc in range(8):
            st("gpsimd", lambda e, t=t, cc=cc: e.tensor_copy(
                out=XB[:, QT * cc:QT * (cc + 1)],
                in_=XR[:, TOK * cc + QT * t:TOK * cc + QT * t + QT]), "pool", 1)
        pool_xb = bld.cnt["pool"]
        wt("tensor", "pool", pool_xb)
        if t == 0:
            wt("tensor", "dwq", dw_qkv)
            wt("tensor", "dx", dx_pre)
        for hg in range(4):
            ocs = [2 * hg, 2 * hg + 1, 8 + 2 * hg, 8 + 2 * hg + 1,
                   16 + 2 * hg, 16 + 2 * hg + 1]
            wt("vector", "da", bld.cnt["da"])
            for i6, oc in enumerate(ocs):
                bk = i6 % 2
                wt("tensor", "dve", bld.cnt["dve"])
                for kc in range(8):
                    st("tensor", lambda e, bk=bk, oc=oc, kc=kc: e.matmul(
                        bank(PS, bk), U1[:, 1024 * oc + 128 * kc:1024 * oc + 128 * (kc + 1)],
                        XB[:, QT * kc:QT * (kc + 1)],
                        start=(kc == 0), stop=(kc == 7)), "pe", 1)
                pe_mm = bld.cnt["pe"]
                wt("vector", "pe", pe_mm)
                st("vector", lambda e, bk=bk, i6=i6, oc=oc: e.tensor_scalar_add(
                    out=SCb[:, 512 * i6:512 * (i6 + 1)], in0=bank(PS, bk),
                    scalar1=CONf[:, BO["bqkv"] + oc:BO["bqkv"] + oc + 1]),
                    "dve", 1)
            dve_qkv = bld.cnt["dve"]
            wt("tensor", "dve", dve_qkv)
            if t == 0 and hg == 0:
                wt("tensor", "pool", pool_consts)
            for src, wname, bnk in ((0, "q2lT", 2), (2, "kv2lT", 3), (4, "kv2lT", 4)):
                for hh in range(4):
                    rb = 64 * (hh % 2)
                    cb = 32 * hh
                    ch = src + hh // 2
                    st("tensor", lambda e, rb=rb, cb=cb, ch=ch, wname=wname, bnk=bnk, hh=hh: e.matmul(
                        PS[cb:cb + 32, 512 * bnk:512 * bnk + 512],
                        CONb[rb:rb + 64, CO[wname]:CO[wname] + 32],
                        SCb[rb:rb + 64, 512 * ch:512 * ch + 512],
                        start=True, stop=True, tile_position=(rb, cb)), "pe", 1)
            pe_lat = bld.cnt["pe"]
            wt("vector", "pe", pe_lat)
            wt("vector", "pool", bld.cnt["pool"])
            st("vector", lambda e: e.tensor_scalar_add(out=SC2[:, 0:512], in0=bank(PS, 2),
                scalar1=CONf[:, BO["bq2l"]:BO["bq2l"] + 1]), "dve", 1)
            st("vector", lambda e: e.tensor_scalar_add(out=SC2[:, 512:1024], in0=bank(PS, 3),
                scalar1=CONf[:, BO["bkv2l"]:BO["bkv2l"] + 1]), "dve", 1)
            st("vector", lambda e: e.tensor_scalar_add(out=SCb[:, 0:512], in0=bank(PS, 4),
                scalar1=CONf[:, BO["bkv2l"]:BO["bkv2l"] + 1]), "dve", 1)
            act_lat = bld.cnt["dve"]
            for off in (0, 512):
                wt("tensor", "dve", bld.cnt["dve"])
                st("tensor", lambda e, off=off: e.matmul(
                    bank(PS, 5), CONf[:, FO["r1T"]:FO["r1T"] + 128], SC2[:, off:off + 512],
                    start=True, stop=True), "pe", 1)
                pe_rot = bld.cnt["pe"]
                wt("vector", "pe", pe_rot)
                if hg == 0 and off == 0:
                    wt("vector", "dt", dx_tab)
                st("vector", lambda e, off=off: e.tensor_mul(
                    out=SC2[:, 1024 + off:1024 + off + 512], in0=SC2[:, off:off + 512],
                    in1=TAB[:, 0:512]), "dve", 1)
                wt("vector", "dve", bld.cnt["dve"])
                st("vector", lambda e, off=off: e.tensor_mul(
                    out=SC2[:, off:off + 512], in0=bank(PS, 5), in1=TAB[:, 512:1024]), "dve", 1)
                wt("vector", "dve", bld.cnt["dve"])
                st("vector", lambda e, off=off: e.tensor_add(
                    out=SC2[:, off:off + 512], in0=SC2[:, off:off + 512],
                    in1=SC2[:, 1024 + off:1024 + off + 512]), "dve", 1)
            dve_rope = bld.cnt["dve"]
            wt("tensor", "dve", dve_rope)
            for hh in range(4):
                st("tensor", lambda e, hh=hh: e.matmul(
                    PS[32 * hh:32 * hh + 32, 512 * 6:512 * 6 + 512],
                    CONf[32 * hh:32 * hh + 32, FO["wqkT"]:FO["wqkT"] + 32],
                    SC2[32 * hh:32 * hh + 32, 0:512],
                    start=True, stop=True, tile_position=(32 * hh, 32 * hh)), "pe", 1)
            pe_qf = bld.cnt["pe"]
            wt("vector", "pe", pe_qf)
            st("vector", lambda e, hg=hg, t=t: e.tensor_scalar_add(
                out=QF[:, 2048 * hg + 512 * t:2048 * hg + 512 * t + 512], in0=bank(PS, 6),
                scalar1=CONf[:, BO["bwqk"]:BO["bwqk"] + 1]), "dve", 1)
            act_qf = bld.cnt["dve"]
            wt("gpsimd", "dve", dve_rope)
            wt("gpsimd", "da", bld.cnt["da"])
            st("gpsimd", lambda e: e.tensor_copy(out=SCb[:, 512:1024],
                                                 in_=SC2[:, 512:1024]), "pool", 1)
            pool_k = bld.cnt["pool"]
            wt("sync", "pool", pool_k)
            st("sync", lambda e, hg=hg, t=t: e.dma_start(
                out=ag["ag1k_i"].ap()[128 * hg:128 * hg + 128, QT * t:QT * (t + 1)],
                in_=SCb[:, 512:1024]), "da", 16)
            for s in range(4):
                wt("tensor", "dve", act_lat)
                st("tensor", lambda e, s=s: e.transpose(
                    PSB[:, 128 * s:128 * s + 128], SCb[:, 128 * s:128 * s + 128],
                    CONb[:, CO["ident"]:CO["ident"] + 128]), "pe", 1)
            pe_tp = bld.cnt["pe"]
            wt("vector", "pe", pe_tp)
            wt("vector", "da", bld.cnt["da"])
            st("vector", lambda e: e.tensor_copy(out=SCb[:, 1024:1536],
                                                 in_=PSB[:, 0:512]), "dve", 1)
            dve_vt = bld.cnt["dve"]
            wt("sync", "dve", dve_vt)
            st("sync", lambda e, hg=hg, t=t: e.dma_start(
                out=ag["ag1v_i"].ap()[QT * t:QT * t + 512, 128 * hg:128 * hg + 128]
                .rearrange("(s p) f -> p s f", p=128),
                in_=SCb[:, 1024:1536].rearrange("p (s f) -> p s f", s=4)), "da", 16)
            wt("scalar", "dve", act_qf)
            st("scalar", lambda e, hg=hg, t=t: e.activation(
                out=SCb[:, 2048:2560],
                in_=QF[:, 2048 * hg + 512 * t:2048 * hg + 512 * t + 512],
                func=AF.Square), "act", 1)
            act_sq = bld.cnt["act"]
            wt("tensor", "act", act_sq)
            wt("tensor", "dve", bld.cnt["dve"])
            st("tensor", lambda e: e.matmul(
                PS[0:4, 512 * 6:512 * 6 + 512], CONb[:, CO["ones1"]:CO["ones1"] + 4],
                SCb[:, 2048:2560], start=True, stop=True), "pe", 1)
            pe_q2 = bld.cnt["pe"]
            wt("vector", "pe", pe_q2)
            st("vector", lambda e, hg=hg, t=t: e.reduce_max(
                ST[0:4, 64 + 16 * hg + t:64 + 16 * hg + t + 1],
                PS[0:4, 512 * 6:512 * 6 + 512], axis=AX.X), "dve", 1)
    da_A = bld.cnt["da"]

    wt("gpsimd", "da", da_A)
    st("gpsimd", lambda e: e.collective_compute(
        "AllGather", ALU.bypass, replica_groups=ag["RG"],
        ins=[ag["ag1k_i"].ap()], outs=[ag["ag1k_o"].ap()]), "cc", 1)
    wt("gpsimd", "cc", bld.cnt["cc"])
    st("gpsimd", lambda e: e.collective_compute(
        "AllGather", ALU.bypass, replica_groups=ag["RG"],
        ins=[ag["ag1v_i"].ap()], outs=[ag["ag1v_o"].ap()]), "cc", 1)
    cc1 = bld.cnt["cc"]

    wt("gpsimd", "cc", cc1)
    wt("sync", "cc", cc1)
    last_pe_kst = {0: 0, 1: 0}
    for hg in range(4):
        for rr in range(4):
            pp = rr % 2
            wt("sync", "pe", last_pe_kst[pp])
            st("sync", lambda e, hg=hg, rr=rr, pp=pp: e.dma_start(
                out=U1[:, KST0 + 2048 * pp:KST0 + 2048 * pp + 2048],
                in_=ag["ag1k_o"].ap()[512 * rr + 128 * hg:512 * rr + 128 * hg + 128, :]),
                "dk%d" % pp, 16)
            dk_n = bld.cnt["dk%d" % pp]
            wt("scalar", "dk%d" % pp, dk_n)
            for q4 in range(4):
                wt("scalar", "pe", bld.cnt["pe"])
                st("scalar", lambda e, pp=pp, q4=q4: e.activation(
                    out=SCb[:, 512 * (q4 % 2):512 * (q4 % 2) + 512],
                    in_=U1[:, KST0 + 2048 * pp + 512 * q4:KST0 + 2048 * pp + 512 * q4 + 512],
                    func=AF.Square), "act", 1)
                a_n = bld.cnt["act"]
                wt("tensor", "act", a_n)
                wt("tensor", "dve", bld.cnt["dve"])
                st("tensor", lambda e, q4=q4: e.matmul(
                    PS[0:4, 512 * 6:512 * 6 + 512], CONb[:, CO["ones1"]:CO["ones1"] + 4],
                    SCb[:, 512 * (q4 % 2):512 * (q4 % 2) + 512], start=True, stop=True),
                    "pe", 1)
                p_n = bld.cnt["pe"]
                wt("vector", "pe", p_n)
                st("vector", lambda e, hg=hg, rr=rr, q4=q4: e.reduce_max(
                    ST[0:4, 80 + 16 * hg + 4 * rr + q4:81 + 16 * hg + 4 * rr + q4],
                    PS[0:4, 512 * 6:512 * 6 + 512], axis=AX.X), "dve", 1)
            last_pe_kst[pp] = bld.cnt["pe"]
    wt("vector", "dve", bld.cnt["dve"])
    for hg in range(4):
        st("vector", lambda e, hg=hg: e.reduce_max(
            ST[0:4, 32 + hg:33 + hg], ST[0:4, 64 + 16 * hg:64 + 16 * hg + 4], axis=AX.X),
            "dve", 1)
        st("vector", lambda e, hg=hg: e.reduce_max(
            ST[0:4, 40 + hg:41 + hg], ST[0:4, 80 + 16 * hg:80 + 16 * hg + 16], axis=AX.X),
            "dve", 1)
    dve_mx = bld.cnt["dve"]
    wt("scalar", "dve", dve_mx)
    st("scalar", lambda e: e.activation(out=ST[0:4, 48:52], in_=ST[0:4, 32:36],
                                        func=AF.Sqrt), "act", 1)
    st("scalar", lambda e: e.activation(out=ST[0:4, 52:56], in_=ST[0:4, 40:44],
                                        func=AF.Sqrt), "act", 1)
    act_sq2 = bld.cnt["act"]
    wt("vector", "act", act_sq2)
    st("vector", lambda e: e.tensor_mul(out=ST[0:4, 56:60], in0=ST[0:4, 48:52],
                                        in1=ST[0:4, 52:56]), "dve", 1)
    wt("vector", "dve", bld.cnt["dve"])
    st("vector", lambda e: e.tensor_scalar_mul(out=ST[0:4, 56:60], in0=ST[0:4, 56:60],
                                               scalar1=-float(SC1)), "dve", 1)
    wt("vector", "dve", bld.cnt["dve"])
    st("vector", lambda e: e.tensor_copy(out=SCb[0:4, 0:4], in_=ST[0:4, 56:60]), "dve", 1)
    dve_bias = bld.cnt["dve"]
    wt("tensor", "dve", dve_bias)
    for hg in range(4):
        wt("tensor", "dve", bld.cnt["dve"])
        st("tensor", lambda e, hg=hg: e.matmul(
            PS[:, 512 * 6:512 * 6 + 1], CONb[0:4, CO["esel1T"]:CO["esel1T"] + 128],
            SCb[0:4, hg:hg + 1], start=True, stop=True), "pe", 1)
        p_b = bld.cnt["pe"]
        wt("vector", "pe", p_b)
        st("vector", lambda e, hg=hg: e.tensor_copy(
            out=ST[:, 8 + hg:9 + hg], in_=PS[:, 512 * 6:512 * 6 + 1]), "dve", 1)
    dve_bx = bld.cnt["dve"]

    for t in range(NT):
        wt("sync", "dve", bld.cnt["dve"])
        for mi in range(4):
            st("sync", lambda e, mi=mi: e.dma_start(
                out=U1[:, MSK0 + 2048 * mi:MSK0 + 2048 * mi + 2048]
                .rearrange("p (a f) -> p a f", a=4),
                in_=I["mask"][512 * mi:512 * mi + 512, :]
                .rearrange("(a p) f -> p a f", p=128)), "dm", 16)
        dk_msk = bld.cnt["dm"]
        for hg in range(4):
            first = True
            wt("tensor", "dve", bld.cnt["dve"])
            for m2 in range(t + 1):
                pp = m2 % 2
                wt("sync", "pe", last_pe_kst[pp])
                for rr in range(4):
                    st("sync", lambda e, rr=rr, hg=hg, pp=pp, m2=m2: e.dma_start(
                        out=U1[:, KST0 + 2048 * pp + 512 * rr:KST0 + 2048 * pp + 512 * rr + 512],
                        in_=ag["ag1k_o"].ap()[512 * rr + 128 * hg:512 * rr + 128 * hg + 128,
                                              QT * m2:QT * m2 + 512]), "dk%d" % pp, 16)
                    st("sync", lambda e, rr=rr, hg=hg, pp=pp, m2=m2: e.dma_start(
                        out=U1[:, VST0 + 2048 * pp + 512 * rr:VST0 + 2048 * pp + 512 * rr + 512]
                        .rearrange("p (s f) -> p s f", s=4),
                        in_=ag["ag1v_o"].ap()[TOK * rr + QT * m2:TOK * rr + QT * m2 + 512,
                                              128 * hg:128 * hg + 128]
                        .rearrange("(s p) f -> p s f", p=128)), "dk%d" % pp, 16)
                dk_kv = bld.cnt["dk%d" % pp]
                for rr in range(4):
                    for kb in range(4):
                        is_last = (m2 == t and rr == 3 and kb == 3)
                        wt("tensor", "dk%d" % pp, dk_kv)
                        wt("tensor", "act", bld.cnt["act"])
                        for hh in range(4):
                            st("tensor", lambda e, hh=hh, pp=pp, rr=rr, kb=kb, hg=hg, t=t: e.matmul(
                                bank(PS, hh),
                                U1[32 * hh:32 * hh + 32,
                                   KST0 + 2048 * pp + 512 * rr + 128 * kb:
                                   KST0 + 2048 * pp + 512 * rr + 128 * kb + 128],
                                QF[32 * hh:32 * hh + 32,
                                   2048 * hg + 512 * t:2048 * hg + 512 * t + 512],
                                start=True, stop=True, skip_group_check=True,
                                tile_position=(32 * hh, 0)), "pe", 1)
                        pe_s = bld.cnt["pe"]
                        wt("scalar", "pe", pe_s)
                        st("scalar", lambda e, hg=hg: e.activation(
                            out=U1[:, PT0:PT0 + 2048], in_=PS[:, 0:2048], func=AF.Exp,
                            bias=ST[:, 8 + hg:9 + hg], scale=float(SC1)), "act", 1)
                        act_e = bld.cnt["act"]
                        if m2 == t:
                            wt("vector", "act", act_e)
                            if rr == 0 and kb == 0:
                                wt("vector", "dm", dk_msk)
                            for hh in range(4):
                                st("vector", lambda e, hh=hh, rr=rr, kb=kb: e.tensor_mul(
                                    out=U1[:, PT0 + 512 * hh:PT0 + 512 * hh + 512],
                                    in0=U1[:, PT0 + 512 * hh:PT0 + 512 * hh + 512],
                                    in1=U1[:, MSK0 + 512 * (4 * rr + kb):
                                           MSK0 + 512 * (4 * rr + kb) + 512]), "dve", 1)
                            wt("tensor", "dve", bld.cnt["dve"])
                        else:
                            wt("tensor", "act", act_e)
                        for hh in range(4):
                            st("tensor", lambda e, hh=hh, pp=pp, rr=rr, kb=kb, first=first,
                               is_last=is_last: e.matmul(
                                PS[32 * hh:32 * hh + 32, 512 * 4:512 * 4 + 512],
                                U1[:, VST0 + 2048 * pp + 512 * rr + 128 * kb + 32 * hh:
                                   VST0 + 2048 * pp + 512 * rr + 128 * kb + 32 * hh + 32],
                                U1[:, PT0 + 512 * hh:PT0 + 512 * hh + 512],
                                start=first, stop=is_last, skip_group_check=True,
                                tile_position=(0, 32 * hh)), "pe", 1)
                        for hh in range(4):
                            st("tensor", lambda e, hh=hh, first=first, is_last=is_last: e.matmul(
                                PS[0:4, 512 * 5:512 * 5 + 512],
                                CONb[:, CO["oneh1"] + 4 * hh:CO["oneh1"] + 4 * hh + 4],
                                U1[:, PT0 + 512 * hh:PT0 + 512 * hh + 512],
                                start=(first and hh == 0), stop=(is_last and hh == 3),
                                skip_group_check=True), "pe", 1)
                        first = False
                last_pe_kst[pp] = bld.cnt["pe"]
            pe_attn = bld.cnt["pe"]
            wt("scalar", "pe", pe_attn)
            st("scalar", lambda e: e.activation(out=ST[0:4, 1024:1536],
                                                in_=PS[0:4, 512 * 5:512 * 5 + 512],
                                                func=AF.Ln), "act", 1)
            wt("scalar", "act", bld.cnt["act"])
            st("scalar", lambda e: e.activation(out=ST[0:4, 1024:1536], in_=ST[0:4, 1024:1536],
                                                func=AF.Exp, scale=-1.0), "act", 1)
            act_rc = bld.cnt["act"]
            wt("vector", "act", act_rc)
            st("vector", lambda e: e.tensor_copy(out=SCb[0:4, 1024:1536],
                                                 in_=ST[0:4, 1024:1536]), "dve", 1)
            dve_rcb = bld.cnt["dve"]
            wt("tensor", "dve", dve_rcb)
            st("tensor", lambda e: e.matmul(
                bank(PS, 6), CONb[0:4, CO["esel1T"]:CO["esel1T"] + 128], SCb[0:4, 1024:1536],
                start=True, stop=True), "pe", 1)
            pe_rb = bld.cnt["pe"]
            wt("vector", "pe", pe_rb)
            st("vector", lambda e: e.tensor_copy(out=SC2[:, 0:512], in_=bank(PS, 6)), "dve", 1)
            dve_rb2 = bld.cnt["dve"]
            wt("vector", "dve", dve_rb2)
            st("vector", lambda e: e.tensor_mul(out=SCb[:, 2048:2560], in0=bank(PS, 4),
                                                in1=SC2[:, 0:512]), "dve", 1)
            dve_yn = bld.cnt["dve"]
            wt("tensor", "dve", dve_yn)
            for hh in range(4):
                bnk = 5 + hh // 2
                st("tensor", lambda e, hh=hh, bnk=bnk: e.matmul(
                    PS[64 * (hh % 2):64 * (hh % 2) + 64, 512 * bnk:512 * bnk + 512],
                    CONb[32 * hh:32 * hh + 32, CO["l2oT"]:CO["l2oT"] + 64],
                    SCb[32 * hh:32 * hh + 32, 2048:2560],
                    start=True, stop=True, tile_position=(32 * hh, 64 * (hh % 2))), "pe", 1)
            pe_l2o = bld.cnt["pe"]
            wt("vector", "pe", pe_l2o)
            for half in range(2):
                st("vector", lambda e, hg=hg, half=half: e.tensor_scalar_add(
                    out=U1[:, YH0 + 1024 * hg + 512 * half:YH0 + 1024 * hg + 512 * half + 512],
                    in0=bank(PS, 5 + half),
                    scalar1=CONf[:, BO["bl2o"]:BO["bl2o"] + 1]), "dve", 1)
        act_yh = bld.cnt["dve"]
        last_pe_cp = {0: bld.cnt["pe"], 1: bld.cnt["pe"]}
        for oc in range(8):
            wt("gpsimd", "pe", last_pe_cp[oc % 2])
            for kc in range(8):
                st("gpsimd", lambda e, oc=oc, kc=kc: e.dma_start(
                    out=WR[:, 1024 * (oc % 2) + 128 * kc:1024 * (oc % 2) + 128 * kc + 128],
                    in_=I["cprojT"][128 * kc:128 * kc + 128, 128 * oc:128 * oc + 128]),
                    "dwa%d" % (oc % 2), 16)
            dw_c = bld.cnt["dwa%d" % (oc % 2)]
            wt("tensor", "dwa%d" % (oc % 2), dw_c)
            wt("tensor", "dve", bld.cnt["dve"])
            for kc in range(8):
                st("tensor", lambda e, oc=oc, kc=kc: e.matmul(
                    bank(PS, oc % 2),
                    WR[:, 1024 * (oc % 2) + 128 * kc:1024 * (oc % 2) + 128 * kc + 128],
                    U1[:, YH0 + 512 * kc:YH0 + 512 * kc + 512],
                    start=(kc == 0), stop=(kc == 7)), "pe", 1)
            last_pe_cp[oc % 2] = bld.cnt["pe"]
            pe_cp = bld.cnt["pe"]
            wt("vector", "pe", pe_cp)
            st("vector", lambda e, oc=oc, t=t: e.scalar_tensor_tensor(
                out=XR[:, TOK * oc + QT * t:TOK * oc + QT * t + 512], in0=bank(PS, oc % 2),
                scalar=CONf[:, BO["bcproj"] + oc:BO["bcproj"] + oc + 1],
                in1=XR[:, TOK * oc + QT * t:TOK * oc + QT * t + 512],
                op0=ALU.add, op1=ALU.add), "dve", 1)
        emit_stats_norm(bld, M, t, None, "ln")

    wt("gpsimd", "dve", bld.cnt["dve"])
    wt("gpsimd", "act", bld.cnt["act"])
    wt("gpsimd", "pe", bld.cnt["pe"])
    for wname, kcn, mcn in (("wkvd", 8, 2), ("wqd", 8, 2), ("wku", 2, 4),
                            ("wropek", 8, 4), ("wvu", 2, 8)):
        for mc in range(mcn):
            for kc in range(kcn):
                st("gpsimd", lambda e, wname=wname, mc=mc, kc=kc, kcn=kcn: e.dma_start(
                    out=U1[:, WOFF[wname] + 128 * (kcn * mc + kc):
                           WOFF[wname] + 128 * (kcn * mc + kc) + 128],
                    in_=I[wname][128 * kc:128 * kc + 128, 128 * mc:128 * mc + 128]), "dwq", 16)
    for mc in range(8):
        for kc in range(2):
            st("gpsimd", lambda e, mc=mc, kc=kc: e.dma_start(
                out=WR[:, WQQ0 + 128 * (2 * mc + kc):WQQ0 + 128 * (2 * mc + kc) + 128],
                in_=I["wqq"][128 * kc:128 * kc + 128, 128 * mc:128 * mc + 128]), "dwq", 16)
    dw_C = bld.cnt["dwq"]

    for t in range(NT):
        wt("sync", "dve", bld.cnt["dve"])
        st("sync", lambda e, t=t: e.dma_start(out=TAB[:, 0:512],
                                              in_=I["cos2"][:, QT * t:QT * t + 512]), "dt", 16)
        st("sync", lambda e, t=t: e.dma_start(out=TAB[:, 512:1024],
                                              in_=I["sin2"][:, QT * t:QT * t + 512]), "dt", 16)
        dx_tab = bld.cnt["dt"]
        emit_stats_norm(bld, M, t, BO["wln1"], "rms")
        wt("tensor", "dve", bld.cnt["dve"])
        if t == 0:
            wt("tensor", "dwq", dw_C)
        for mc in range(2):
            for src, bnk in (("wkvd", 0), ("wqd", 2)):
                for kc in range(8):
                    st("tensor", lambda e, src=src, mc=mc, kc=kc, bnk=bnk: e.matmul(
                        bank(PS, bnk + mc),
                        U1[:, WOFF[src] + 128 * (8 * mc + kc):WOFF[src] + 128 * (8 * mc + kc) + 128],
                        XB[:, QT * kc:QT * (kc + 1)], start=(kc == 0), stop=(kc == 7)), "pe", 1)
        pe_lat2 = bld.cnt["pe"]
        wt("scalar", "pe", pe_lat2)
        for mc in range(2):
            st("scalar", lambda e, mc=mc: e.activation(
                out=SCb[:, 512 * mc:512 * mc + 512], in_=bank(PS, mc), func=AF.Copy), "act", 1)
        for mc in range(2):
            st("scalar", lambda e, mc=mc, t=t: e.activation(
                out=QF[:, 512 * (2 * t + mc):512 * (2 * t + mc) + 512],
                in_=bank(PS, 2 + mc), func=AF.Copy), "act", 1)
        act_lat2 = bld.cnt["act"]
        for g in range(8):
            bnk = 4 + g % 2
            wt("tensor", "act", bld.cnt["act"])
            wt("tensor", "dve", bld.cnt["dve"])
            for hh in range(2):
                h = 2 * g + hh
                for kc in range(2):
                    st("tensor", lambda e, h=h, hh=hh, kc=kc, bnk=bnk: e.matmul(
                        PS[64 * hh:64 * hh + 32, 512 * bnk:512 * bnk + 512],
                        U1[:, WOFF["wku"] + 128 * (2 * (h // 4) + kc) + 32 * (h % 4):
                           WOFF["wku"] + 128 * (2 * (h // 4) + kc) + 32 * (h % 4) + 32],
                        SCb[:, 512 * kc:512 * kc + 512],
                        start=(kc == 0), stop=(kc == 1),
                        tile_position=(0, 64 * hh)), "pe", 1)
                for kc in range(8):
                    st("tensor", lambda e, h=h, hh=hh, kc=kc, bnk=bnk: e.matmul(
                        PS[64 * hh + 32:64 * hh + 64, 512 * bnk:512 * bnk + 512],
                        U1[:, WOFF["wropek"] + 128 * (8 * (h // 4) + kc) + 32 * (h % 4):
                           WOFF["wropek"] + 128 * (8 * (h // 4) + kc) + 32 * (h % 4) + 32],
                        XB[:, QT * kc:QT * (kc + 1)],
                        start=(kc == 0), stop=(kc == 7),
                        tile_position=(0, 64 * hh + 32)), "pe", 1)
            pe_kk = bld.cnt["pe"]
            wt("scalar", "pe", pe_kk)
            st("scalar", lambda e, bnk=bnk: e.activation(out=SC2[:, 0:512], in_=bank(PS, bnk),
                                                         func=AF.Copy), "act", 1)
            act_kk = bld.cnt["act"]
            wt("tensor", "act", act_kk)
            st("tensor", lambda e: e.matmul(bank(PS, 6), CONf[:, FO["r2T"]:FO["r2T"] + 128],
                                            SC2[:, 0:512], start=True, stop=True), "pe", 1)
            pe_rot2 = bld.cnt["pe"]
            wt("vector", "pe", pe_rot2)
            wt("vector", "da", bld.cnt["da"])
            if g == 0:
                wt("vector", "dt", dx_tab)
            st("vector", lambda e: e.tensor_mul(out=SC2[:, 512:1024], in0=SC2[:, 0:512],
                                                in1=TAB[:, 0:512]), "dve", 1)
            wt("vector", "dve", bld.cnt["dve"])
            st("vector", lambda e: e.tensor_mul(out=SC2[:, 0:512], in0=bank(PS, 6),
                                                in1=TAB[:, 512:1024]), "dve", 1)
            wt("vector", "dve", bld.cnt["dve"])
            st("vector", lambda e: e.tensor_add(out=SCb[:, 1024:1536], in0=SC2[:, 0:512],
                                                in1=SC2[:, 512:1024]), "dve", 1)
            dve_kk = bld.cnt["dve"]
            wt("sync", "dve", dve_kk)
            st("sync", lambda e, g=g, t=t: e.dma_start(
                out=ag["ag2k_i"].ap()[128 * g:128 * g + 128, QT * t:QT * (t + 1)],
                in_=SCb[:, 1024:1536]), "da", 16)
            vb = 4 + (g + 1) % 2
            for kc in range(2):
                st("tensor", lambda e, g=g, kc=kc, vb=vb: e.matmul(
                    bank(PS, vb),
                    U1[:, WOFF["wvu"] + 128 * (2 * g + kc):WOFF["wvu"] + 128 * (2 * g + kc) + 128],
                    SCb[:, 512 * kc:512 * kc + 512], start=(kc == 0), stop=(kc == 1)), "pe", 1)
            pe_v2 = bld.cnt["pe"]
            wt("scalar", "pe", pe_v2)
            st("scalar", lambda e, vb=vb: e.activation(out=SCb[:, 1536:2048],
                                                       in_=bank(PS, vb), func=AF.Copy), "act", 1)
            act_v2 = bld.cnt["act"]
            for s in range(4):
                wt("tensor", "act", act_v2)
                st("tensor", lambda e, s=s: e.transpose(
                    PSB[:, 128 * s:128 * s + 128], SCb[:, 1536 + 128 * s:1536 + 128 * s + 128],
                    CONb[:, CO["ident"]:CO["ident"] + 128]), "pe", 1)
            pe_vt2 = bld.cnt["pe"]
            wt("vector", "pe", pe_vt2)
            wt("vector", "da", bld.cnt["da"])
            st("vector", lambda e: e.tensor_copy(out=SCb[:, 2048:2560], in_=PSB[:, 0:512]),
               "dve", 1)
            dve_vt2 = bld.cnt["dve"]
            wt("sync", "dve", dve_vt2)
            st("sync", lambda e, g=g, t=t: e.dma_start(
                out=ag["ag2v_i"].ap()[QT * t:QT * t + 512, 128 * g:128 * g + 128]
                .rearrange("(s p) f -> p s f", p=128),
                in_=SCb[:, 2048:2560].rearrange("p (s f) -> p s f", s=4)), "da", 16)
    da_C = bld.cnt["da"]
    wt("gpsimd", "da", da_C)
    st("gpsimd", lambda e: e.collective_compute(
        "AllGather", ALU.bypass, replica_groups=ag["RG"],
        ins=[ag["ag2k_i"].ap()], outs=[ag["ag2k_o"].ap()]), "cc", 1)
    wt("gpsimd", "cc", bld.cnt["cc"])
    st("gpsimd", lambda e: e.collective_compute(
        "AllGather", ALU.bypass, replica_groups=ag["RG"],
        ins=[ag["ag2v_i"].ap()], outs=[ag["ag2v_o"].ap()]), "cc", 1)
    cc2 = bld.cnt["cc"]

    wt("gpsimd", "cc", cc2)
    wt("sync", "cc", cc2)
    last_pe_kst = {0: bld.cnt["pe"], 1: bld.cnt["pe"]}
    for g in range(8):
        for rr in range(4):
            pp = rr % 2
            wt("sync", "pe", last_pe_kst[pp])
            st("sync", lambda e, g=g, rr=rr, pp=pp: e.dma_start(
                out=U1[:, KST0 + 2048 * pp:KST0 + 2048 * pp + 2048],
                in_=ag["ag2k_o"].ap()[1024 * rr + 128 * g:1024 * rr + 128 * g + 128, :]),
                "dk%d" % pp, 16)
            dk_n = bld.cnt["dk%d" % pp]
            wt("scalar", "dk%d" % pp, dk_n)
            for q4 in range(4):
                wt("scalar", "pe", bld.cnt["pe"])
                st("scalar", lambda e, pp=pp, q4=q4: e.activation(
                    out=SCb[:, 512 * (q4 % 2):512 * (q4 % 2) + 512],
                    in_=U1[:, KST0 + 2048 * pp + 512 * q4:KST0 + 2048 * pp + 512 * q4 + 512],
                    func=AF.Square), "act", 1)
                a_n = bld.cnt["act"]
                wt("tensor", "act", a_n)
                wt("tensor", "dve", bld.cnt["dve"])
                st("tensor", lambda e, q4=q4: e.matmul(
                    PS[0:2, 512 * 6:512 * 6 + 512], CONb[:, CO["ones2"]:CO["ones2"] + 2],
                    SCb[:, 512 * (q4 % 2):512 * (q4 % 2) + 512], start=True, stop=True),
                    "pe", 1)
                p_n = bld.cnt["pe"]
                wt("vector", "pe", p_n)
                st("vector", lambda e, g=g, rr=rr, q4=q4: e.reduce_max(
                    ST[0:2, 128 + 16 * g + 4 * rr + q4:129 + 16 * g + 4 * rr + q4],
                    PS[0:2, 512 * 6:512 * 6 + 512], axis=AX.X), "dve", 1)
            last_pe_kst[pp] = bld.cnt["pe"]
    wt("vector", "dve", bld.cnt["dve"])
    for g in range(8):
        st("vector", lambda e, g=g: e.reduce_max(
            ST[0:2, 256 + g:257 + g], ST[0:2, 128 + 16 * g:128 + 16 * g + 16], axis=AX.X),
            "dve", 1)
    dve_kn2 = bld.cnt["dve"]
    wt("scalar", "dve", dve_kn2)
    st("scalar", lambda e: e.activation(out=ST[0:2, 264:272], in_=ST[0:2, 256:264],
                                        func=AF.Sqrt), "act", 1)
    act_kn2 = bld.cnt["act"]
    pe_gu_prev = bld.cnt["pe"]
    pe_dn_prev = bld.cnt["pe"]

    for t in range(NT):
        wt("sync", "act", act_kn2)
        wt("sync", "dve", bld.cnt["dve"])
        st("sync", lambda e, t=t: e.dma_start(out=TAB[:, 0:512],
                                              in_=I["cos2"][:, QT * t:QT * t + 512]), "dt", 16)
        st("sync", lambda e, t=t: e.dma_start(out=TAB[:, 512:1024],
                                              in_=I["sin2"][:, QT * t:QT * t + 512]), "dt", 16)
        dx_tab = bld.cnt["dt"]
        for mi in range(4):
            st("sync", lambda e, mi=mi: e.dma_start(
                out=U1[:, MSK0 + 2048 * mi:MSK0 + 2048 * mi + 2048]
                .rearrange("p (a f) -> p a f", a=4),
                in_=I["mask"][512 * mi:512 * mi + 512, :]
                .rearrange("(a p) f -> p a f", p=128)), "dm", 16)
        dk_msk = bld.cnt["dm"]
        for g in range(8):
            bnk = g % 2
            wt("tensor", "act", bld.cnt["act"])
            wt("tensor", "dve", bld.cnt["dve"])
            if t == 0 and g == 0:
                wt("tensor", "dt", dx_tab)
            for kc in range(2):
                st("tensor", lambda e, g=g, kc=kc, t=t, bnk=bnk: e.matmul(
                    bank(PS, bnk),
                    WR[:, WQQ0 + 128 * (2 * g + kc):WQQ0 + 128 * (2 * g + kc) + 128],
                    QF[:, 512 * (2 * t + kc):512 * (2 * t + kc) + 512],
                    start=(kc == 0), stop=(kc == 1)), "pe", 1)
            pe_qq = bld.cnt["pe"]
            wt("scalar", "pe", pe_qq)
            st("scalar", lambda e, bnk=bnk: e.activation(out=SC2[:, 0:512], in_=bank(PS, bnk),
                                                         func=AF.Copy), "act", 1)
            act_q2 = bld.cnt["act"]
            wt("tensor", "act", act_q2)
            st("tensor", lambda e: e.matmul(bank(PS, 6), CONf[:, FO["r2T"]:FO["r2T"] + 128],
                                            SC2[:, 0:512], start=True, stop=True), "pe", 1)
            pe_rot = bld.cnt["pe"]
            wt("vector", "pe", pe_rot)
            if g == 0:
                wt("vector", "dt", dx_tab)
            st("vector", lambda e: e.tensor_mul(out=SC2[:, 512:1024], in0=SC2[:, 0:512],
                                                in1=TAB[:, 0:512]), "dve", 1)
            wt("vector", "dve", bld.cnt["dve"])
            st("vector", lambda e: e.tensor_mul(out=SC2[:, 0:512], in0=bank(PS, 6),
                                                in1=TAB[:, 512:1024]), "dve", 1)
            wt("vector", "dve", bld.cnt["dve"])
            st("vector", lambda e, g=g: e.tensor_add(
                out=QF[:, 4096 + 512 * g:4096 + 512 * g + 512],
                in0=SC2[:, 0:512], in1=SC2[:, 512:1024]), "dve", 1)
            dve_qq = bld.cnt["dve"]
            wt("scalar", "dve", dve_qq)
            st("scalar", lambda e, g=g: e.activation(
                out=SCb[:, 2048:2560], in_=QF[:, 4096 + 512 * g:4096 + 512 * g + 512],
                func=AF.Square), "act", 1)
            act_sq = bld.cnt["act"]
            wt("tensor", "act", act_sq)
            wt("tensor", "dve", bld.cnt["dve"])
            st("tensor", lambda e: e.matmul(PS[0:2, 512 * 6:512 * 6 + 512],
                                            CONb[:, CO["ones2"]:CO["ones2"] + 2],
                                            SCb[:, 2048:2560], start=True, stop=True), "pe", 1)
            pe_q2m = bld.cnt["pe"]
            wt("vector", "pe", pe_q2m)
            st("vector", lambda e: e.reduce_max(ST[0:2, 272:273],
                                                PS[0:2, 512 * 6:512 * 6 + 512], axis=AX.X),
               "dve", 1)
            wt("scalar", "dve", bld.cnt["dve"])
            st("scalar", lambda e: e.activation(out=ST[0:2, 272:273], in_=ST[0:2, 272:273],
                                                func=AF.Sqrt), "act", 1)
            act_qs = bld.cnt["act"]
            wt("vector", "act", act_qs)
            st("vector", lambda e, g=g: e.tensor_mul(out=ST[0:2, 273:274], in0=ST[0:2, 272:273],
                                                     in1=ST[0:2, 264 + g:265 + g]), "dve", 1)
            wt("vector", "dve", bld.cnt["dve"])
            st("vector", lambda e: e.tensor_scalar_mul(out=ST[0:2, 273:274],
                                                       in0=ST[0:2, 273:274],
                                                       scalar1=-float(SC2s)), "dve", 1)
            wt("vector", "dve", bld.cnt["dve"])
            st("vector", lambda e: e.tensor_copy(out=SCb[0:2, 8:9], in_=ST[0:2, 273:274]),
               "dve", 1)
            dve_bias2 = bld.cnt["dve"]
            wt("tensor", "dve", dve_bias2)
            st("tensor", lambda e: e.matmul(PS[:, 512 * 6:512 * 6 + 1],
                                            CONb[0:2, CO["esel2T"]:CO["esel2T"] + 128],
                                            SCb[0:2, 8:9], start=True, stop=True), "pe", 1)
            pe_b2 = bld.cnt["pe"]
            wt("vector", "pe", pe_b2)
            st("vector", lambda e: e.tensor_copy(out=ST[:, 12:13],
                                                 in_=PS[:, 512 * 6:512 * 6 + 1]), "dve", 1)
            dve_b2 = bld.cnt["dve"]
            first = True
            for m2 in range(t + 1):
                pp = m2 % 2
                wt("sync", "pe", last_pe_kst[pp])
                wt("sync", "dve", dve_b2) if m2 == 0 else None
                for rr in range(4):
                    st("sync", lambda e, rr=rr, g=g, pp=pp, m2=m2: e.dma_start(
                        out=U1[:, KST0 + 2048 * pp + 512 * rr:KST0 + 2048 * pp + 512 * rr + 512],
                        in_=ag["ag2k_o"].ap()[1024 * rr + 128 * g:1024 * rr + 128 * g + 128,
                                              QT * m2:QT * m2 + 512]), "dk%d" % pp, 16)
                    st("sync", lambda e, rr=rr, g=g, pp=pp, m2=m2: e.dma_start(
                        out=U1[:, VST0 + 2048 * pp + 512 * rr:VST0 + 2048 * pp + 512 * rr + 512]
                        .rearrange("p (s f) -> p s f", s=4),
                        in_=ag["ag2v_o"].ap()[TOK * rr + QT * m2:TOK * rr + QT * m2 + 512,
                                              128 * g:128 * g + 128]
                        .rearrange("(s p) f -> p s f", p=128)), "dk%d" % pp, 16)
                dk_kv = bld.cnt["dk%d" % pp]
                for rr in range(4):
                    for kb in range(4):
                        is_last = (m2 == t and rr == 3 and kb == 3)
                        wt("tensor", "dk%d" % pp, dk_kv)
                        wt("tensor", "act", bld.cnt["act"])
                        wt("tensor", "dve", dve_b2)
                        for hh in range(2):
                            st("tensor", lambda e, hh=hh, pp=pp, rr=rr, kb=kb, g=g: e.matmul(
                                bank(PS, hh),
                                U1[64 * hh:64 * hh + 64,
                                   KST0 + 2048 * pp + 512 * rr + 128 * kb:
                                   KST0 + 2048 * pp + 512 * rr + 128 * kb + 128],
                                QF[64 * hh:64 * hh + 64, 4096 + 512 * g:4096 + 512 * g + 512],
                                start=True, stop=True, skip_group_check=True,
                                tile_position=(64 * hh, 0)), "pe", 1)
                        pe_s = bld.cnt["pe"]
                        wt("scalar", "pe", pe_s)
                        st("scalar", lambda e: e.activation(
                            out=U1[:, PT0:PT0 + 1024], in_=PS[:, 0:1024], func=AF.Exp,
                            bias=ST[:, 12:13], scale=float(SC2s)), "act", 1)
                        act_e = bld.cnt["act"]
                        if m2 == t:
                            wt("vector", "act", act_e)
                            if rr == 0 and kb == 0:
                                wt("vector", "dm", dk_msk)
                            for hh in range(2):
                                st("vector", lambda e, hh=hh, rr=rr, kb=kb: e.tensor_mul(
                                    out=U1[:, PT0 + 512 * hh:PT0 + 512 * hh + 512],
                                    in0=U1[:, PT0 + 512 * hh:PT0 + 512 * hh + 512],
                                    in1=U1[:, MSK0 + 512 * (4 * rr + kb):
                                           MSK0 + 512 * (4 * rr + kb) + 512]), "dve", 1)
                            wt("tensor", "dve", bld.cnt["dve"])
                        else:
                            wt("tensor", "act", act_e)
                        for hh in range(2):
                            st("tensor", lambda e, hh=hh, pp=pp, rr=rr, kb=kb, first=first,
                               is_last=is_last: e.matmul(
                                PS[64 * hh:64 * hh + 64, 512 * 4:512 * 4 + 512],
                                U1[:, VST0 + 2048 * pp + 512 * rr + 128 * kb + 64 * hh:
                                   VST0 + 2048 * pp + 512 * rr + 128 * kb + 64 * hh + 64],
                                U1[:, PT0 + 512 * hh:PT0 + 512 * hh + 512],
                                start=first, stop=is_last, skip_group_check=True,
                                tile_position=(0, 64 * hh)), "pe", 1)
                        for hh in range(2):
                            st("tensor", lambda e, hh=hh, first=first, is_last=is_last: e.matmul(
                                PS[0:2, 512 * 5:512 * 5 + 512],
                                CONb[:, CO["oneh2"] + 2 * hh:CO["oneh2"] + 2 * hh + 2],
                                U1[:, PT0 + 512 * hh:PT0 + 512 * hh + 512],
                                start=(first and hh == 0), stop=(is_last and hh == 1),
                                skip_group_check=True), "pe", 1)
                        first = False
                last_pe_kst[pp] = bld.cnt["pe"]
            pe_attn = bld.cnt["pe"]
            wt("scalar", "pe", pe_attn)
            st("scalar", lambda e: e.activation(out=ST[0:2, 1024:1536],
                                                in_=PS[0:2, 512 * 5:512 * 5 + 512],
                                                func=AF.Ln), "act", 1)
            wt("scalar", "act", bld.cnt["act"])
            st("scalar", lambda e: e.activation(out=ST[0:2, 1024:1536], in_=ST[0:2, 1024:1536],
                                                func=AF.Exp, scale=-1.0), "act", 1)
            act_rc = bld.cnt["act"]
            wt("vector", "act", act_rc)
            st("vector", lambda e: e.tensor_copy(out=SCb[0:2, 1024:1536],
                                                 in_=ST[0:2, 1024:1536]), "dve", 1)
            dve_rcb = bld.cnt["dve"]
            wt("tensor", "dve", dve_rcb)
            st("tensor", lambda e: e.matmul(bank(PS, 6),
                                            CONb[0:2, CO["esel2T"]:CO["esel2T"] + 128],
                                            SCb[0:2, 1024:1536], start=True, stop=True), "pe", 1)
            pe_rb = bld.cnt["pe"]
            wt("vector", "pe", pe_rb)
            st("vector", lambda e: e.tensor_copy(out=SC2[:, 0:512], in_=bank(PS, 6)), "dve", 1)
            wt("vector", "dve", bld.cnt["dve"])
            st("vector", lambda e, g=g: e.tensor_mul(
                out=U1[:, YH0 + 512 * g:YH0 + 512 * g + 512], in0=bank(PS, 4),
                in1=SC2[:, 0:512]), "dve", 1)
        dve_y2 = bld.cnt["dve"]
        last_pe_wo = {0: bld.cnt["pe"], 1: bld.cnt["pe"]}
        for oc in range(8):
            wt("gpsimd", "pe", last_pe_wo[oc % 2])
            for kc in range(8):
                st("gpsimd", lambda e, oc=oc, kc=kc: e.dma_start(
                    out=WR[:, 1024 * (oc % 2) + 128 * kc:1024 * (oc % 2) + 128 * kc + 128],
                    in_=I["wo"][128 * kc:128 * kc + 128, 128 * oc:128 * oc + 128]),
                    "dwa%d" % (oc % 2), 16)
            dw_o = bld.cnt["dwa%d" % (oc % 2)]
            wt("tensor", "dwa%d" % (oc % 2), dw_o)
            wt("tensor", "dve", bld.cnt["dve"])
            for kc in range(8):
                st("tensor", lambda e, oc=oc, kc=kc: e.matmul(
                    bank(PS, oc % 2),
                    WR[:, 1024 * (oc % 2) + 128 * kc:1024 * (oc % 2) + 128 * kc + 128],
                    U1[:, YH0 + 512 * kc:YH0 + 512 * kc + 512],
                    start=(kc == 0), stop=(kc == 7)), "pe", 1)
            last_pe_wo[oc % 2] = bld.cnt["pe"]
            pe_wo = bld.cnt["pe"]
            wt("vector", "pe", pe_wo)
            st("vector", lambda e, oc=oc, t=t: e.tensor_add(
                out=SC2[:, 512 * (oc % 4):512 * (oc % 4) + 512], in0=bank(PS, oc % 2),
                in1=XR[:, TOK * oc + QT * t:TOK * oc + QT * t + 512]), "dve", 1)
            wt("vector", "dve", bld.cnt["dve"])
            st("vector", lambda e, oc=oc, t=t: e.tensor_add(
                out=XR[:, TOK * oc + QT * t:TOK * oc + QT * t + 512],
                in0=XR[:, TOK * oc + QT * t:TOK * oc + QT * t + 512],
                in1=SC2[:, 512 * (oc % 4):512 * (oc % 4) + 512]), "dve", 1)
            st("vector", lambda e, oc=oc: e.tensor_copy(
                out=XB[:, QT * oc:QT * oc + 512],
                in_=SC2[:, 512 * (oc % 4):512 * (oc % 4) + 512]), "dve", 1)
        emit_stats_norm(bld, M, t, BO["wln2"], "rmsxb")

        wt("gpsimd", "dve", bld.cnt["dve"])
        for kc in range(8):
            st("gpsimd", lambda e, kc=kc: e.dma_start(
                out=WR[:, WRT0 + 8 * kc:WRT0 + 8 * kc + ER],
                in_=I["wrout"][128 * kc:128 * kc + 128, :]), "dwr", 16)
        dw_r = bld.cnt["dwr"]
        wt("tensor", "dwr", dw_r)
        wt("tensor", "dve", bld.cnt["dve"])
        for kc in range(8):
            st("tensor", lambda e, kc=kc: e.matmul(
                PS[0:ER, 512 * 6:512 * 6 + 512], WR[:, WRT0 + 8 * kc:WRT0 + 8 * kc + ER],
                XB[:, QT * kc:QT * (kc + 1)], start=(kc == 0), stop=(kc == 7)), "pe", 1)
        pe_lg = bld.cnt["pe"]
        wt("scalar", "pe", pe_lg)
        st("scalar", lambda e: e.activation(out=SC2[0:ER, 0:512],
                                            in_=PS[0:ER, 512 * 6:512 * 6 + 512],
                                            func=AF.Sigmoid,
                                            bias=CONf[0:ER, BO["brout"]:BO["brout"] + 1]), "act", 1)
        act_pr = bld.cnt["act"]
        wt("tensor", "act", act_pr)
        for s in range(4):
            st("tensor", lambda e, s=s: e.transpose(
                PS[:, 512 * 5 + 128 * s:512 * 5 + 128 * s + 128],
                SC2[:, 128 * s:128 * s + 128],
                CONf[:, FO["identf"]:FO["identf"] + 128]), "pe", 1)
        pe_tp = bld.cnt["pe"]
        wt("vector", "pe", pe_tp)
        for s in range(4):
            st("vector", lambda e, s=s: e.tensor_copy(
                out=SC2[:, 1024 + 8 * s:1024 + 8 * s + ER],
                in_=PS[:, 512 * 5 + 128 * s:512 * 5 + 128 * s + ER]), "dve", 1)
        for s in range(4):
            pb = 1024 + 8 * s
            wt("vector", "dve", bld.cnt["dve"])
            st("vector", lambda e, pb=pb: e.reduce_max(ST[:, 0:1], SC2[:, pb:pb + ER],
                                                       axis=AX.X), "dve", 1)
            wt("vector", "dve", bld.cnt["dve"])
            st("vector", lambda e, pb=pb: e.tensor_scalar(
                out=ST[:, 8:8 + ER], in0=SC2[:, pb:pb + ER], scalar1=ST[:, 0:1],
                scalar2=None, op0=ALU.is_equal), "dve", 1)
            wt("vector", "dve", bld.cnt["dve"])
            st("vector", lambda e, pb=pb: e.scalar_tensor_tensor(
                out=ST[:, 8:8 + ER], in0=ST[:, 8:8 + ER], scalar=-1e30,
                in1=SC2[:, pb:pb + ER], op0=ALU.mult, op1=ALU.add), "dve", 1)
            wt("vector", "dve", bld.cnt["dve"])
            st("vector", lambda e: e.reduce_max(ST[:, 1:2], ST[:, 8:8 + ER], axis=AX.X),
               "dve", 1)
            wt("vector", "dve", bld.cnt["dve"])
            st("vector", lambda e, pb=pb: e.tensor_scalar(
                out=ST[:, 8:8 + ER], in0=SC2[:, pb:pb + ER], scalar1=ST[:, 1:2],
                scalar2=None, op0=ALU.is_ge), "dve", 1)
            wt("vector", "dve", bld.cnt["dve"])
            st("vector", lambda e, pb=pb: e.tensor_mul(
                out=ST[:, 8:8 + ER], in0=ST[:, 8:8 + ER], in1=SC2[:, pb:pb + ER]), "dve", 1)
            wt("vector", "dve", bld.cnt["dve"])
            st("vector", lambda e: e.tensor_add(out=ST[:, 0:1], in0=ST[:, 0:1],
                                                in1=ST[:, 1:2]), "dve", 1)
            dve_d = bld.cnt["dve"]
            wt("scalar", "dve", dve_d)
            st("scalar", lambda e: e.activation(out=ST[:, 1:2], in_=ST[:, 0:1], func=AF.Ln),
               "act", 1)
            wt("scalar", "act", bld.cnt["act"])
            st("scalar", lambda e: e.activation(out=ST[:, 1:2], in_=ST[:, 1:2], func=AF.Exp,
                                                scale=-1.0), "act", 1)
            act_rr2 = bld.cnt["act"]
            wt("vector", "act", act_rr2)
            wt("vector", "dve", bld.cnt["dve"])
            st("vector", lambda e, s=s: e.tensor_scalar(
                out=SC2[:, 1280 + 8 * s:1280 + 8 * s + ER], in0=ST[:, 8:8 + ER],
                scalar1=ST[:, 1:2], scalar2=None, op0=ALU.mult), "dve", 1)
        dve_wts = bld.cnt["dve"]
        wt("vector", "dve", dve_wts)
        for s in range(4):
            st("vector", lambda e, s=s: e.tensor_copy(
                out=SCb[:, 2048 + 8 * s:2048 + 8 * s + ER],
                in_=SC2[:, 1280 + 8 * s:1280 + 8 * s + ER]), "dve", 1)
        dve_wtb = bld.cnt["dve"]
        wt("tensor", "dve", dve_wtb)
        for s in range(4):
            st("tensor", lambda e, s=s: e.transpose(
                PSB[0:ER, 128 * s:128 * s + 128], SCb[:, 2048 + 8 * s:2048 + 8 * s + ER],
                CONb[:, CO["ident"]:CO["ident"] + 128]), "pe", 1)
        pe_wtt = bld.cnt["pe"]
        wt("vector", "pe", pe_wtt)
        st("vector", lambda e: e.tensor_copy(out=SCb[0:ER, 1024:1536], in_=PSB[0:ER, 0:512]),
           "dve", 1)
        dve_wfm = bld.cnt["dve"]

        for net in range(1 + ER):
            wn = ("wgs", "wus", "wds") if net == 0 else ("wgr", "wur", "wdr")
            e_idx = max(0, net - 1)
            if net > 0:
                wt("tensor", "dve", bld.cnt["dve"])
                st("tensor", lambda e, e_idx=e_idx: e.matmul(
                    bank(PS, 6), CONb[0:ER, CO["sele"] + 128 * e_idx:CO["sele"] + 128 * e_idx + 128],
                    SCb[0:ER, 1024:1536], start=True, stop=True), "pe", 1)
                pe_wb = bld.cnt["pe"]
                wt("vector", "pe", pe_wb)
                st("vector", lambda e: e.tensor_copy(out=SCb[:, 2560:3072], in_=bank(PS, 6)),
                   "dve", 1)
            for ic in range(8):
                wt("gpsimd", "pe", pe_gu_prev)
                for kc in range(8):
                    st("gpsimd", lambda e, wn=wn, e_idx=e_idx, ic=ic, kc=kc, net=net: e.dma_start(
                        out=WR[:, 4096 + 128 * kc:4096 + 128 * kc + 128],
                        in_=(I[wn[0]][128 * kc:128 * kc + 128, 128 * ic:128 * ic + 128]
                             if net == 0 else
                             I[wn[0]][e_idx, 128 * kc:128 * kc + 128, 128 * ic:128 * ic + 128])),
                        "dwg", 16)
                    st("gpsimd", lambda e, wn=wn, e_idx=e_idx, ic=ic, kc=kc, net=net: e.dma_start(
                        out=WR[:, 5120 + 128 * kc:5120 + 128 * kc + 128],
                        in_=(I[wn[1]][128 * kc:128 * kc + 128, 128 * ic:128 * ic + 128]
                             if net == 0 else
                             I[wn[1]][e_idx, 128 * kc:128 * kc + 128, 128 * ic:128 * ic + 128])),
                        "dwg", 16)
                dw_gu = bld.cnt["dwg"]
                wt("tensor", "dwg", dw_gu)
                wt("tensor", "act", bld.cnt["act"])
                wt("tensor", "dve", bld.cnt["dve"])
                for kc in range(8):
                    st("tensor", lambda e, kc=kc: e.matmul(
                        bank(PS, 2), WR[:, 4096 + 128 * kc:4096 + 128 * kc + 128],
                        XB[:, QT * kc:QT * (kc + 1)], start=(kc == 0), stop=(kc == 7)), "pe", 1)
                for kc in range(8):
                    st("tensor", lambda e, kc=kc: e.matmul(
                        bank(PS, 3), WR[:, 5120 + 128 * kc:5120 + 128 * kc + 128],
                        XB[:, QT * kc:QT * (kc + 1)], start=(kc == 0), stop=(kc == 7)), "pe", 1)
                pe_gu = bld.cnt["pe"]
                pe_gu_prev = pe_gu
                wt("scalar", "pe", pe_gu)
                if ic >= 1:
                    wt("scalar", "dve", bld.cnt["dve"])
                st("scalar", lambda e: e.activation(out=SCb[:, 0:512], in_=bank(PS, 2),
                                                    func=AF.Silu), "act", 1)
                st("scalar", lambda e: e.activation(out=SCb[:, 512:1024], in_=bank(PS, 3),
                                                    func=AF.Copy), "act", 1)
                act_gu = bld.cnt["act"]
                wt("vector", "act", act_gu)
                st("vector", lambda e: e.tensor_mul(out=SCb[:, 0:512], in0=SCb[:, 0:512],
                                                    in1=SCb[:, 512:1024]), "dve", 1)
                wt("vector", "dve", bld.cnt["dve"])
                if net > 0:
                    st("vector", lambda e, ic=ic: e.tensor_mul(
                        out=U1[:, 512 * ic:512 * ic + 512], in0=SCb[:, 0:512],
                        in1=SCb[:, 2560:3072]), "dve", 1)
                else:
                    st("vector", lambda e, ic=ic: e.tensor_copy(
                        out=U1[:, 512 * ic:512 * ic + 512], in_=SCb[:, 0:512]), "dve", 1)
            dve_hm = bld.cnt["dve"]
            for oc in range(8):
                wt("gpsimd", "pe", pe_dn_prev)
                for kc in range(8):
                    st("gpsimd", lambda e, wn=wn, e_idx=e_idx, oc=oc, kc=kc, net=net: e.dma_start(
                        out=WR[:, 6144 + 128 * kc:6144 + 128 * kc + 128],
                        in_=(I[wn[2]][128 * kc:128 * kc + 128, 128 * oc:128 * oc + 128]
                             if net == 0 else
                             I[wn[2]][e_idx, 128 * kc:128 * kc + 128, 128 * oc:128 * oc + 128])),
                        "dwd", 16)
                dw_d = bld.cnt["dwd"]
                wt("tensor", "dwd", dw_d)
                wt("tensor", "dve", bld.cnt["dve"])
                for kc in range(8):
                    st("tensor", lambda e, oc=oc, kc=kc: e.matmul(
                        bank(PS, 4 + oc % 2), WR[:, 6144 + 128 * kc:6144 + 128 * kc + 128],
                        U1[:, 512 * kc:512 * kc + 512], start=(kc == 0), stop=(kc == 7)), "pe", 1)
                pe_dn = bld.cnt["pe"]
                pe_dn_prev = pe_dn
                wt("vector", "pe", pe_dn)
                if net == 0:
                    st("vector", lambda e, oc=oc: e.tensor_copy(
                        out=TAB[:, ACC0 + 512 * oc:ACC0 + 512 * oc + 512],
                        in_=bank(PS, 4 + oc % 2)), "dve", 1)
                else:
                    st("vector", lambda e, oc=oc: e.tensor_add(
                        out=TAB[:, ACC0 + 512 * oc:ACC0 + 512 * oc + 512],
                        in0=TAB[:, ACC0 + 512 * oc:ACC0 + 512 * oc + 512],
                        in1=bank(PS, 4 + oc % 2)), "dve", 1)
        last_pe_kst = {0: bld.cnt["pe"], 1: bld.cnt["pe"]}
        wt("vector", "dve", bld.cnt["dve"])
        for oc in range(8):
            st("vector", lambda e, oc=oc, t=t: e.tensor_add(
                out=XR[:, TOK * oc + QT * t:TOK * oc + QT * t + 512],
                in0=XR[:, TOK * oc + QT * t:TOK * oc + QT * t + 512],
                in1=TAB[:, ACC0 + 512 * oc:ACC0 + 512 * oc + 512]), "dve", 1)
        emit_stats_norm(bld, M, t, None, "ln")
        dve_fin = bld.cnt["dve"]
        wt("sync", "dve", dve_fin)
        for oc in range(8):
            st("sync", lambda e, oc=oc, t=t: e.dma_start(
                out=y_out[128 * oc:128 * oc + 128, QT * t:QT * t + 512],
                in_=XR[:, TOK * oc + QT * t:TOK * oc + QT * t + 512]), "do", 16)
    wt("sync", "do", bld.cnt["do"])


def _host_inputs(kw):
    f32 = np.float32
    x = np.ascontiguousarray(np.asarray(kw["L0_attn_x"], f32))

    t = np.arange(T, dtype=np.float64)
    inv1 = 1.0 / 10000.0 ** (np.arange(0, L1, 2, dtype=np.float64) / L1)
    fr1 = np.outer(t, inv1)
    cos1 = np.repeat(np.cos(fr1), 2, axis=-1)[:, :L1].astype(f32)
    sin1 = np.repeat(np.sin(fr1), 2, axis=-1)[:, :L1].astype(f32)
    inv2 = 1.0 / 10000.0 ** (np.arange(0, HH2, 2, dtype=np.float64) / HH2)
    fr2 = np.outer(t, inv2)
    emb2 = np.concatenate([fr2, fr2], axis=-1)
    cos2 = np.cos(emb2).astype(f32)
    sin2 = np.sin(emb2).astype(f32)

    R1 = np.zeros((L1, L1), f32)
    for i in range(L1 // 2):
        R1[2 * i, 2 * i + 1] = -1.0
        R1[2 * i + 1, 2 * i] = 1.0
    R1T = np.kron(np.eye(4, dtype=f32), R1.T)
    R2 = np.zeros((HH2, HH2), f32)
    for i in range(HH2 // 2):
        R2[i, HH2 // 2 + i] = -1.0
        R2[HH2 // 2 + i, i] = 1.0
    R2T = np.zeros((128, 128), f32)
    R2T[32:64, 32:64] = R2.T
    R2T[96:128, 96:128] = R2.T

    sh = {}
    sh["wqkv"] = np.ascontiguousarray(np.asarray(kw["L0_attn_c_attn_w"], f32).T)
    sh["bqkv"] = np.ascontiguousarray(np.asarray(kw["L0_attn_c_attn_b"], f32).reshape(24, 128).T)
    sh["q2lT"] = np.ascontiguousarray(np.tile(np.asarray(kw["L0_attn_q2l_w"], f32).T, (2, 1)))
    sh["kv2lT"] = np.ascontiguousarray(np.tile(np.asarray(kw["L0_attn_kv2l_w"], f32).T, (2, 1)))
    sh["bq2l"] = np.ascontiguousarray(np.tile(np.asarray(kw["L0_attn_q2l_b"], f32), 4).reshape(128, 1))
    sh["bkv2l"] = np.ascontiguousarray(np.tile(np.asarray(kw["L0_attn_kv2l_b"], f32), 4).reshape(128, 1))
    sh["wqkT"] = np.ascontiguousarray(np.tile(np.asarray(kw["L0_attn_wqk_w"], f32).T, (4, 1)))
    sh["bwqk"] = np.ascontiguousarray(np.tile(np.asarray(kw["L0_attn_wqk_b"], f32), 4).reshape(128, 1))
    sh["l2oT"] = np.ascontiguousarray(np.tile(np.asarray(kw["L0_attn_l2o_w"], f32).T, (4, 1)))
    sh["bl2o"] = np.ascontiguousarray(np.tile(np.asarray(kw["L0_attn_l2o_b"], f32), 2).reshape(128, 1))
    sh["cprojT"] = np.ascontiguousarray(np.asarray(kw["L0_attn_cproj_w"], f32).T)
    sh["bcproj"] = np.ascontiguousarray(np.asarray(kw["L0_attn_cproj_b"], f32).reshape(8, 128).T)
    sh["wln1"] = np.ascontiguousarray(np.asarray(kw["L0_ffn_w_ln1"], f32).reshape(8, 128).T)
    sh["wln2"] = np.ascontiguousarray(np.asarray(kw["L0_ffn_w_ln2"], f32).reshape(8, 128).T)
    sh["wkvd"] = np.ascontiguousarray(np.asarray(kw["L0_ffn_W_kvd"], f32))
    sh["wqd"] = np.ascontiguousarray(np.asarray(kw["L0_ffn_W_qd"], f32))
    sh["wku"] = np.ascontiguousarray(np.asarray(kw["L0_ffn_W_ku"], f32))
    wqu = np.asarray(kw["L0_ffn_W_qu"], f32)
    wrq = np.asarray(kw["L0_ffn_W_ropeq"], f32)
    wqq = np.zeros((256, 1024), f32)
    for h in range(16):
        wqq[:, 64 * h:64 * h + 32] = wqu[:, 32 * h:32 * h + 32]
        wqq[:, 64 * h + 32:64 * h + 64] = wrq[:, 32 * h:32 * h + 32]
    sh["wqq"] = np.ascontiguousarray(wqq)
    sh["wropek"] = np.ascontiguousarray(np.asarray(kw["L0_ffn_W_ropek"], f32))
    sh["wvu"] = np.ascontiguousarray(np.asarray(kw["L0_ffn_W_vu"], f32))
    sh["wo"] = np.ascontiguousarray(np.asarray(kw["L0_ffn_W_o"], f32))
    sh["wgs"] = np.ascontiguousarray(np.asarray(kw["L0_ffn_Wg_s"], f32))
    sh["wus"] = np.ascontiguousarray(np.asarray(kw["L0_ffn_Wu_s"], f32))
    sh["wds"] = np.ascontiguousarray(np.asarray(kw["L0_ffn_Wd_s"], f32))
    sh["wgr"] = np.ascontiguousarray(np.asarray(kw["L0_ffn_Wg_r"], f32))
    sh["wur"] = np.ascontiguousarray(np.asarray(kw["L0_ffn_Wu_r"], f32))
    sh["wdr"] = np.ascontiguousarray(np.asarray(kw["L0_ffn_Wd_r"], f32))
    sh["wrout"] = np.ascontiguousarray(np.asarray(kw["L0_ffn_W_router"], f32))
    sh["brout"] = np.ascontiguousarray(np.asarray(kw["L0_ffn_routing_bias"], f32).reshape(ER, 1))
    sh["identb"] = np.eye(128).astype(bfdt)
    sh["identf"] = np.eye(128, dtype=f32)
    sh["r1T"] = R1T
    sh["r2T"] = R2T
    o1 = np.zeros((128, 4), f32); e1 = np.zeros((4, 128), f32)
    for h in range(4):
        o1[32 * h:32 * h + 32, h] = 1.0
        e1[h, 32 * h:32 * h + 32] = 1.0
    sh["ones1"] = o1.astype(bfdt); sh["esel1T"] = e1.astype(bfdt)
    o2 = np.zeros((128, 2), f32); e2 = np.zeros((2, 128), f32)
    for h in range(2):
        o2[64 * h:64 * h + 64, h] = 1.0
        e2[h, 64 * h:64 * h + 64] = 1.0
    sh["ones2"] = o2.astype(bfdt); sh["esel2T"] = e2.astype(bfdt)
    sh["onesc"] = np.ones((128, 1), f32).astype(bfdt)
    sh["onesr"] = np.ones((1, 128), f32)
    sh["onesrb"] = np.ones((1, 128), f32).astype(bfdt)
    oh1 = np.zeros((128, 16), f32)
    for h in range(4):
        oh1[:, 4 * h + h] = 1.0
    sh["oneh1"] = oh1.astype(bfdt)
    oh2 = np.zeros((128, 4), f32)
    for h in range(2):
        oh2[:, 2 * h + h] = 1.0
    sh["oneh2"] = oh2.astype(bfdt)
    sele = np.zeros((ER, ER * 128), f32)
    for e in range(ER):
        sele[e, 128 * e:128 * e + 128] = 1.0
    sh["sele"] = sele.astype(bfdt)

    in_maps = []
    row_list = []
    for c in range(8):
        b, rr = c // 4, c % 4
        rows = np.concatenate([np.arange(QT * (4 * j + rr), QT * (4 * j + rr + 1))
                               for j in range(NT)])
        row_list.append((b, rows))
        m = dict(sh)
        m["x"] = np.ascontiguousarray(x[b][rows].T)
        m["cos1"] = np.ascontiguousarray(np.tile(cos1[rows].T, (4, 1)))
        m["sin1"] = np.ascontiguousarray(np.tile(sin1[rows].T, (4, 1)))
        c2t = np.ones((128, TOK), f32); s2t = np.zeros((128, TOK), f32)
        c2t[32:64] = cos2[rows].T; s2t[32:64] = sin2[rows].T
        c2t[96:128] = cos2[rows].T; s2t[96:128] = sin2[rows].T
        m["cos2"] = np.ascontiguousarray(c2t)
        m["sin2"] = np.ascontiguousarray(s2t)
        msk = np.zeros((16, 128, QT), f32)
        for rp in range(4):
            for s in range(4):
                kpos = 512 * rp + 128 * s + np.arange(128)[:, None]
                qpos = 512 * rr + np.arange(QT)[None, :]
                msk[4 * rp + s] = (kpos <= qpos).astype(f32)
        m["mask"] = np.ascontiguousarray(msk.reshape(16 * 128, QT).astype(bfdt))
        in_maps.append(m)
    return in_maps, row_list


_NC_CACHE = None


def kernel(**kw):
    global _NC_CACHE
    in_maps, row_list = _host_inputs(kw)
    if _NC_CACHE is None:
        _NC_CACHE = build_nc()
    nc = _NC_CACHE
    res = run_bass_kernel_spmd(nc, in_maps, core_ids=list(range(8)))
    out = np.zeros((B, T, C), np.float32)
    for c in range(8):
        b, rows = row_list[c]
        out[b][rows] = res.results[c]["y"].T
    return out


if __name__ == "__main__":
    nc = build_nc()
    n = sum(len(bb.instructions) for bb in nc.m.basic_blocks) if hasattr(nc, "m") else -1
    print("built ok")


# version 03
# speedup vs baseline: 1.1203x; 1.1203x over previous
import numpy as np
import ml_dtypes
import contextlib
import concourse.bass as bass
import concourse.mybir as mybir
from concourse.bass_utils import run_bass_kernel_spmd

F32 = mybir.dt.float32
BF16 = mybir.dt.bfloat16
AF = mybir.ActivationFunctionType
ALU = mybir.AluOpType
AX = mybir.AxisListType
bfdt = ml_dtypes.bfloat16

B, T, C = 2, 8192, 1024
QT = 512
NT = 4
TOK = QT * NT
NH1, HD1, L1 = 16, 64, 32
NH2, HD2, HH2, L2 = 16, 64, 32, 256
ER = 7
SC1 = 1.0 / np.sqrt(np.float32(L1))
SC2s = 1.0 / np.sqrt(np.float32(HD2))

ENGINES = ("sync", "tensor", "scalar", "vector", "gpsimd")

KST0, VST0, PT0, MSK0, YH0 = 0, 4096, 8192, 10240, 18432
WOFF = dict(wkvd=0, wqd=2048, wku=4096, wropek=5120, wvu=9216)
WQQ0 = 2048
WRT0 = 7296

CO = dict(ident=0, ones1=128, esel1T=132, ones2=260, esel2T=262, onesc=390,
          l2oT=391, q2lT=455, kv2lT=487, oneh1=519, oneh2=535, onesrb=539,
          sele=668)
FO = dict(r1T=0, r2T=128, wqkT=256, identf=352, onesr=480, l2oT=608, q2lT=672,
          kv2lT=704)
BO = dict(bqkv=288, bq2l=312, bkv2l=313, bwqk=314, bl2o=315, bcproj=316,
          wln1=324, wln2=332, brout=340)
ACC0 = 1024


class Bld:
    def __init__(self):
        self.steps = {e: [] for e in ENGINES}
        self.cnt = {}
        self.sems = {}

    def emit(self, engn, fn):
        self.steps[engn].append(fn)

    def wait(self, engn, sem, val=None):
        v = self.cnt[sem] if val is None else val
        assert v <= self.cnt[sem], f"wait on future {sem} {v}>{self.cnt[sem]}"
        if v <= 0:
            return
        h = self.sems[sem]
        self.emit(engn, lambda eng, h=h, v=v: eng.wait_ge(h, v))

    def step(self, engn, fn, sem=None, amt=1):
        if sem is None:
            self.emit(engn, fn)
            return None
        h = self.sems[sem]
        self.cnt[sem] += amt
        self.emit(engn, lambda eng, fn=fn, h=h, amt=amt: fn(eng).then_inc(h, amt))
        return self.cnt[sem]


def bank(PS, i, n=512):
    return PS[:, 512 * i:512 * i + n]


def build_nc():
    nc = bass.Bass("TRN2", target_bir_lowering=False, debug=False)
    I = {}

    def inp(name, shape, dt=F32):
        I[name] = nc.dram_tensor(name, list(shape), dt, kind="ExternalInput").ap()

    inp("x", (C, TOK))
    inp("cos1", (128, TOK)); inp("sin1", (128, TOK))
    inp("cos2", (128, TOK)); inp("sin2", (128, TOK))
    inp("mask", (16 * 128, QT), BF16)
    inp("wqkv", (C, 3 * C)); inp("bqkv", (128, 24))
    inp("q2lT", (128, 32)); inp("kv2lT", (128, 32))
    inp("bq2l", (128, 1)); inp("bkv2l", (128, 1))
    inp("wqkT", (128, 32)); inp("bwqk", (128, 1))
    inp("l2oT", (128, 64)); inp("bl2o", (128, 1))
    inp("cprojT", (C, C)); inp("bcproj", (128, 8))
    inp("wln1", (128, 8)); inp("wln2", (128, 8))
    inp("wkvd", (C, 256)); inp("wqd", (C, 256))
    inp("wku", (256, 512)); inp("wqq", (256, 1024))
    inp("wropek", (C, 512)); inp("wvu", (256, 1024))
    inp("wo", (C, C))
    inp("wgs", (C, C)); inp("wus", (C, C)); inp("wds", (C, C))
    inp("wgr", (ER, C, C)); inp("wur", (ER, C, C)); inp("wdr", (ER, C, C))
    inp("wrout", (C, ER)); inp("brout", (ER, 1))
    inp("identb", (128, 128), BF16)
    inp("identf", (128, 128))
    inp("r1T", (128, 128)); inp("r2T", (128, 128))
    inp("ones1", (128, 4), BF16); inp("esel1T", (4, 128), BF16)
    inp("ones2", (128, 2), BF16); inp("esel2T", (2, 128), BF16)
    inp("onesc", (128, 1), BF16); inp("onesr", (1, 128))
    inp("oneh1", (128, 16), BF16); inp("oneh2", (128, 4), BF16)
    inp("onesrb", (1, 128), BF16); inp("sele", (ER, ER * 128), BF16)

    y_out = nc.dram_tensor("y", [C, TOK], F32, kind="ExternalOutput").ap()

    ag = {}
    ag["ag1k_i"] = nc.dram_tensor("ag1k_i", [512, TOK], BF16)
    ag["ag1k_o"] = nc.dram_tensor("ag1k_o", [4 * 512, TOK], BF16)
    ag["ag1v_i"] = nc.dram_tensor("ag1v_i", [TOK, 512], BF16)
    ag["ag1v_o"] = nc.dram_tensor("ag1v_o", [4 * TOK, 512], BF16)
    ag["ag2k_i"] = nc.dram_tensor("ag2k_i", [1024, TOK], BF16)
    ag["ag2k_o"] = nc.dram_tensor("ag2k_o", [4 * 1024, TOK], BF16)
    ag["ag2v_i"] = nc.dram_tensor("ag2v_i", [TOK, 1024], BF16)
    ag["ag2v_o"] = nc.dram_tensor("ag2v_o", [4 * TOK, 1024], BF16)
    ag["RG"] = [[0, 1, 2, 3], [4, 5, 6, 7]]

    bld = Bld()
    with contextlib.ExitStack() as ctx:
        E = ctx.enter_context
        U1 = E(nc.sbuf_tensor("U1", [128, 24576], BF16))
        XB = E(nc.sbuf_tensor("XB", [128, 8 * QT], BF16))
        QF = E(nc.sbuf_tensor("QF", [128, 8192], BF16))
        WR = E(nc.sbuf_tensor("WR", [128, 8192], BF16))
        SCb = E(nc.sbuf_tensor("SCb", [128, 3072], BF16))
        CONb = E(nc.sbuf_tensor("CONb", [128, 1664], BF16))
        XR = E(nc.sbuf_tensor("XR", [128, 8 * TOK], F32))
        TAB = E(nc.sbuf_tensor("TAB", [128, 5120], F32))
        ST = E(nc.sbuf_tensor("ST", [128, 1536], F32))
        SC2 = E(nc.sbuf_tensor("SC2", [128, 2048], F32))
        CONf = E(nc.sbuf_tensor("CONf", [128, 1024], F32))
        STG = E(nc.sbuf_tensor("STG", [128, 2048], F32))
        PS = E(nc.psum_tensor("PS", [128, 3584], F32))
        PSB = E(nc.psum_tensor("PSB", [128, 1024], BF16))

        for s in ("dx", "dt", "dm", "dk0", "dk1", "da", "do", "cc", "dwq",
                  "dwa0", "dwa1", "dwg", "dwd", "dwr", "dws0", "dws1",
                  "pe", "act", "dve", "pool"):
            bld.sems[s] = E(nc.semaphore(s))
            bld.cnt[s] = 0

        M = dict(U1=U1, XB=XB, QF=QF, WR=WR, SCb=SCb, CONb=CONb, XR=XR, TAB=TAB,
                 ST=ST, SC2=SC2, CONf=CONf, STG=STG, PS=PS, PSB=PSB)
        build_program(nc, bld, I, y_out, M, ag)

        block = E(nc.Block())

        @block.sync
        def _(eng):
            for fn in bld.steps["sync"]:
                fn(eng)

        @block.tensor
        def _(eng):
            for fn in bld.steps["tensor"]:
                fn(eng)

        @block.scalar
        def _(eng):
            for fn in bld.steps["scalar"]:
                fn(eng)

        @block.vector
        def _(eng):
            for fn in bld.steps["vector"]:
                fn(eng)

        @block.gpsimd
        def _(eng):
            for fn in bld.steps["gpsimd"]:
                fn(eng)

    return nc


def emit_stats_norm(bld, M, t, wcol, mode):
    XR, XB, SCb, ST, SC2, CONb, CONf, PS = (M["XR"], M["XB"], M["SCb"], M["ST"],
                                            M["SC2"], M["CONb"], M["CONf"], M["PS"])
    st, wt = bld.step, bld.wait
    if mode != "rmsxb":
        wt("gpsimd", "dve", bld.cnt["dve"])
        wt("gpsimd", "pe", bld.cnt["pe"])
        for oc in range(8):
            st("gpsimd", lambda e, oc=oc, t=t: e.tensor_copy(
                out=XB[:, 512 * oc:512 * oc + 512],
                in_=XR[:, TOK * oc + QT * t:TOK * oc + QT * t + 512]), "pool", 1)
        wt("tensor", "pool", bld.cnt["pool"])
        wt("scalar", "pool", bld.cnt["pool"])
    else:
        wt("tensor", "dve", bld.cnt["dve"])
        wt("scalar", "dve", bld.cnt["dve"])
    wt("scalar", "pe", bld.cnt["pe"])
    if mode == "ln":
        for oc in range(8):
            st("tensor", lambda e, oc=oc: e.matmul(
                PS[0:1, 512 * 6:512 * 6 + 512], CONb[:, CO["onesc"]:CO["onesc"] + 1],
                XB[:, 512 * oc:512 * oc + 512], start=(oc == 0), stop=(oc == 7)), "pe", 1)
    for oc in range(8):
        if oc >= 2:
            wt("scalar", "pe", bld.cnt["pe"])
        st("scalar", lambda e, oc=oc: e.activation(
            out=SCb[:, 512 * (oc % 2):512 * (oc % 2) + 512],
            in_=XB[:, 512 * oc:512 * oc + 512], func=AF.Square), "act", 1)
        a_sq = bld.cnt["act"]
        wt("tensor", "act", a_sq)
        st("tensor", lambda e, oc=oc: e.matmul(
            PS[0:1, 512 * 5:512 * 5 + 512], CONb[:, CO["onesc"]:CO["onesc"] + 1],
            SCb[:, 512 * (oc % 2):512 * (oc % 2) + 512], start=(oc == 0), stop=(oc == 7)),
            "pe", 1)
    pe_ssq = bld.cnt["pe"]
    wt("scalar", "pe", pe_ssq)
    if mode == "ln":
        st("scalar", lambda e: e.activation(out=ST[0:1, 0:512],
                                            in_=PS[0:1, 512 * 6:512 * 6 + 512],
                                            func=AF.Copy, scale=1.0 / 1024.0), "act", 1)
        st("scalar", lambda e: e.activation(out=ST[0:1, 512:1024],
                                            in_=PS[0:1, 512 * 5:512 * 5 + 512],
                                            func=AF.Copy, scale=1.0 / 1024.0), "act", 1)
        act_ms = bld.cnt["act"]
        wt("vector", "act", act_ms)
        st("vector", lambda e: e.tensor_mul(out=ST[0:1, 1024:1536], in0=ST[0:1, 0:512],
                                            in1=ST[0:1, 0:512]), "dve", 1)
        wt("vector", "dve", bld.cnt["dve"])
        st("vector", lambda e: e.tensor_sub(out=ST[0:1, 512:1024], in0=ST[0:1, 512:1024],
                                            in1=ST[0:1, 1024:1536]), "dve", 1)
        wt("vector", "dve", bld.cnt["dve"])
        st("vector", lambda e: e.tensor_scalar_add(out=ST[0:1, 512:1024],
                                                   in0=ST[0:1, 512:1024], scalar1=1e-5),
           "dve", 1)
        dve_v = bld.cnt["dve"]
        wt("scalar", "dve", dve_v)
        st("scalar", lambda e: e.activation(out=ST[0:1, 512:1024], in_=ST[0:1, 512:1024],
                                            func=AF.Ln), "act", 1)
        wt("scalar", "act", bld.cnt["act"])
        st("scalar", lambda e: e.activation(out=ST[0:1, 512:1024], in_=ST[0:1, 512:1024],
                                            func=AF.Exp, scale=-0.5), "act", 1)
        act_rs = bld.cnt["act"]
        wt("vector", "act", act_rs)
        st("vector", lambda e: e.tensor_mul(out=ST[0:1, 0:512], in0=ST[0:1, 0:512],
                                            in1=ST[0:1, 512:1024]), "dve", 1)
        dve_mr = bld.cnt["dve"]
        wt("tensor", "dve", dve_mr)
        st("tensor", lambda e: e.matmul(PS[:, 512 * 6:512 * 6 + 512],
                                        CONf[0:1, FO["onesr"]:FO["onesr"] + 128],
                                        ST[0:1, 512:1024], start=True, stop=True), "pe", 1)
        st("tensor", lambda e: e.matmul(PS[:, 512 * 5:512 * 5 + 512],
                                        CONf[0:1, FO["onesr"]:FO["onesr"] + 128],
                                        ST[0:1, 0:512], start=True, stop=True), "pe", 1)
        pe_bc = bld.cnt["pe"]
        wt("vector", "pe", pe_bc)
        st("vector", lambda e: e.tensor_copy(out=SC2[:, 1024:1536],
                                             in_=PS[:, 512 * 6:512 * 6 + 512]), "dve", 1)
        st("vector", lambda e: e.tensor_copy(out=SC2[:, 1536:2048],
                                             in_=PS[:, 512 * 5:512 * 5 + 512]), "dve", 1)
        wt("vector", "dve", bld.cnt["dve"])
        for oc in range(8):
            st("vector", lambda e, oc=oc, t=t: e.tensor_mul(
                out=XR[:, TOK * oc + QT * t:TOK * oc + QT * t + 512],
                in0=XR[:, TOK * oc + QT * t:TOK * oc + QT * t + 512],
                in1=SC2[:, 1024:1536]), "dve", 1)
            wt("vector", "dve", bld.cnt["dve"])
            st("vector", lambda e, oc=oc, t=t: e.tensor_sub(
                out=XR[:, TOK * oc + QT * t:TOK * oc + QT * t + 512],
                in0=XR[:, TOK * oc + QT * t:TOK * oc + QT * t + 512],
                in1=SC2[:, 1536:2048]), "dve", 1)
    else:
        st("scalar", lambda e: e.activation(out=ST[0:1, 512:1024],
                                            in_=PS[0:1, 512 * 5:512 * 5 + 512],
                                            func=AF.Sqrt, scale=1.0 / 1024.0), "act", 1)
        act_rt = bld.cnt["act"]
        wt("vector", "act", act_rt)
        st("vector", lambda e: e.tensor_scalar_add(out=ST[0:1, 512:1024],
                                                   in0=ST[0:1, 512:1024], scalar1=1e-5),
           "dve", 1)
        dve_e = bld.cnt["dve"]
        wt("scalar", "dve", dve_e)
        st("scalar", lambda e: e.activation(out=ST[0:1, 512:1024], in_=ST[0:1, 512:1024],
                                            func=AF.Ln), "act", 1)
        wt("scalar", "act", bld.cnt["act"])
        st("scalar", lambda e: e.activation(out=ST[0:1, 512:1024], in_=ST[0:1, 512:1024],
                                            func=AF.Exp, scale=-1.0), "act", 1)
        act_rc = bld.cnt["act"]
        wt("tensor", "act", act_rc)
        st("tensor", lambda e: e.matmul(PS[:, 512 * 6:512 * 6 + 512],
                                        CONf[0:1, FO["onesr"]:FO["onesr"] + 128],
                                        ST[0:1, 512:1024], start=True, stop=True), "pe", 1)
        pe_bc = bld.cnt["pe"]
        wt("vector", "pe", pe_bc)
        st("vector", lambda e: e.tensor_copy(out=SC2[:, 1024:1536],
                                             in_=PS[:, 512 * 6:512 * 6 + 512]), "dve", 1)
        wt("vector", "dve", bld.cnt["dve"])
        for oc in range(8):
            if mode == "rms":
                st("vector", lambda e, oc=oc, t=t, wcol=wcol: e.scalar_tensor_tensor(
                    out=XB[:, 512 * oc:512 * oc + 512],
                    in0=XR[:, TOK * oc + QT * t:TOK * oc + QT * t + 512],
                    scalar=CONf[:, wcol + oc:wcol + oc + 1], in1=SC2[:, 1024:1536],
                    op0=ALU.mult, op1=ALU.mult), "dve", 1)
            else:
                st("vector", lambda e, oc=oc, wcol=wcol: e.scalar_tensor_tensor(
                    out=XB[:, 512 * oc:512 * oc + 512], in0=XB[:, 512 * oc:512 * oc + 512],
                    scalar=CONf[:, wcol + oc:wcol + oc + 1], in1=SC2[:, 1024:1536],
                    op0=ALU.mult, op1=ALU.mult), "dve", 1)


def build_program(nc, bld, I, y_out, M, ag):
    U1, XB, QF, WR = M["U1"], M["XB"], M["QF"], M["WR"]
    SCb, CONb = M["SCb"], M["CONb"]
    XR, TAB, ST, SC2, CONf = M["XR"], M["TAB"], M["ST"], M["SC2"], M["CONf"]
    STG = M["STG"]
    PS, PSB = M["PS"], M["PSB"]
    st, wt = bld.step, bld.wait

    stg = {"pp": 0, "pool_done": {0: 0, 1: 0}}

    def wload(dst, src2d, ncols, kcn, wait_pe=None):
        pp = stg["pp"]; stg["pp"] ^= 1
        wt("sync", "pool", stg["pool_done"][pp])
        st("sync", lambda e, pp=pp, src2d=src2d, ncols=ncols, kcn=kcn: e.dma_start(
            out=STG[:, 1024 * pp:1024 * pp + ncols].rearrange("p (k f) -> p k f", k=kcn),
            in_=src2d.rearrange("(k p) f -> p k f", p=128)), "dws%d" % pp, 16)
        v = bld.cnt["dws%d" % pp]
        wt("gpsimd", "dws%d" % pp, v)
        if wait_pe is not None:
            wt("gpsimd", "pe", wait_pe)
        st("gpsimd", lambda e, pp=pp, dst=dst, ncols=ncols: e.tensor_copy(
            out=dst, in_=STG[:, 1024 * pp:1024 * pp + ncols]), "pool", 1)
        stg["pool_done"][pp] = bld.cnt["pool"]
        return bld.cnt["pool"]

    for cc in range(8):
        st("sync", lambda e, cc=cc: e.dma_start(
            out=XR[:, TOK * cc:TOK * (cc + 1)], in_=I["x"][128 * cc:128 * (cc + 1), :]),
            "dx", 16)
    for nm, sl in (("identb", (CO["ident"], 128, 128)), ("ones1", (CO["ones1"], 4, 128)),
                   ("ones2", (CO["ones2"], 2, 128)), ("onesc", (CO["onesc"], 1, 128)),
                   ("oneh1", (CO["oneh1"], 16, 128)), ("oneh2", (CO["oneh2"], 4, 128))):
        off, w, p = sl
        st("sync", lambda e, nm=nm, off=off, w=w: e.dma_start(
            out=CONb[:, off:off + w], in_=I[nm]), "dx", 16)
    st("sync", lambda e: e.dma_start(out=CONb[0:4, CO["esel1T"]:CO["esel1T"] + 128],
                                     in_=I["esel1T"]), "dx", 16)
    st("sync", lambda e: e.dma_start(out=CONb[0:2, CO["esel2T"]:CO["esel2T"] + 128],
                                     in_=I["esel2T"]), "dx", 16)
    st("sync", lambda e: e.dma_start(out=CONb[0:1, CO["onesrb"]:CO["onesrb"] + 128],
                                     in_=I["onesrb"]), "dx", 16)
    st("sync", lambda e: e.dma_start(out=CONb[0:ER, CO["sele"]:CO["sele"] + ER * 128],
                                     in_=I["sele"]), "dx", 16)
    for nm, off, w in (("r1T", FO["r1T"], 128), ("r2T", FO["r2T"], 128),
                       ("wqkT", FO["wqkT"], 32), ("identf", FO["identf"], 128),
                       ("l2oT", FO["l2oT"], 64), ("q2lT", FO["q2lT"], 32),
                       ("kv2lT", FO["kv2lT"], 32)):
        st("sync", lambda e, nm=nm, off=off, w=w: e.dma_start(
            out=CONf[:, off:off + w], in_=I[nm]), "dx", 16)
    st("sync", lambda e: e.dma_start(out=CONf[0:1, FO["onesr"]:FO["onesr"] + 128],
                                     in_=I["onesr"]), "dx", 16)
    for nm, off, w in (("bqkv", BO["bqkv"], 24), ("bq2l", BO["bq2l"], 1),
                       ("bkv2l", BO["bkv2l"], 1), ("bwqk", BO["bwqk"], 1),
                       ("bl2o", BO["bl2o"], 1), ("bcproj", BO["bcproj"], 8),
                       ("wln1", BO["wln1"], 8), ("wln2", BO["wln2"], 8)):
        st("sync", lambda e, nm=nm, off=off, w=w: e.dma_start(
            out=CONf[:, off:off + w], in_=I[nm]), "dx", 16)
    st("sync", lambda e: e.dma_start(out=CONf[0:ER, BO["brout"]:BO["brout"] + 1],
                                     in_=I["brout"]), "dx", 16)
    dx_pre = bld.cnt["dx"]

    wt("gpsimd", "dx", dx_pre)
    st("gpsimd", lambda e: e.tensor_copy(out=CONb[:, CO["l2oT"]:CO["l2oT"] + 64],
                                         in_=CONf[:, FO["l2oT"]:FO["l2oT"] + 64]), "pool", 1)
    st("gpsimd", lambda e: e.tensor_copy(out=CONb[:, CO["q2lT"]:CO["q2lT"] + 32],
                                         in_=CONf[:, FO["q2lT"]:FO["q2lT"] + 32]), "pool", 1)
    st("gpsimd", lambda e: e.tensor_copy(out=CONb[:, CO["kv2lT"]:CO["kv2lT"] + 32],
                                         in_=CONf[:, FO["kv2lT"]:FO["kv2lT"] + 32]), "pool", 1)
    pool_consts = bld.cnt["pool"]

    for oc in range(24):
        wload(U1[:, 1024 * oc:1024 * oc + 1024],
              I["wqkv"][:, 128 * oc:128 * oc + 128], 1024, 8)
    dw_qkv = bld.cnt["pool"]

    for t in range(NT):
        wt("sync", "dve", bld.cnt["dve"])
        st("sync", lambda e, t=t: e.dma_start(out=TAB[:, 0:512],
                                              in_=I["cos1"][:, QT * t:QT * t + 512]), "dt", 16)
        st("sync", lambda e, t=t: e.dma_start(out=TAB[:, 512:1024],
                                              in_=I["sin1"][:, QT * t:QT * t + 512]), "dt", 16)
        dx_tab = bld.cnt["dt"]
        wt("gpsimd", "dx", dx_pre)
        wt("gpsimd", "pe", bld.cnt["pe"])
        for cc in range(8):
            st("gpsimd", lambda e, t=t, cc=cc: e.tensor_copy(
                out=XB[:, QT * cc:QT * (cc + 1)],
                in_=XR[:, TOK * cc + QT * t:TOK * cc + QT * t + QT]), "pool", 1)
        pool_xb = bld.cnt["pool"]
        wt("tensor", "pool", pool_xb)
        if t == 0:
            wt("tensor", "pool", dw_qkv)
            wt("tensor", "dx", dx_pre)
        for hg in range(4):
            ocs = [2 * hg, 2 * hg + 1, 8 + 2 * hg, 8 + 2 * hg + 1,
                   16 + 2 * hg, 16 + 2 * hg + 1]
            wt("vector", "da", bld.cnt["da"])
            for i6, oc in enumerate(ocs):
                bk = i6 % 2
                wt("tensor", "dve", bld.cnt["dve"])
                for kc in range(8):
                    st("tensor", lambda e, bk=bk, oc=oc, kc=kc: e.matmul(
                        bank(PS, bk), U1[:, 1024 * oc + 128 * kc:1024 * oc + 128 * (kc + 1)],
                        XB[:, QT * kc:QT * (kc + 1)],
                        start=(kc == 0), stop=(kc == 7)), "pe", 1)
                pe_mm = bld.cnt["pe"]
                wt("vector", "pe", pe_mm)
                st("vector", lambda e, bk=bk, i6=i6, oc=oc: e.tensor_scalar_add(
                    out=SCb[:, 512 * i6:512 * (i6 + 1)], in0=bank(PS, bk),
                    scalar1=CONf[:, BO["bqkv"] + oc:BO["bqkv"] + oc + 1]),
                    "dve", 1)
            dve_qkv = bld.cnt["dve"]
            wt("tensor", "dve", dve_qkv)
            if t == 0 and hg == 0:
                wt("tensor", "pool", pool_consts)
            for src, wname, bnk in ((0, "q2lT", 2), (2, "kv2lT", 3), (4, "kv2lT", 4)):
                for hh in range(4):
                    rb = 64 * (hh % 2)
                    cb = 32 * hh
                    ch = src + hh // 2
                    st("tensor", lambda e, rb=rb, cb=cb, ch=ch, wname=wname, bnk=bnk, hh=hh: e.matmul(
                        PS[cb:cb + 32, 512 * bnk:512 * bnk + 512],
                        CONb[rb:rb + 64, CO[wname]:CO[wname] + 32],
                        SCb[rb:rb + 64, 512 * ch:512 * ch + 512],
                        start=True, stop=True, tile_position=(rb, cb)), "pe", 1)
            pe_lat = bld.cnt["pe"]
            wt("vector", "pe", pe_lat)
            wt("vector", "pool", bld.cnt["pool"])
            st("vector", lambda e: e.tensor_scalar_add(out=SC2[:, 0:512], in0=bank(PS, 2),
                scalar1=CONf[:, BO["bq2l"]:BO["bq2l"] + 1]), "dve", 1)
            st("vector", lambda e: e.tensor_scalar_add(out=SC2[:, 512:1024], in0=bank(PS, 3),
                scalar1=CONf[:, BO["bkv2l"]:BO["bkv2l"] + 1]), "dve", 1)
            st("vector", lambda e: e.tensor_scalar_add(out=SCb[:, 0:512], in0=bank(PS, 4),
                scalar1=CONf[:, BO["bkv2l"]:BO["bkv2l"] + 1]), "dve", 1)
            act_lat = bld.cnt["dve"]
            for off in (0, 512):
                wt("tensor", "dve", bld.cnt["dve"])
                st("tensor", lambda e, off=off: e.matmul(
                    bank(PS, 5), CONf[:, FO["r1T"]:FO["r1T"] + 128], SC2[:, off:off + 512],
                    start=True, stop=True), "pe", 1)
                pe_rot = bld.cnt["pe"]
                wt("vector", "pe", pe_rot)
                if hg == 0 and off == 0:
                    wt("vector", "dt", dx_tab)
                st("vector", lambda e, off=off: e.tensor_mul(
                    out=SC2[:, 1024 + off:1024 + off + 512], in0=SC2[:, off:off + 512],
                    in1=TAB[:, 0:512]), "dve", 1)
                wt("vector", "dve", bld.cnt["dve"])
                st("vector", lambda e, off=off: e.tensor_mul(
                    out=SC2[:, off:off + 512], in0=bank(PS, 5), in1=TAB[:, 512:1024]), "dve", 1)
                wt("vector", "dve", bld.cnt["dve"])
                st("vector", lambda e, off=off: e.tensor_add(
                    out=SC2[:, off:off + 512], in0=SC2[:, off:off + 512],
                    in1=SC2[:, 1024 + off:1024 + off + 512]), "dve", 1)
            dve_rope = bld.cnt["dve"]
            wt("tensor", "dve", dve_rope)
            for hh in range(4):
                st("tensor", lambda e, hh=hh: e.matmul(
                    PS[32 * hh:32 * hh + 32, 512 * 6:512 * 6 + 512],
                    CONf[32 * hh:32 * hh + 32, FO["wqkT"]:FO["wqkT"] + 32],
                    SC2[32 * hh:32 * hh + 32, 0:512],
                    start=True, stop=True, tile_position=(32 * hh, 32 * hh)), "pe", 1)
            pe_qf = bld.cnt["pe"]
            wt("vector", "pe", pe_qf)
            st("vector", lambda e, hg=hg, t=t: e.tensor_scalar_add(
                out=QF[:, 2048 * hg + 512 * t:2048 * hg + 512 * t + 512], in0=bank(PS, 6),
                scalar1=CONf[:, BO["bwqk"]:BO["bwqk"] + 1]), "dve", 1)
            act_qf = bld.cnt["dve"]
            wt("gpsimd", "dve", dve_rope)
            wt("gpsimd", "da", bld.cnt["da"])
            st("gpsimd", lambda e: e.tensor_copy(out=SCb[:, 512:1024],
                                                 in_=SC2[:, 512:1024]), "pool", 1)
            pool_k = bld.cnt["pool"]
            wt("sync", "pool", pool_k)
            st("sync", lambda e, hg=hg, t=t: e.dma_start(
                out=ag["ag1k_i"].ap()[128 * hg:128 * hg + 128, QT * t:QT * (t + 1)],
                in_=SCb[:, 512:1024]), "da", 16)
            for s in range(4):
                wt("tensor", "dve", act_lat)
                st("tensor", lambda e, s=s: e.transpose(
                    PSB[:, 128 * s:128 * s + 128], SCb[:, 128 * s:128 * s + 128],
                    CONb[:, CO["ident"]:CO["ident"] + 128]), "pe", 1)
            pe_tp = bld.cnt["pe"]
            wt("vector", "pe", pe_tp)
            wt("vector", "da", bld.cnt["da"])
            st("vector", lambda e: e.tensor_copy(out=SCb[:, 1024:1536],
                                                 in_=PSB[:, 0:512]), "dve", 1)
            dve_vt = bld.cnt["dve"]
            wt("sync", "dve", dve_vt)
            st("sync", lambda e, hg=hg, t=t: e.dma_start(
                out=ag["ag1v_i"].ap()[QT * t:QT * t + 512, 128 * hg:128 * hg + 128]
                .rearrange("(s p) f -> p s f", p=128),
                in_=SCb[:, 1024:1536].rearrange("p (s f) -> p s f", s=4)), "da", 16)
            wt("scalar", "dve", act_qf)
            st("scalar", lambda e, hg=hg, t=t: e.activation(
                out=SCb[:, 2048:2560],
                in_=QF[:, 2048 * hg + 512 * t:2048 * hg + 512 * t + 512],
                func=AF.Square), "act", 1)
            act_sq = bld.cnt["act"]
            wt("tensor", "act", act_sq)
            wt("tensor", "dve", bld.cnt["dve"])
            st("tensor", lambda e: e.matmul(
                PS[0:4, 512 * 6:512 * 6 + 512], CONb[:, CO["ones1"]:CO["ones1"] + 4],
                SCb[:, 2048:2560], start=True, stop=True), "pe", 1)
            pe_q2 = bld.cnt["pe"]
            wt("vector", "pe", pe_q2)
            st("vector", lambda e, hg=hg, t=t: e.reduce_max(
                ST[0:4, 64 + 16 * hg + t:64 + 16 * hg + t + 1],
                PS[0:4, 512 * 6:512 * 6 + 512], axis=AX.X), "dve", 1)
    da_A = bld.cnt["da"]

    wt("gpsimd", "da", da_A)
    st("gpsimd", lambda e: e.collective_compute(
        "AllGather", ALU.bypass, replica_groups=ag["RG"],
        ins=[ag["ag1k_i"].ap()], outs=[ag["ag1k_o"].ap()]), "cc", 1)
    wt("gpsimd", "cc", bld.cnt["cc"])
    st("gpsimd", lambda e: e.collective_compute(
        "AllGather", ALU.bypass, replica_groups=ag["RG"],
        ins=[ag["ag1v_i"].ap()], outs=[ag["ag1v_o"].ap()]), "cc", 1)
    cc1 = bld.cnt["cc"]

    wt("gpsimd", "cc", cc1)
    wt("sync", "cc", cc1)
    last_pe_kst = {0: 0, 1: 0}
    for hg in range(4):
        for rr in range(4):
            pp = rr % 2
            wt("sync", "pe", last_pe_kst[pp])
            st("sync", lambda e, hg=hg, rr=rr, pp=pp: e.dma_start(
                out=U1[:, KST0 + 2048 * pp:KST0 + 2048 * pp + 2048],
                in_=ag["ag1k_o"].ap()[512 * rr + 128 * hg:512 * rr + 128 * hg + 128, :]),
                "dk%d" % pp, 16)
            dk_n = bld.cnt["dk%d" % pp]
            wt("scalar", "dk%d" % pp, dk_n)
            for q4 in range(4):
                wt("scalar", "pe", bld.cnt["pe"])
                st("scalar", lambda e, pp=pp, q4=q4: e.activation(
                    out=SCb[:, 512 * (q4 % 2):512 * (q4 % 2) + 512],
                    in_=U1[:, KST0 + 2048 * pp + 512 * q4:KST0 + 2048 * pp + 512 * q4 + 512],
                    func=AF.Square), "act", 1)
                a_n = bld.cnt["act"]
                wt("tensor", "act", a_n)
                wt("tensor", "dve", bld.cnt["dve"])
                st("tensor", lambda e, q4=q4: e.matmul(
                    PS[0:4, 512 * 6:512 * 6 + 512], CONb[:, CO["ones1"]:CO["ones1"] + 4],
                    SCb[:, 512 * (q4 % 2):512 * (q4 % 2) + 512], start=True, stop=True),
                    "pe", 1)
                p_n = bld.cnt["pe"]
                wt("vector", "pe", p_n)
                st("vector", lambda e, hg=hg, rr=rr, q4=q4: e.reduce_max(
                    ST[0:4, 80 + 16 * hg + 4 * rr + q4:81 + 16 * hg + 4 * rr + q4],
                    PS[0:4, 512 * 6:512 * 6 + 512], axis=AX.X), "dve", 1)
            last_pe_kst[pp] = bld.cnt["pe"]
    wt("vector", "dve", bld.cnt["dve"])
    for hg in range(4):
        st("vector", lambda e, hg=hg: e.reduce_max(
            ST[0:4, 32 + hg:33 + hg], ST[0:4, 64 + 16 * hg:64 + 16 * hg + 4], axis=AX.X),
            "dve", 1)
        st("vector", lambda e, hg=hg: e.reduce_max(
            ST[0:4, 40 + hg:41 + hg], ST[0:4, 80 + 16 * hg:80 + 16 * hg + 16], axis=AX.X),
            "dve", 1)
    dve_mx = bld.cnt["dve"]
    wt("scalar", "dve", dve_mx)
    st("scalar", lambda e: e.activation(out=ST[0:4, 48:52], in_=ST[0:4, 32:36],
                                        func=AF.Sqrt), "act", 1)
    st("scalar", lambda e: e.activation(out=ST[0:4, 52:56], in_=ST[0:4, 40:44],
                                        func=AF.Sqrt), "act", 1)
    act_sq2 = bld.cnt["act"]
    wt("vector", "act", act_sq2)
    st("vector", lambda e: e.tensor_mul(out=ST[0:4, 56:60], in0=ST[0:4, 48:52],
                                        in1=ST[0:4, 52:56]), "dve", 1)
    wt("vector", "dve", bld.cnt["dve"])
    st("vector", lambda e: e.tensor_scalar_mul(out=ST[0:4, 56:60], in0=ST[0:4, 56:60],
                                               scalar1=-float(SC1)), "dve", 1)
    wt("vector", "dve", bld.cnt["dve"])
    st("vector", lambda e: e.tensor_copy(out=SCb[0:4, 0:4], in_=ST[0:4, 56:60]), "dve", 1)
    dve_bias = bld.cnt["dve"]
    wt("tensor", "dve", dve_bias)
    for hg in range(4):
        wt("tensor", "dve", bld.cnt["dve"])
        st("tensor", lambda e, hg=hg: e.matmul(
            PS[:, 512 * 6:512 * 6 + 1], CONb[0:4, CO["esel1T"]:CO["esel1T"] + 128],
            SCb[0:4, hg:hg + 1], start=True, stop=True), "pe", 1)
        p_b = bld.cnt["pe"]
        wt("vector", "pe", p_b)
        st("vector", lambda e, hg=hg: e.tensor_copy(
            out=ST[:, 8 + hg:9 + hg], in_=PS[:, 512 * 6:512 * 6 + 1]), "dve", 1)
    dve_bx = bld.cnt["dve"]

    for t in range(NT):
        wt("sync", "dve", bld.cnt["dve"])
        for mi in range(4):
            st("sync", lambda e, mi=mi: e.dma_start(
                out=U1[:, MSK0 + 2048 * mi:MSK0 + 2048 * mi + 2048]
                .rearrange("p (a f) -> p a f", a=4),
                in_=I["mask"][512 * mi:512 * mi + 512, :]
                .rearrange("(a p) f -> p a f", p=128)), "dm", 16)
        dk_msk = bld.cnt["dm"]
        for hg in range(4):
            first = True
            wt("tensor", "dve", bld.cnt["dve"])
            for m2 in range(t + 1):
                pp = m2 % 2
                wt("sync", "pe", last_pe_kst[pp])
                for rr in range(4):
                    st("sync", lambda e, rr=rr, hg=hg, pp=pp, m2=m2: e.dma_start(
                        out=U1[:, KST0 + 2048 * pp + 512 * rr:KST0 + 2048 * pp + 512 * rr + 512],
                        in_=ag["ag1k_o"].ap()[512 * rr + 128 * hg:512 * rr + 128 * hg + 128,
                                              QT * m2:QT * m2 + 512]), "dk%d" % pp, 16)
                    st("sync", lambda e, rr=rr, hg=hg, pp=pp, m2=m2: e.dma_start(
                        out=U1[:, VST0 + 2048 * pp + 512 * rr:VST0 + 2048 * pp + 512 * rr + 512]
                        .rearrange("p (s f) -> p s f", s=4),
                        in_=ag["ag1v_o"].ap()[TOK * rr + QT * m2:TOK * rr + QT * m2 + 512,
                                              128 * hg:128 * hg + 128]
                        .rearrange("(s p) f -> p s f", p=128)), "dk%d" % pp, 16)
                dk_kv = bld.cnt["dk%d" % pp]
                for rr in range(4):
                    for kb in range(4):
                        is_last = (m2 == t and rr == 3 and kb == 3)
                        wt("tensor", "dk%d" % pp, dk_kv)
                        wt("tensor", "act", bld.cnt["act"])
                        for hh in range(4):
                            st("tensor", lambda e, hh=hh, pp=pp, rr=rr, kb=kb, hg=hg, t=t: e.matmul(
                                bank(PS, hh),
                                U1[32 * hh:32 * hh + 32,
                                   KST0 + 2048 * pp + 512 * rr + 128 * kb:
                                   KST0 + 2048 * pp + 512 * rr + 128 * kb + 128],
                                QF[32 * hh:32 * hh + 32,
                                   2048 * hg + 512 * t:2048 * hg + 512 * t + 512],
                                start=True, stop=True, skip_group_check=True,
                                tile_position=(32 * hh, 0)), "pe", 1)
                        pe_s = bld.cnt["pe"]
                        wt("scalar", "pe", pe_s)
                        st("scalar", lambda e, hg=hg: e.activation(
                            out=U1[:, PT0:PT0 + 2048], in_=PS[:, 0:2048], func=AF.Exp,
                            bias=ST[:, 8 + hg:9 + hg], scale=float(SC1)), "act", 1)
                        act_e = bld.cnt["act"]
                        if m2 == t:
                            wt("vector", "act", act_e)
                            if rr == 0 and kb == 0:
                                wt("vector", "dm", dk_msk)
                            for hh in range(4):
                                st("vector", lambda e, hh=hh, rr=rr, kb=kb: e.tensor_mul(
                                    out=U1[:, PT0 + 512 * hh:PT0 + 512 * hh + 512],
                                    in0=U1[:, PT0 + 512 * hh:PT0 + 512 * hh + 512],
                                    in1=U1[:, MSK0 + 512 * (4 * rr + kb):
                                           MSK0 + 512 * (4 * rr + kb) + 512]), "dve", 1)
                            wt("tensor", "dve", bld.cnt["dve"])
                        else:
                            wt("tensor", "act", act_e)
                        for hh in range(4):
                            st("tensor", lambda e, hh=hh, pp=pp, rr=rr, kb=kb, first=first,
                               is_last=is_last: e.matmul(
                                PS[32 * hh:32 * hh + 32, 512 * 4:512 * 4 + 512],
                                U1[:, VST0 + 2048 * pp + 512 * rr + 128 * kb + 32 * hh:
                                   VST0 + 2048 * pp + 512 * rr + 128 * kb + 32 * hh + 32],
                                U1[:, PT0 + 512 * hh:PT0 + 512 * hh + 512],
                                start=first, stop=is_last, skip_group_check=True,
                                tile_position=(0, 32 * hh)), "pe", 1)
                        for hh in range(4):
                            st("tensor", lambda e, hh=hh, first=first, is_last=is_last: e.matmul(
                                PS[0:4, 512 * 5:512 * 5 + 512],
                                CONb[:, CO["oneh1"] + 4 * hh:CO["oneh1"] + 4 * hh + 4],
                                U1[:, PT0 + 512 * hh:PT0 + 512 * hh + 512],
                                start=(first and hh == 0), stop=(is_last and hh == 3),
                                skip_group_check=True), "pe", 1)
                        first = False
                last_pe_kst[pp] = bld.cnt["pe"]
            pe_attn = bld.cnt["pe"]
            wt("scalar", "pe", pe_attn)
            st("scalar", lambda e: e.activation(out=ST[0:4, 1024:1536],
                                                in_=PS[0:4, 512 * 5:512 * 5 + 512],
                                                func=AF.Ln), "act", 1)
            wt("scalar", "act", bld.cnt["act"])
            st("scalar", lambda e: e.activation(out=ST[0:4, 1024:1536], in_=ST[0:4, 1024:1536],
                                                func=AF.Exp, scale=-1.0), "act", 1)
            act_rc = bld.cnt["act"]
            wt("vector", "act", act_rc)
            st("vector", lambda e: e.tensor_copy(out=SCb[0:4, 1024:1536],
                                                 in_=ST[0:4, 1024:1536]), "dve", 1)
            dve_rcb = bld.cnt["dve"]
            wt("tensor", "dve", dve_rcb)
            st("tensor", lambda e: e.matmul(
                bank(PS, 6), CONb[0:4, CO["esel1T"]:CO["esel1T"] + 128], SCb[0:4, 1024:1536],
                start=True, stop=True), "pe", 1)
            pe_rb = bld.cnt["pe"]
            wt("vector", "pe", pe_rb)
            st("vector", lambda e: e.tensor_copy(out=SC2[:, 0:512], in_=bank(PS, 6)), "dve", 1)
            dve_rb2 = bld.cnt["dve"]
            wt("vector", "dve", dve_rb2)
            st("vector", lambda e: e.tensor_mul(out=SCb[:, 2048:2560], in0=bank(PS, 4),
                                                in1=SC2[:, 0:512]), "dve", 1)
            dve_yn = bld.cnt["dve"]
            wt("tensor", "dve", dve_yn)
            for hh in range(4):
                bnk = 5 + hh // 2
                st("tensor", lambda e, hh=hh, bnk=bnk: e.matmul(
                    PS[64 * (hh % 2):64 * (hh % 2) + 64, 512 * bnk:512 * bnk + 512],
                    CONb[32 * hh:32 * hh + 32, CO["l2oT"]:CO["l2oT"] + 64],
                    SCb[32 * hh:32 * hh + 32, 2048:2560],
                    start=True, stop=True, tile_position=(32 * hh, 64 * (hh % 2))), "pe", 1)
            pe_l2o = bld.cnt["pe"]
            wt("vector", "pe", pe_l2o)
            for half in range(2):
                st("vector", lambda e, hg=hg, half=half: e.tensor_scalar_add(
                    out=U1[:, YH0 + 1024 * hg + 512 * half:YH0 + 1024 * hg + 512 * half + 512],
                    in0=bank(PS, 5 + half),
                    scalar1=CONf[:, BO["bl2o"]:BO["bl2o"] + 1]), "dve", 1)
        act_yh = bld.cnt["dve"]
        last_pe_cp = {0: bld.cnt["pe"], 1: bld.cnt["pe"]}
        for oc in range(8):
            dw_c = wload(WR[:, 1024 * (oc % 2):1024 * (oc % 2) + 1024],
                         I["cprojT"][:, 128 * oc:128 * oc + 128], 1024, 8,
                         wait_pe=last_pe_cp[oc % 2])
            wt("tensor", "pool", dw_c)
            wt("tensor", "dve", bld.cnt["dve"])
            for kc in range(8):
                st("tensor", lambda e, oc=oc, kc=kc: e.matmul(
                    bank(PS, oc % 2),
                    WR[:, 1024 * (oc % 2) + 128 * kc:1024 * (oc % 2) + 128 * kc + 128],
                    U1[:, YH0 + 512 * kc:YH0 + 512 * kc + 512],
                    start=(kc == 0), stop=(kc == 7)), "pe", 1)
            last_pe_cp[oc % 2] = bld.cnt["pe"]
            pe_cp = bld.cnt["pe"]
            wt("vector", "pe", pe_cp)
            st("vector", lambda e, oc=oc, t=t: e.scalar_tensor_tensor(
                out=XR[:, TOK * oc + QT * t:TOK * oc + QT * t + 512], in0=bank(PS, oc % 2),
                scalar=CONf[:, BO["bcproj"] + oc:BO["bcproj"] + oc + 1],
                in1=XR[:, TOK * oc + QT * t:TOK * oc + QT * t + 512],
                op0=ALU.add, op1=ALU.add), "dve", 1)
        emit_stats_norm(bld, M, t, None, "ln")

    wt("gpsimd", "dve", bld.cnt["dve"])
    wt("gpsimd", "act", bld.cnt["act"])
    wt("gpsimd", "pe", bld.cnt["pe"])
    for wname, kcn, mcn in (("wkvd", 8, 2), ("wqd", 8, 2), ("wku", 2, 4),
                            ("wropek", 8, 4), ("wvu", 2, 8)):
        for mc in range(mcn):
            wload(U1[:, WOFF[wname] + 128 * kcn * mc:WOFF[wname] + 128 * kcn * (mc + 1)],
                  I[wname][:, 128 * mc:128 * mc + 128], 128 * kcn, kcn)
    for mc in range(8):
        wload(WR[:, WQQ0 + 256 * mc:WQQ0 + 256 * (mc + 1)],
              I["wqq"][:, 128 * mc:128 * mc + 128], 256, 2)
    dw_C = bld.cnt["pool"]

    for t in range(NT):
        wt("sync", "dve", bld.cnt["dve"])
        st("sync", lambda e, t=t: e.dma_start(out=TAB[:, 0:512],
                                              in_=I["cos2"][:, QT * t:QT * t + 512]), "dt", 16)
        st("sync", lambda e, t=t: e.dma_start(out=TAB[:, 512:1024],
                                              in_=I["sin2"][:, QT * t:QT * t + 512]), "dt", 16)
        dx_tab = bld.cnt["dt"]
        emit_stats_norm(bld, M, t, BO["wln1"], "rms")
        wt("tensor", "dve", bld.cnt["dve"])
        if t == 0:
            wt("tensor", "pool", dw_C)
        for mc in range(2):
            for src, bnk in (("wkvd", 0), ("wqd", 2)):
                for kc in range(8):
                    st("tensor", lambda e, src=src, mc=mc, kc=kc, bnk=bnk: e.matmul(
                        bank(PS, bnk + mc),
                        U1[:, WOFF[src] + 128 * (8 * mc + kc):WOFF[src] + 128 * (8 * mc + kc) + 128],
                        XB[:, QT * kc:QT * (kc + 1)], start=(kc == 0), stop=(kc == 7)), "pe", 1)
        pe_lat2 = bld.cnt["pe"]
        wt("scalar", "pe", pe_lat2)
        for mc in range(2):
            st("scalar", lambda e, mc=mc: e.activation(
                out=SCb[:, 512 * mc:512 * mc + 512], in_=bank(PS, mc), func=AF.Copy), "act", 1)
        for mc in range(2):
            st("scalar", lambda e, mc=mc, t=t: e.activation(
                out=QF[:, 512 * (2 * t + mc):512 * (2 * t + mc) + 512],
                in_=bank(PS, 2 + mc), func=AF.Copy), "act", 1)
        act_lat2 = bld.cnt["act"]
        for g in range(8):
            bnk = 4 + g % 2
            wt("tensor", "act", bld.cnt["act"])
            wt("tensor", "dve", bld.cnt["dve"])
            for hh in range(2):
                h = 2 * g + hh
                for kc in range(2):
                    st("tensor", lambda e, h=h, hh=hh, kc=kc, bnk=bnk: e.matmul(
                        PS[64 * hh:64 * hh + 32, 512 * bnk:512 * bnk + 512],
                        U1[:, WOFF["wku"] + 128 * (2 * (h // 4) + kc) + 32 * (h % 4):
                           WOFF["wku"] + 128 * (2 * (h // 4) + kc) + 32 * (h % 4) + 32],
                        SCb[:, 512 * kc:512 * kc + 512],
                        start=(kc == 0), stop=(kc == 1),
                        tile_position=(0, 64 * hh)), "pe", 1)
                for kc in range(8):
                    st("tensor", lambda e, h=h, hh=hh, kc=kc, bnk=bnk: e.matmul(
                        PS[64 * hh + 32:64 * hh + 64, 512 * bnk:512 * bnk + 512],
                        U1[:, WOFF["wropek"] + 128 * (8 * (h // 4) + kc) + 32 * (h % 4):
                           WOFF["wropek"] + 128 * (8 * (h // 4) + kc) + 32 * (h % 4) + 32],
                        XB[:, QT * kc:QT * (kc + 1)],
                        start=(kc == 0), stop=(kc == 7),
                        tile_position=(0, 64 * hh + 32)), "pe", 1)
            pe_kk = bld.cnt["pe"]
            wt("scalar", "pe", pe_kk)
            st("scalar", lambda e, bnk=bnk: e.activation(out=SC2[:, 0:512], in_=bank(PS, bnk),
                                                         func=AF.Copy), "act", 1)
            act_kk = bld.cnt["act"]
            wt("tensor", "act", act_kk)
            st("tensor", lambda e: e.matmul(bank(PS, 6), CONf[:, FO["r2T"]:FO["r2T"] + 128],
                                            SC2[:, 0:512], start=True, stop=True), "pe", 1)
            pe_rot2 = bld.cnt["pe"]
            wt("vector", "pe", pe_rot2)
            wt("vector", "da", bld.cnt["da"])
            if g == 0:
                wt("vector", "dt", dx_tab)
            st("vector", lambda e: e.tensor_mul(out=SC2[:, 512:1024], in0=SC2[:, 0:512],
                                                in1=TAB[:, 0:512]), "dve", 1)
            wt("vector", "dve", bld.cnt["dve"])
            st("vector", lambda e: e.tensor_mul(out=SC2[:, 0:512], in0=bank(PS, 6),
                                                in1=TAB[:, 512:1024]), "dve", 1)
            wt("vector", "dve", bld.cnt["dve"])
            st("vector", lambda e: e.tensor_add(out=SCb[:, 1024:1536], in0=SC2[:, 0:512],
                                                in1=SC2[:, 512:1024]), "dve", 1)
            dve_kk = bld.cnt["dve"]
            wt("sync", "dve", dve_kk)
            st("sync", lambda e, g=g, t=t: e.dma_start(
                out=ag["ag2k_i"].ap()[128 * g:128 * g + 128, QT * t:QT * (t + 1)],
                in_=SCb[:, 1024:1536]), "da", 16)
            vb = 4 + (g + 1) % 2
            for kc in range(2):
                st("tensor", lambda e, g=g, kc=kc, vb=vb: e.matmul(
                    bank(PS, vb),
                    U1[:, WOFF["wvu"] + 128 * (2 * g + kc):WOFF["wvu"] + 128 * (2 * g + kc) + 128],
                    SCb[:, 512 * kc:512 * kc + 512], start=(kc == 0), stop=(kc == 1)), "pe", 1)
            pe_v2 = bld.cnt["pe"]
            wt("scalar", "pe", pe_v2)
            st("scalar", lambda e, vb=vb: e.activation(out=SCb[:, 1536:2048],
                                                       in_=bank(PS, vb), func=AF.Copy), "act", 1)
            act_v2 = bld.cnt["act"]
            for s in range(4):
                wt("tensor", "act", act_v2)
                st("tensor", lambda e, s=s: e.transpose(
                    PSB[:, 128 * s:128 * s + 128], SCb[:, 1536 + 128 * s:1536 + 128 * s + 128],
                    CONb[:, CO["ident"]:CO["ident"] + 128]), "pe", 1)
            pe_vt2 = bld.cnt["pe"]
            wt("vector", "pe", pe_vt2)
            wt("vector", "da", bld.cnt["da"])
            st("vector", lambda e: e.tensor_copy(out=SCb[:, 2048:2560], in_=PSB[:, 0:512]),
               "dve", 1)
            dve_vt2 = bld.cnt["dve"]
            wt("sync", "dve", dve_vt2)
            st("sync", lambda e, g=g, t=t: e.dma_start(
                out=ag["ag2v_i"].ap()[QT * t:QT * t + 512, 128 * g:128 * g + 128]
                .rearrange("(s p) f -> p s f", p=128),
                in_=SCb[:, 2048:2560].rearrange("p (s f) -> p s f", s=4)), "da", 16)
    da_C = bld.cnt["da"]
    wt("gpsimd", "da", da_C)
    st("gpsimd", lambda e: e.collective_compute(
        "AllGather", ALU.bypass, replica_groups=ag["RG"],
        ins=[ag["ag2k_i"].ap()], outs=[ag["ag2k_o"].ap()]), "cc", 1)
    wt("gpsimd", "cc", bld.cnt["cc"])
    st("gpsimd", lambda e: e.collective_compute(
        "AllGather", ALU.bypass, replica_groups=ag["RG"],
        ins=[ag["ag2v_i"].ap()], outs=[ag["ag2v_o"].ap()]), "cc", 1)
    cc2 = bld.cnt["cc"]

    wt("gpsimd", "cc", cc2)
    wt("sync", "cc", cc2)
    last_pe_kst = {0: bld.cnt["pe"], 1: bld.cnt["pe"]}
    for g in range(8):
        for rr in range(4):
            pp = rr % 2
            wt("sync", "pe", last_pe_kst[pp])
            st("sync", lambda e, g=g, rr=rr, pp=pp: e.dma_start(
                out=U1[:, KST0 + 2048 * pp:KST0 + 2048 * pp + 2048],
                in_=ag["ag2k_o"].ap()[1024 * rr + 128 * g:1024 * rr + 128 * g + 128, :]),
                "dk%d" % pp, 16)
            dk_n = bld.cnt["dk%d" % pp]
            wt("scalar", "dk%d" % pp, dk_n)
            for q4 in range(4):
                wt("scalar", "pe", bld.cnt["pe"])
                st("scalar", lambda e, pp=pp, q4=q4: e.activation(
                    out=SCb[:, 512 * (q4 % 2):512 * (q4 % 2) + 512],
                    in_=U1[:, KST0 + 2048 * pp + 512 * q4:KST0 + 2048 * pp + 512 * q4 + 512],
                    func=AF.Square), "act", 1)
                a_n = bld.cnt["act"]
                wt("tensor", "act", a_n)
                wt("tensor", "dve", bld.cnt["dve"])
                st("tensor", lambda e, q4=q4: e.matmul(
                    PS[0:2, 512 * 6:512 * 6 + 512], CONb[:, CO["ones2"]:CO["ones2"] + 2],
                    SCb[:, 512 * (q4 % 2):512 * (q4 % 2) + 512], start=True, stop=True),
                    "pe", 1)
                p_n = bld.cnt["pe"]
                wt("vector", "pe", p_n)
                st("vector", lambda e, g=g, rr=rr, q4=q4: e.reduce_max(
                    ST[0:2, 128 + 16 * g + 4 * rr + q4:129 + 16 * g + 4 * rr + q4],
                    PS[0:2, 512 * 6:512 * 6 + 512], axis=AX.X), "dve", 1)
            last_pe_kst[pp] = bld.cnt["pe"]
    wt("vector", "dve", bld.cnt["dve"])
    for g in range(8):
        st("vector", lambda e, g=g: e.reduce_max(
            ST[0:2, 256 + g:257 + g], ST[0:2, 128 + 16 * g:128 + 16 * g + 16], axis=AX.X),
            "dve", 1)
    dve_kn2 = bld.cnt["dve"]
    wt("scalar", "dve", dve_kn2)
    st("scalar", lambda e: e.activation(out=ST[0:2, 264:272], in_=ST[0:2, 256:264],
                                        func=AF.Sqrt), "act", 1)
    act_kn2 = bld.cnt["act"]
    pe_gu_prev = bld.cnt["pe"]
    pe_dn_prev = bld.cnt["pe"]

    for t in range(NT):
        wt("sync", "act", act_kn2)
        wt("sync", "dve", bld.cnt["dve"])
        st("sync", lambda e, t=t: e.dma_start(out=TAB[:, 0:512],
                                              in_=I["cos2"][:, QT * t:QT * t + 512]), "dt", 16)
        st("sync", lambda e, t=t: e.dma_start(out=TAB[:, 512:1024],
                                              in_=I["sin2"][:, QT * t:QT * t + 512]), "dt", 16)
        dx_tab = bld.cnt["dt"]
        for mi in range(4):
            st("sync", lambda e, mi=mi: e.dma_start(
                out=U1[:, MSK0 + 2048 * mi:MSK0 + 2048 * mi + 2048]
                .rearrange("p (a f) -> p a f", a=4),
                in_=I["mask"][512 * mi:512 * mi + 512, :]
                .rearrange("(a p) f -> p a f", p=128)), "dm", 16)
        dk_msk = bld.cnt["dm"]
        for g in range(8):
            bnk = g % 2
            wt("tensor", "act", bld.cnt["act"])
            wt("tensor", "dve", bld.cnt["dve"])
            if t == 0 and g == 0:
                wt("tensor", "dt", dx_tab)
            for kc in range(2):
                st("tensor", lambda e, g=g, kc=kc, t=t, bnk=bnk: e.matmul(
                    bank(PS, bnk),
                    WR[:, WQQ0 + 128 * (2 * g + kc):WQQ0 + 128 * (2 * g + kc) + 128],
                    QF[:, 512 * (2 * t + kc):512 * (2 * t + kc) + 512],
                    start=(kc == 0), stop=(kc == 1)), "pe", 1)
            pe_qq = bld.cnt["pe"]
            wt("scalar", "pe", pe_qq)
            st("scalar", lambda e, bnk=bnk: e.activation(out=SC2[:, 0:512], in_=bank(PS, bnk),
                                                         func=AF.Copy), "act", 1)
            act_q2 = bld.cnt["act"]
            wt("tensor", "act", act_q2)
            st("tensor", lambda e: e.matmul(bank(PS, 6), CONf[:, FO["r2T"]:FO["r2T"] + 128],
                                            SC2[:, 0:512], start=True, stop=True), "pe", 1)
            pe_rot = bld.cnt["pe"]
            wt("vector", "pe", pe_rot)
            if g == 0:
                wt("vector", "dt", dx_tab)
            st("vector", lambda e: e.tensor_mul(out=SC2[:, 512:1024], in0=SC2[:, 0:512],
                                                in1=TAB[:, 0:512]), "dve", 1)
            wt("vector", "dve", bld.cnt["dve"])
            st("vector", lambda e: e.tensor_mul(out=SC2[:, 0:512], in0=bank(PS, 6),
                                                in1=TAB[:, 512:1024]), "dve", 1)
            wt("vector", "dve", bld.cnt["dve"])
            st("vector", lambda e, g=g: e.tensor_add(
                out=QF[:, 4096 + 512 * g:4096 + 512 * g + 512],
                in0=SC2[:, 0:512], in1=SC2[:, 512:1024]), "dve", 1)
            dve_qq = bld.cnt["dve"]
            wt("scalar", "dve", dve_qq)
            st("scalar", lambda e, g=g: e.activation(
                out=SCb[:, 2048:2560], in_=QF[:, 4096 + 512 * g:4096 + 512 * g + 512],
                func=AF.Square), "act", 1)
            act_sq = bld.cnt["act"]
            wt("tensor", "act", act_sq)
            wt("tensor", "dve", bld.cnt["dve"])
            st("tensor", lambda e: e.matmul(PS[0:2, 512 * 6:512 * 6 + 512],
                                            CONb[:, CO["ones2"]:CO["ones2"] + 2],
                                            SCb[:, 2048:2560], start=True, stop=True), "pe", 1)
            pe_q2m = bld.cnt["pe"]
            wt("vector", "pe", pe_q2m)
            st("vector", lambda e: e.reduce_max(ST[0:2, 272:273],
                                                PS[0:2, 512 * 6:512 * 6 + 512], axis=AX.X),
               "dve", 1)
            wt("scalar", "dve", bld.cnt["dve"])
            st("scalar", lambda e: e.activation(out=ST[0:2, 272:273], in_=ST[0:2, 272:273],
                                                func=AF.Sqrt), "act", 1)
            act_qs = bld.cnt["act"]
            wt("vector", "act", act_qs)
            st("vector", lambda e, g=g: e.tensor_mul(out=ST[0:2, 273:274], in0=ST[0:2, 272:273],
                                                     in1=ST[0:2, 264 + g:265 + g]), "dve", 1)
            wt("vector", "dve", bld.cnt["dve"])
            st("vector", lambda e: e.tensor_scalar_mul(out=ST[0:2, 273:274],
                                                       in0=ST[0:2, 273:274],
                                                       scalar1=-float(SC2s)), "dve", 1)
            wt("vector", "dve", bld.cnt["dve"])
            st("vector", lambda e: e.tensor_copy(out=SCb[0:2, 8:9], in_=ST[0:2, 273:274]),
               "dve", 1)
            dve_bias2 = bld.cnt["dve"]
            wt("tensor", "dve", dve_bias2)
            st("tensor", lambda e: e.matmul(PS[:, 512 * 6:512 * 6 + 1],
                                            CONb[0:2, CO["esel2T"]:CO["esel2T"] + 128],
                                            SCb[0:2, 8:9], start=True, stop=True), "pe", 1)
            pe_b2 = bld.cnt["pe"]
            wt("vector", "pe", pe_b2)
            st("vector", lambda e: e.tensor_copy(out=ST[:, 12:13],
                                                 in_=PS[:, 512 * 6:512 * 6 + 1]), "dve", 1)
            dve_b2 = bld.cnt["dve"]
            first = True
            for m2 in range(t + 1):
                pp = m2 % 2
                wt("sync", "pe", last_pe_kst[pp])
                wt("sync", "dve", dve_b2) if m2 == 0 else None
                for rr in range(4):
                    st("sync", lambda e, rr=rr, g=g, pp=pp, m2=m2: e.dma_start(
                        out=U1[:, KST0 + 2048 * pp + 512 * rr:KST0 + 2048 * pp + 512 * rr + 512],
                        in_=ag["ag2k_o"].ap()[1024 * rr + 128 * g:1024 * rr + 128 * g + 128,
                                              QT * m2:QT * m2 + 512]), "dk%d" % pp, 16)
                    st("sync", lambda e, rr=rr, g=g, pp=pp, m2=m2: e.dma_start(
                        out=U1[:, VST0 + 2048 * pp + 512 * rr:VST0 + 2048 * pp + 512 * rr + 512]
                        .rearrange("p (s f) -> p s f", s=4),
                        in_=ag["ag2v_o"].ap()[TOK * rr + QT * m2:TOK * rr + QT * m2 + 512,
                                              128 * g:128 * g + 128]
                        .rearrange("(s p) f -> p s f", p=128)), "dk%d" % pp, 16)
                dk_kv = bld.cnt["dk%d" % pp]
                for rr in range(4):
                    for kb in range(4):
                        is_last = (m2 == t and rr == 3 and kb == 3)
                        wt("tensor", "dk%d" % pp, dk_kv)
                        wt("tensor", "act", bld.cnt["act"])
                        wt("tensor", "dve", dve_b2)
                        for hh in range(2):
                            st("tensor", lambda e, hh=hh, pp=pp, rr=rr, kb=kb, g=g: e.matmul(
                                bank(PS, hh),
                                U1[64 * hh:64 * hh + 64,
                                   KST0 + 2048 * pp + 512 * rr + 128 * kb:
                                   KST0 + 2048 * pp + 512 * rr + 128 * kb + 128],
                                QF[64 * hh:64 * hh + 64, 4096 + 512 * g:4096 + 512 * g + 512],
                                start=True, stop=True, skip_group_check=True,
                                tile_position=(64 * hh, 0)), "pe", 1)
                        pe_s = bld.cnt["pe"]
                        wt("scalar", "pe", pe_s)
                        st("scalar", lambda e: e.activation(
                            out=U1[:, PT0:PT0 + 1024], in_=PS[:, 0:1024], func=AF.Exp,
                            bias=ST[:, 12:13], scale=float(SC2s)), "act", 1)
                        act_e = bld.cnt["act"]
                        if m2 == t:
                            wt("vector", "act", act_e)
                            if rr == 0 and kb == 0:
                                wt("vector", "dm", dk_msk)
                            for hh in range(2):
                                st("vector", lambda e, hh=hh, rr=rr, kb=kb: e.tensor_mul(
                                    out=U1[:, PT0 + 512 * hh:PT0 + 512 * hh + 512],
                                    in0=U1[:, PT0 + 512 * hh:PT0 + 512 * hh + 512],
                                    in1=U1[:, MSK0 + 512 * (4 * rr + kb):
                                           MSK0 + 512 * (4 * rr + kb) + 512]), "dve", 1)
                            wt("tensor", "dve", bld.cnt["dve"])
                        else:
                            wt("tensor", "act", act_e)
                        for hh in range(2):
                            st("tensor", lambda e, hh=hh, pp=pp, rr=rr, kb=kb, first=first,
                               is_last=is_last: e.matmul(
                                PS[64 * hh:64 * hh + 64, 512 * 4:512 * 4 + 512],
                                U1[:, VST0 + 2048 * pp + 512 * rr + 128 * kb + 64 * hh:
                                   VST0 + 2048 * pp + 512 * rr + 128 * kb + 64 * hh + 64],
                                U1[:, PT0 + 512 * hh:PT0 + 512 * hh + 512],
                                start=first, stop=is_last, skip_group_check=True,
                                tile_position=(0, 64 * hh)), "pe", 1)
                        for hh in range(2):
                            st("tensor", lambda e, hh=hh, first=first, is_last=is_last: e.matmul(
                                PS[0:2, 512 * 5:512 * 5 + 512],
                                CONb[:, CO["oneh2"] + 2 * hh:CO["oneh2"] + 2 * hh + 2],
                                U1[:, PT0 + 512 * hh:PT0 + 512 * hh + 512],
                                start=(first and hh == 0), stop=(is_last and hh == 1),
                                skip_group_check=True), "pe", 1)
                        first = False
                last_pe_kst[pp] = bld.cnt["pe"]
            pe_attn = bld.cnt["pe"]
            wt("scalar", "pe", pe_attn)
            st("scalar", lambda e: e.activation(out=ST[0:2, 1024:1536],
                                                in_=PS[0:2, 512 * 5:512 * 5 + 512],
                                                func=AF.Ln), "act", 1)
            wt("scalar", "act", bld.cnt["act"])
            st("scalar", lambda e: e.activation(out=ST[0:2, 1024:1536], in_=ST[0:2, 1024:1536],
                                                func=AF.Exp, scale=-1.0), "act", 1)
            act_rc = bld.cnt["act"]
            wt("vector", "act", act_rc)
            st("vector", lambda e: e.tensor_copy(out=SCb[0:2, 1024:1536],
                                                 in_=ST[0:2, 1024:1536]), "dve", 1)
            dve_rcb = bld.cnt["dve"]
            wt("tensor", "dve", dve_rcb)
            st("tensor", lambda e: e.matmul(bank(PS, 6),
                                            CONb[0:2, CO["esel2T"]:CO["esel2T"] + 128],
                                            SCb[0:2, 1024:1536], start=True, stop=True), "pe", 1)
            pe_rb = bld.cnt["pe"]
            wt("vector", "pe", pe_rb)
            st("vector", lambda e: e.tensor_copy(out=SC2[:, 0:512], in_=bank(PS, 6)), "dve", 1)
            wt("vector", "dve", bld.cnt["dve"])
            st("vector", lambda e, g=g: e.tensor_mul(
                out=U1[:, YH0 + 512 * g:YH0 + 512 * g + 512], in0=bank(PS, 4),
                in1=SC2[:, 0:512]), "dve", 1)
        dve_y2 = bld.cnt["dve"]
        last_pe_wo = {0: bld.cnt["pe"], 1: bld.cnt["pe"]}
        for oc in range(8):
            dw_o = wload(WR[:, 1024 * (oc % 2):1024 * (oc % 2) + 1024],
                         I["wo"][:, 128 * oc:128 * oc + 128], 1024, 8,
                         wait_pe=last_pe_wo[oc % 2])
            wt("tensor", "pool", dw_o)
            wt("tensor", "dve", bld.cnt["dve"])
            for kc in range(8):
                st("tensor", lambda e, oc=oc, kc=kc: e.matmul(
                    bank(PS, oc % 2),
                    WR[:, 1024 * (oc % 2) + 128 * kc:1024 * (oc % 2) + 128 * kc + 128],
                    U1[:, YH0 + 512 * kc:YH0 + 512 * kc + 512],
                    start=(kc == 0), stop=(kc == 7)), "pe", 1)
            last_pe_wo[oc % 2] = bld.cnt["pe"]
            pe_wo = bld.cnt["pe"]
            wt("vector", "pe", pe_wo)
            st("vector", lambda e, oc=oc, t=t: e.tensor_add(
                out=SC2[:, 512 * (oc % 4):512 * (oc % 4) + 512], in0=bank(PS, oc % 2),
                in1=XR[:, TOK * oc + QT * t:TOK * oc + QT * t + 512]), "dve", 1)
            wt("vector", "dve", bld.cnt["dve"])
            st("vector", lambda e, oc=oc, t=t: e.tensor_add(
                out=XR[:, TOK * oc + QT * t:TOK * oc + QT * t + 512],
                in0=XR[:, TOK * oc + QT * t:TOK * oc + QT * t + 512],
                in1=SC2[:, 512 * (oc % 4):512 * (oc % 4) + 512]), "dve", 1)
            st("vector", lambda e, oc=oc: e.tensor_copy(
                out=XB[:, QT * oc:QT * oc + 512],
                in_=SC2[:, 512 * (oc % 4):512 * (oc % 4) + 512]), "dve", 1)
        emit_stats_norm(bld, M, t, BO["wln2"], "rmsxb")

        wt("gpsimd", "dve", bld.cnt["dve"])
        for kc in range(8):
            st("gpsimd", lambda e, kc=kc: e.dma_start(
                out=WR[:, WRT0 + 8 * kc:WRT0 + 8 * kc + ER],
                in_=I["wrout"][128 * kc:128 * kc + 128, :]), "dwr", 16)
        dw_r = bld.cnt["dwr"]
        wt("tensor", "dwr", dw_r)
        wt("tensor", "dve", bld.cnt["dve"])
        for kc in range(8):
            st("tensor", lambda e, kc=kc: e.matmul(
                PS[0:ER, 512 * 6:512 * 6 + 512], WR[:, WRT0 + 8 * kc:WRT0 + 8 * kc + ER],
                XB[:, QT * kc:QT * (kc + 1)], start=(kc == 0), stop=(kc == 7)), "pe", 1)
        pe_lg = bld.cnt["pe"]
        wt("scalar", "pe", pe_lg)
        st("scalar", lambda e: e.activation(out=SC2[0:ER, 0:512],
                                            in_=PS[0:ER, 512 * 6:512 * 6 + 512],
                                            func=AF.Sigmoid,
                                            bias=CONf[0:ER, BO["brout"]:BO["brout"] + 1]), "act", 1)
        act_pr = bld.cnt["act"]
        wt("tensor", "act", act_pr)
        for s in range(4):
            st("tensor", lambda e, s=s: e.transpose(
                PS[:, 512 * 5 + 128 * s:512 * 5 + 128 * s + 128],
                SC2[:, 128 * s:128 * s + 128],
                CONf[:, FO["identf"]:FO["identf"] + 128]), "pe", 1)
        pe_tp = bld.cnt["pe"]
        wt("vector", "pe", pe_tp)
        for s in range(4):
            st("vector", lambda e, s=s: e.tensor_copy(
                out=SC2[:, 1024 + 8 * s:1024 + 8 * s + ER],
                in_=PS[:, 512 * 5 + 128 * s:512 * 5 + 128 * s + ER]), "dve", 1)
        for s in range(4):
            pb = 1024 + 8 * s
            wt("vector", "dve", bld.cnt["dve"])
            st("vector", lambda e, pb=pb: e.reduce_max(ST[:, 0:1], SC2[:, pb:pb + ER],
                                                       axis=AX.X), "dve", 1)
            wt("vector", "dve", bld.cnt["dve"])
            st("vector", lambda e, pb=pb: e.tensor_scalar(
                out=ST[:, 8:8 + ER], in0=SC2[:, pb:pb + ER], scalar1=ST[:, 0:1],
                scalar2=None, op0=ALU.is_equal), "dve", 1)
            wt("vector", "dve", bld.cnt["dve"])
            st("vector", lambda e, pb=pb: e.scalar_tensor_tensor(
                out=ST[:, 8:8 + ER], in0=ST[:, 8:8 + ER], scalar=-1e30,
                in1=SC2[:, pb:pb + ER], op0=ALU.mult, op1=ALU.add), "dve", 1)
            wt("vector", "dve", bld.cnt["dve"])
            st("vector", lambda e: e.reduce_max(ST[:, 1:2], ST[:, 8:8 + ER], axis=AX.X),
               "dve", 1)
            wt("vector", "dve", bld.cnt["dve"])
            st("vector", lambda e, pb=pb: e.tensor_scalar(
                out=ST[:, 8:8 + ER], in0=SC2[:, pb:pb + ER], scalar1=ST[:, 1:2],
                scalar2=None, op0=ALU.is_ge), "dve", 1)
            wt("vector", "dve", bld.cnt["dve"])
            st("vector", lambda e, pb=pb: e.tensor_mul(
                out=ST[:, 8:8 + ER], in0=ST[:, 8:8 + ER], in1=SC2[:, pb:pb + ER]), "dve", 1)
            wt("vector", "dve", bld.cnt["dve"])
            st("vector", lambda e: e.tensor_add(out=ST[:, 0:1], in0=ST[:, 0:1],
                                                in1=ST[:, 1:2]), "dve", 1)
            dve_d = bld.cnt["dve"]
            wt("scalar", "dve", dve_d)
            st("scalar", lambda e: e.activation(out=ST[:, 1:2], in_=ST[:, 0:1], func=AF.Ln),
               "act", 1)
            wt("scalar", "act", bld.cnt["act"])
            st("scalar", lambda e: e.activation(out=ST[:, 1:2], in_=ST[:, 1:2], func=AF.Exp,
                                                scale=-1.0), "act", 1)
            act_rr2 = bld.cnt["act"]
            wt("vector", "act", act_rr2)
            wt("vector", "dve", bld.cnt["dve"])
            st("vector", lambda e, s=s: e.tensor_scalar(
                out=SC2[:, 1280 + 8 * s:1280 + 8 * s + ER], in0=ST[:, 8:8 + ER],
                scalar1=ST[:, 1:2], scalar2=None, op0=ALU.mult), "dve", 1)
        dve_wts = bld.cnt["dve"]
        wt("vector", "dve", dve_wts)
        for s in range(4):
            st("vector", lambda e, s=s: e.tensor_copy(
                out=SCb[:, 2048 + 8 * s:2048 + 8 * s + ER],
                in_=SC2[:, 1280 + 8 * s:1280 + 8 * s + ER]), "dve", 1)
        dve_wtb = bld.cnt["dve"]
        wt("tensor", "dve", dve_wtb)
        for s in range(4):
            st("tensor", lambda e, s=s: e.transpose(
                PSB[0:ER, 128 * s:128 * s + 128], SCb[:, 2048 + 8 * s:2048 + 8 * s + ER],
                CONb[:, CO["ident"]:CO["ident"] + 128]), "pe", 1)
        pe_wtt = bld.cnt["pe"]
        wt("vector", "pe", pe_wtt)
        st("vector", lambda e: e.tensor_copy(out=SCb[0:ER, 1024:1536], in_=PSB[0:ER, 0:512]),
           "dve", 1)
        dve_wfm = bld.cnt["dve"]

        for net in range(1 + ER):
            wn = ("wgs", "wus", "wds") if net == 0 else ("wgr", "wur", "wdr")
            e_idx = max(0, net - 1)
            if net > 0:
                wt("tensor", "dve", bld.cnt["dve"])
                st("tensor", lambda e, e_idx=e_idx: e.matmul(
                    bank(PS, 6), CONb[0:ER, CO["sele"] + 128 * e_idx:CO["sele"] + 128 * e_idx + 128],
                    SCb[0:ER, 1024:1536], start=True, stop=True), "pe", 1)
                pe_wb = bld.cnt["pe"]
                wt("vector", "pe", pe_wb)
                st("vector", lambda e: e.tensor_copy(out=SCb[:, 2560:3072], in_=bank(PS, 6)),
                   "dve", 1)
            for ic in range(8):
                g_src = (I[wn[0]][:, 128 * ic:128 * ic + 128] if net == 0
                         else I[wn[0]][e_idx, :, 128 * ic:128 * ic + 128])
                u_src = (I[wn[1]][:, 128 * ic:128 * ic + 128] if net == 0
                         else I[wn[1]][e_idx, :, 128 * ic:128 * ic + 128])
                wload(WR[:, 4096:5120], g_src, 1024, 8, wait_pe=pe_gu_prev)
                dw_gu = wload(WR[:, 5120:6144], u_src, 1024, 8, wait_pe=pe_gu_prev)
                wt("tensor", "pool", dw_gu)
                wt("tensor", "act", bld.cnt["act"])
                wt("tensor", "dve", bld.cnt["dve"])
                for kc in range(8):
                    st("tensor", lambda e, kc=kc: e.matmul(
                        bank(PS, 2), WR[:, 4096 + 128 * kc:4096 + 128 * kc + 128],
                        XB[:, QT * kc:QT * (kc + 1)], start=(kc == 0), stop=(kc == 7)), "pe", 1)
                for kc in range(8):
                    st("tensor", lambda e, kc=kc: e.matmul(
                        bank(PS, 3), WR[:, 5120 + 128 * kc:5120 + 128 * kc + 128],
                        XB[:, QT * kc:QT * (kc + 1)], start=(kc == 0), stop=(kc == 7)), "pe", 1)
                pe_gu = bld.cnt["pe"]
                pe_gu_prev = pe_gu
                wt("scalar", "pe", pe_gu)
                if ic >= 1:
                    wt("scalar", "dve", bld.cnt["dve"])
                st("scalar", lambda e: e.activation(out=SCb[:, 0:512], in_=bank(PS, 2),
                                                    func=AF.Silu), "act", 1)
                st("scalar", lambda e: e.activation(out=SCb[:, 512:1024], in_=bank(PS, 3),
                                                    func=AF.Copy), "act", 1)
                act_gu = bld.cnt["act"]
                wt("vector", "act", act_gu)
                st("vector", lambda e: e.tensor_mul(out=SCb[:, 0:512], in0=SCb[:, 0:512],
                                                    in1=SCb[:, 512:1024]), "dve", 1)
                wt("vector", "dve", bld.cnt["dve"])
                if net > 0:
                    st("vector", lambda e, ic=ic: e.tensor_mul(
                        out=U1[:, 512 * ic:512 * ic + 512], in0=SCb[:, 0:512],
                        in1=SCb[:, 2560:3072]), "dve", 1)
                else:
                    st("vector", lambda e, ic=ic: e.tensor_copy(
                        out=U1[:, 512 * ic:512 * ic + 512], in_=SCb[:, 0:512]), "dve", 1)
            dve_hm = bld.cnt["dve"]
            for oc in range(8):
                d_src = (I[wn[2]][:, 128 * oc:128 * oc + 128] if net == 0
                         else I[wn[2]][e_idx, :, 128 * oc:128 * oc + 128])
                dw_d = wload(WR[:, 6144:7168], d_src, 1024, 8, wait_pe=pe_dn_prev)
                wt("tensor", "pool", dw_d)
                wt("tensor", "dve", bld.cnt["dve"])
                for kc in range(8):
                    st("tensor", lambda e, oc=oc, kc=kc: e.matmul(
                        bank(PS, 4 + oc % 2), WR[:, 6144 + 128 * kc:6144 + 128 * kc + 128],
                        U1[:, 512 * kc:512 * kc + 512], start=(kc == 0), stop=(kc == 7)), "pe", 1)
                pe_dn = bld.cnt["pe"]
                pe_dn_prev = pe_dn
                wt("vector", "pe", pe_dn)
                if net == 0:
                    st("vector", lambda e, oc=oc: e.tensor_copy(
                        out=TAB[:, ACC0 + 512 * oc:ACC0 + 512 * oc + 512],
                        in_=bank(PS, 4 + oc % 2)), "dve", 1)
                else:
                    st("vector", lambda e, oc=oc: e.tensor_add(
                        out=TAB[:, ACC0 + 512 * oc:ACC0 + 512 * oc + 512],
                        in0=TAB[:, ACC0 + 512 * oc:ACC0 + 512 * oc + 512],
                        in1=bank(PS, 4 + oc % 2)), "dve", 1)
        last_pe_kst = {0: bld.cnt["pe"], 1: bld.cnt["pe"]}
        wt("vector", "dve", bld.cnt["dve"])
        for oc in range(8):
            st("vector", lambda e, oc=oc, t=t: e.tensor_add(
                out=XR[:, TOK * oc + QT * t:TOK * oc + QT * t + 512],
                in0=XR[:, TOK * oc + QT * t:TOK * oc + QT * t + 512],
                in1=TAB[:, ACC0 + 512 * oc:ACC0 + 512 * oc + 512]), "dve", 1)
        emit_stats_norm(bld, M, t, None, "ln")
        dve_fin = bld.cnt["dve"]
        wt("sync", "dve", dve_fin)
        for oc in range(8):
            st("sync", lambda e, oc=oc, t=t: e.dma_start(
                out=y_out[128 * oc:128 * oc + 128, QT * t:QT * t + 512],
                in_=XR[:, TOK * oc + QT * t:TOK * oc + QT * t + 512]), "do", 16)
    wt("sync", "do", bld.cnt["do"])


def _host_inputs(kw):
    f32 = np.float32
    x = np.ascontiguousarray(np.asarray(kw["L0_attn_x"], f32))

    t = np.arange(T, dtype=np.float64)
    inv1 = 1.0 / 10000.0 ** (np.arange(0, L1, 2, dtype=np.float64) / L1)
    fr1 = np.outer(t, inv1)
    cos1 = np.repeat(np.cos(fr1), 2, axis=-1)[:, :L1].astype(f32)
    sin1 = np.repeat(np.sin(fr1), 2, axis=-1)[:, :L1].astype(f32)
    inv2 = 1.0 / 10000.0 ** (np.arange(0, HH2, 2, dtype=np.float64) / HH2)
    fr2 = np.outer(t, inv2)
    emb2 = np.concatenate([fr2, fr2], axis=-1)
    cos2 = np.cos(emb2).astype(f32)
    sin2 = np.sin(emb2).astype(f32)

    R1 = np.zeros((L1, L1), f32)
    for i in range(L1 // 2):
        R1[2 * i, 2 * i + 1] = -1.0
        R1[2 * i + 1, 2 * i] = 1.0
    R1T = np.kron(np.eye(4, dtype=f32), R1.T)
    R2 = np.zeros((HH2, HH2), f32)
    for i in range(HH2 // 2):
        R2[i, HH2 // 2 + i] = -1.0
        R2[HH2 // 2 + i, i] = 1.0
    R2T = np.zeros((128, 128), f32)
    R2T[32:64, 32:64] = R2.T
    R2T[96:128, 96:128] = R2.T

    sh = {}
    sh["wqkv"] = np.ascontiguousarray(np.asarray(kw["L0_attn_c_attn_w"], f32).T)
    sh["bqkv"] = np.ascontiguousarray(np.asarray(kw["L0_attn_c_attn_b"], f32).reshape(24, 128).T)
    sh["q2lT"] = np.ascontiguousarray(np.tile(np.asarray(kw["L0_attn_q2l_w"], f32).T, (2, 1)))
    sh["kv2lT"] = np.ascontiguousarray(np.tile(np.asarray(kw["L0_attn_kv2l_w"], f32).T, (2, 1)))
    sh["bq2l"] = np.ascontiguousarray(np.tile(np.asarray(kw["L0_attn_q2l_b"], f32), 4).reshape(128, 1))
    sh["bkv2l"] = np.ascontiguousarray(np.tile(np.asarray(kw["L0_attn_kv2l_b"], f32), 4).reshape(128, 1))
    sh["wqkT"] = np.ascontiguousarray(np.tile(np.asarray(kw["L0_attn_wqk_w"], f32).T, (4, 1)))
    sh["bwqk"] = np.ascontiguousarray(np.tile(np.asarray(kw["L0_attn_wqk_b"], f32), 4).reshape(128, 1))
    sh["l2oT"] = np.ascontiguousarray(np.tile(np.asarray(kw["L0_attn_l2o_w"], f32).T, (4, 1)))
    sh["bl2o"] = np.ascontiguousarray(np.tile(np.asarray(kw["L0_attn_l2o_b"], f32), 2).reshape(128, 1))
    sh["cprojT"] = np.ascontiguousarray(np.asarray(kw["L0_attn_cproj_w"], f32).T)
    sh["bcproj"] = np.ascontiguousarray(np.asarray(kw["L0_attn_cproj_b"], f32).reshape(8, 128).T)
    sh["wln1"] = np.ascontiguousarray(np.asarray(kw["L0_ffn_w_ln1"], f32).reshape(8, 128).T)
    sh["wln2"] = np.ascontiguousarray(np.asarray(kw["L0_ffn_w_ln2"], f32).reshape(8, 128).T)
    sh["wkvd"] = np.ascontiguousarray(np.asarray(kw["L0_ffn_W_kvd"], f32))
    sh["wqd"] = np.ascontiguousarray(np.asarray(kw["L0_ffn_W_qd"], f32))
    sh["wku"] = np.ascontiguousarray(np.asarray(kw["L0_ffn_W_ku"], f32))
    wqu = np.asarray(kw["L0_ffn_W_qu"], f32)
    wrq = np.asarray(kw["L0_ffn_W_ropeq"], f32)
    wqq = np.zeros((256, 1024), f32)
    for h in range(16):
        wqq[:, 64 * h:64 * h + 32] = wqu[:, 32 * h:32 * h + 32]
        wqq[:, 64 * h + 32:64 * h + 64] = wrq[:, 32 * h:32 * h + 32]
    sh["wqq"] = np.ascontiguousarray(wqq)
    sh["wropek"] = np.ascontiguousarray(np.asarray(kw["L0_ffn_W_ropek"], f32))
    sh["wvu"] = np.ascontiguousarray(np.asarray(kw["L0_ffn_W_vu"], f32))
    sh["wo"] = np.ascontiguousarray(np.asarray(kw["L0_ffn_W_o"], f32))
    sh["wgs"] = np.ascontiguousarray(np.asarray(kw["L0_ffn_Wg_s"], f32))
    sh["wus"] = np.ascontiguousarray(np.asarray(kw["L0_ffn_Wu_s"], f32))
    sh["wds"] = np.ascontiguousarray(np.asarray(kw["L0_ffn_Wd_s"], f32))
    sh["wgr"] = np.ascontiguousarray(np.asarray(kw["L0_ffn_Wg_r"], f32))
    sh["wur"] = np.ascontiguousarray(np.asarray(kw["L0_ffn_Wu_r"], f32))
    sh["wdr"] = np.ascontiguousarray(np.asarray(kw["L0_ffn_Wd_r"], f32))
    sh["wrout"] = np.ascontiguousarray(np.asarray(kw["L0_ffn_W_router"], f32))
    sh["brout"] = np.ascontiguousarray(np.asarray(kw["L0_ffn_routing_bias"], f32).reshape(ER, 1))
    sh["identb"] = np.eye(128).astype(bfdt)
    sh["identf"] = np.eye(128, dtype=f32)
    sh["r1T"] = R1T
    sh["r2T"] = R2T
    o1 = np.zeros((128, 4), f32); e1 = np.zeros((4, 128), f32)
    for h in range(4):
        o1[32 * h:32 * h + 32, h] = 1.0
        e1[h, 32 * h:32 * h + 32] = 1.0
    sh["ones1"] = o1.astype(bfdt); sh["esel1T"] = e1.astype(bfdt)
    o2 = np.zeros((128, 2), f32); e2 = np.zeros((2, 128), f32)
    for h in range(2):
        o2[64 * h:64 * h + 64, h] = 1.0
        e2[h, 64 * h:64 * h + 64] = 1.0
    sh["ones2"] = o2.astype(bfdt); sh["esel2T"] = e2.astype(bfdt)
    sh["onesc"] = np.ones((128, 1), f32).astype(bfdt)
    sh["onesr"] = np.ones((1, 128), f32)
    sh["onesrb"] = np.ones((1, 128), f32).astype(bfdt)
    oh1 = np.zeros((128, 16), f32)
    for h in range(4):
        oh1[:, 4 * h + h] = 1.0
    sh["oneh1"] = oh1.astype(bfdt)
    oh2 = np.zeros((128, 4), f32)
    for h in range(2):
        oh2[:, 2 * h + h] = 1.0
    sh["oneh2"] = oh2.astype(bfdt)
    sele = np.zeros((ER, ER * 128), f32)
    for e in range(ER):
        sele[e, 128 * e:128 * e + 128] = 1.0
    sh["sele"] = sele.astype(bfdt)

    in_maps = []
    row_list = []
    for c in range(8):
        b, rr = c // 4, c % 4
        rows = np.concatenate([np.arange(QT * (4 * j + rr), QT * (4 * j + rr + 1))
                               for j in range(NT)])
        row_list.append((b, rows))
        m = dict(sh)
        m["x"] = np.ascontiguousarray(x[b][rows].T)
        m["cos1"] = np.ascontiguousarray(np.tile(cos1[rows].T, (4, 1)))
        m["sin1"] = np.ascontiguousarray(np.tile(sin1[rows].T, (4, 1)))
        c2t = np.ones((128, TOK), f32); s2t = np.zeros((128, TOK), f32)
        c2t[32:64] = cos2[rows].T; s2t[32:64] = sin2[rows].T
        c2t[96:128] = cos2[rows].T; s2t[96:128] = sin2[rows].T
        m["cos2"] = np.ascontiguousarray(c2t)
        m["sin2"] = np.ascontiguousarray(s2t)
        msk = np.zeros((16, 128, QT), f32)
        for rp in range(4):
            for s in range(4):
                kpos = 512 * rp + 128 * s + np.arange(128)[:, None]
                qpos = 512 * rr + np.arange(QT)[None, :]
                msk[4 * rp + s] = (kpos <= qpos).astype(f32)
        m["mask"] = np.ascontiguousarray(msk.reshape(16 * 128, QT).astype(bfdt))
        in_maps.append(m)
    return in_maps, row_list


_NC_CACHE = None


def kernel(**kw):
    global _NC_CACHE
    in_maps, row_list = _host_inputs(kw)
    if _NC_CACHE is None:
        _NC_CACHE = build_nc()
    nc = _NC_CACHE
    res = run_bass_kernel_spmd(nc, in_maps, core_ids=list(range(8)))
    out = np.zeros((B, T, C), np.float32)
    for c in range(8):
        b, rows = row_list[c]
        out[b][rows] = res.results[c]["y"].T
    return out


if __name__ == "__main__":
    nc = build_nc()
    n = sum(len(bb.instructions) for bb in nc.m.basic_blocks) if hasattr(nc, "m") else -1
    print("built ok")


# version 04 (final)
# speedup vs baseline: 1.2524x; 1.1180x over previous
import numpy as np
import ml_dtypes
import contextlib
import concourse.bass as bass
import concourse.mybir as mybir
from concourse.bass_utils import run_bass_kernel_spmd

F32 = mybir.dt.float32
BF16 = mybir.dt.bfloat16
AF = mybir.ActivationFunctionType
ALU = mybir.AluOpType
AX = mybir.AxisListType
bfdt = ml_dtypes.bfloat16

B, T, C = 2, 8192, 1024
QT = 512
NT = 4
TOK = QT * NT
NH1, HD1, L1 = 16, 64, 32
NH2, HD2, HH2, L2 = 16, 64, 32, 256
ER = 7
SC1 = 1.0 / np.sqrt(np.float32(L1))
SC2s = 1.0 / np.sqrt(np.float32(HD2))

ENGINES = ("sync", "tensor", "scalar", "vector", "gpsimd")

KST0, VST0, PT0, MSK0, YH0 = 0, 4096, 8192, 10240, 18432
WOFF = dict(wkvd=0, wqd=2048, wku=4096, wropek=5120, wvu=9216)
WQQ0 = 2048
WRT0 = 7296

CO = dict(ident=0, ones1=128, esel1T=132, ones2=260, esel2T=262, onesc=390,
          l2oT=391, q2lT=455, kv2lT=487, oneh1=519, oneh2=535, onesrb=539,
          sele=668)
FO = dict(r1T=0, r2T=128, wqkT=256, identf=352, onesr=480, l2oT=608, q2lT=672,
          kv2lT=704)
BO = dict(bqkv=288, bq2l=312, bkv2l=313, bwqk=314, bl2o=315, bcproj=316,
          wln1=324, wln2=332, brout=340)
ACC0 = 1024


class Bld:
    def __init__(self):
        self.steps = {e: [] for e in ENGINES}
        self.cnt = {}
        self.sems = {}

    def emit(self, engn, fn):
        self.steps[engn].append(fn)

    def wait(self, engn, sem, val=None):
        v = self.cnt[sem] if val is None else val
        assert v <= self.cnt[sem], f"wait on future {sem} {v}>{self.cnt[sem]}"
        if v <= 0:
            return
        h = self.sems[sem]
        self.emit(engn, lambda eng, h=h, v=v: eng.wait_ge(h, v))

    def step(self, engn, fn, sem=None, amt=1):
        if sem is None:
            self.emit(engn, fn)
            return None
        h = self.sems[sem]
        self.cnt[sem] += amt
        self.emit(engn, lambda eng, fn=fn, h=h, amt=amt: fn(eng).then_inc(h, amt))
        return self.cnt[sem]


def bank(PS, i, n=512):
    return PS[:, 512 * i:512 * i + n]


def build_nc():
    nc = bass.Bass("TRN2", target_bir_lowering=False, debug=False)
    I = {}

    def inp(name, shape, dt=F32):
        I[name] = nc.dram_tensor(name, list(shape), dt, kind="ExternalInput").ap()

    inp("x", (C, TOK))
    inp("cos1", (128, TOK)); inp("sin1", (128, TOK))
    inp("cos2", (128, TOK)); inp("sin2", (128, TOK))
    inp("mask", (16 * 128, QT), BF16)
    inp("wqkv", (C, 3 * C)); inp("bqkv", (128, 24))
    inp("q2lT", (128, 32)); inp("kv2lT", (128, 32))
    inp("bq2l", (128, 1)); inp("bkv2l", (128, 1))
    inp("wqkT", (128, 32)); inp("bwqk", (128, 1))
    inp("l2oT", (128, 64)); inp("bl2o", (128, 1))
    inp("cprojT", (C, C)); inp("bcproj", (128, 8))
    inp("wln1", (128, 8)); inp("wln2", (128, 8))
    inp("wkvd", (C, 256)); inp("wqd", (C, 256))
    inp("wku", (256, 512)); inp("wqq", (256, 1024))
    inp("wropek", (C, 512)); inp("wvu", (256, 1024))
    inp("wo", (C, C))
    inp("wgs", (C, C)); inp("wus", (C, C)); inp("wds", (C, C))
    inp("wgr", (ER, C, C)); inp("wur", (ER, C, C)); inp("wdr", (ER, C, C))
    inp("wrout", (C, ER)); inp("brout", (ER, 1))
    inp("identb", (128, 128), BF16)
    inp("identf", (128, 128))
    inp("r1T", (128, 128)); inp("r2T", (128, 128))
    inp("ones1", (128, 4), BF16); inp("esel1T", (4, 128), BF16)
    inp("ones2", (128, 2), BF16); inp("esel2T", (2, 128), BF16)
    inp("onesc", (128, 1), BF16); inp("onesr", (1, 128))
    inp("oneh1", (128, 16), BF16); inp("oneh2", (128, 4), BF16)
    inp("onesrb", (1, 128), BF16); inp("sele", (ER, ER * 128), BF16)

    y_out = nc.dram_tensor("y", [C, TOK], F32, kind="ExternalOutput").ap()

    ag = {}
    ag["ag1k_i"] = nc.dram_tensor("ag1k_i", [512, TOK], BF16)
    ag["ag1k_o"] = nc.dram_tensor("ag1k_o", [4 * 512, TOK], BF16)
    ag["ag1v_i"] = nc.dram_tensor("ag1v_i", [TOK, 512], BF16)
    ag["ag1v_o"] = nc.dram_tensor("ag1v_o", [4 * TOK, 512], BF16)
    ag["ag2k_i"] = nc.dram_tensor("ag2k_i", [1024, TOK], BF16)
    ag["ag2k_o"] = nc.dram_tensor("ag2k_o", [4 * 1024, TOK], BF16)
    ag["ag2v_i"] = nc.dram_tensor("ag2v_i", [TOK, 1024], BF16)
    ag["ag2v_o"] = nc.dram_tensor("ag2v_o", [4 * TOK, 1024], BF16)
    ag["RG"] = [[0, 1, 2, 3], [4, 5, 6, 7]]

    bld = Bld()
    with contextlib.ExitStack() as ctx:
        E = ctx.enter_context
        U1 = E(nc.sbuf_tensor("U1", [128, 24576], BF16))
        XB = E(nc.sbuf_tensor("XB", [128, 8 * QT], BF16))
        QF = E(nc.sbuf_tensor("QF", [128, 8192], BF16))
        WR = E(nc.sbuf_tensor("WR", [128, 8192], BF16))
        SCb = E(nc.sbuf_tensor("SCb", [128, 3072], BF16))
        CONb = E(nc.sbuf_tensor("CONb", [128, 1664], BF16))
        XR = E(nc.sbuf_tensor("XR", [128, 8 * TOK], F32))
        TAB = E(nc.sbuf_tensor("TAB", [128, 5120], F32))
        ST = E(nc.sbuf_tensor("ST", [128, 1536], F32))
        SC2 = E(nc.sbuf_tensor("SC2", [128, 2048], F32))
        CONf = E(nc.sbuf_tensor("CONf", [128, 1024], F32))
        STG = E(nc.sbuf_tensor("STG", [128, 2048], F32))
        PS = E(nc.psum_tensor("PS", [128, 3584], F32))
        PSB = E(nc.psum_tensor("PSB", [128, 1024], BF16))

        for s in ("dx", "dt", "dm", "dk0", "dk1", "da", "do", "cc", "dwq",
                  "dwa0", "dwa1", "dwg", "dwd", "dwr", "dws0", "dws1",
                  "pe", "act", "dve", "pool"):
            bld.sems[s] = E(nc.semaphore(s))
            bld.cnt[s] = 0

        M = dict(U1=U1, XB=XB, QF=QF, WR=WR, SCb=SCb, CONb=CONb, XR=XR, TAB=TAB,
                 ST=ST, SC2=SC2, CONf=CONf, STG=STG, PS=PS, PSB=PSB)
        build_program(nc, bld, I, y_out, M, ag)

        block = E(nc.Block())

        @block.sync
        def _(eng):
            for fn in bld.steps["sync"]:
                fn(eng)

        @block.tensor
        def _(eng):
            for fn in bld.steps["tensor"]:
                fn(eng)

        @block.scalar
        def _(eng):
            for fn in bld.steps["scalar"]:
                fn(eng)

        @block.vector
        def _(eng):
            for fn in bld.steps["vector"]:
                fn(eng)

        @block.gpsimd
        def _(eng):
            for fn in bld.steps["gpsimd"]:
                fn(eng)

    return nc


def emit_stats_norm(bld, M, t, wcol, mode):
    XR, XB, SCb, ST, SC2, CONb, CONf, PS = (M["XR"], M["XB"], M["SCb"], M["ST"],
                                            M["SC2"], M["CONb"], M["CONf"], M["PS"])
    st, wt = bld.step, bld.wait
    if mode != "rmsxb":
        wt("gpsimd", "dve", bld.cnt["dve"])
        wt("gpsimd", "pe", bld.cnt["pe"])
        for oc in range(8):
            st("gpsimd", lambda e, oc=oc, t=t: e.tensor_copy(
                out=XB[:, 512 * oc:512 * oc + 512],
                in_=XR[:, TOK * oc + QT * t:TOK * oc + QT * t + 512]), "pool", 1)
        wt("tensor", "pool", bld.cnt["pool"])
        wt("scalar", "pool", bld.cnt["pool"])
    else:
        wt("tensor", "dve", bld.cnt["dve"])
        wt("scalar", "dve", bld.cnt["dve"])
    wt("scalar", "pe", bld.cnt["pe"])
    if mode == "ln":
        for oc in range(8):
            st("tensor", lambda e, oc=oc: e.matmul(
                PS[0:1, 512 * 6:512 * 6 + 512], CONb[:, CO["onesc"]:CO["onesc"] + 1],
                XB[:, 512 * oc:512 * oc + 512], start=(oc == 0), stop=(oc == 7)), "pe", 1)
    for oc in range(8):
        if oc >= 2:
            wt("scalar", "pe", bld.cnt["pe"])
        st("scalar", lambda e, oc=oc: e.activation(
            out=SCb[:, 512 * (oc % 2):512 * (oc % 2) + 512],
            in_=XB[:, 512 * oc:512 * oc + 512], func=AF.Square), "act", 1)
        a_sq = bld.cnt["act"]
        wt("tensor", "act", a_sq)
        st("tensor", lambda e, oc=oc: e.matmul(
            PS[0:1, 512 * 5:512 * 5 + 512], CONb[:, CO["onesc"]:CO["onesc"] + 1],
            SCb[:, 512 * (oc % 2):512 * (oc % 2) + 512], start=(oc == 0), stop=(oc == 7)),
            "pe", 1)
    pe_ssq = bld.cnt["pe"]
    wt("scalar", "pe", pe_ssq)
    if mode == "ln":
        st("scalar", lambda e: e.activation(out=ST[0:1, 0:512],
                                            in_=PS[0:1, 512 * 6:512 * 6 + 512],
                                            func=AF.Copy, scale=1.0 / 1024.0), "act", 1)
        st("scalar", lambda e: e.activation(out=ST[0:1, 512:1024],
                                            in_=PS[0:1, 512 * 5:512 * 5 + 512],
                                            func=AF.Copy, scale=1.0 / 1024.0), "act", 1)
        act_ms = bld.cnt["act"]
        wt("vector", "act", act_ms)
        st("vector", lambda e: e.tensor_mul(out=ST[0:1, 1024:1536], in0=ST[0:1, 0:512],
                                            in1=ST[0:1, 0:512]), "dve", 1)
        wt("vector", "dve", bld.cnt["dve"])
        st("vector", lambda e: e.tensor_sub(out=ST[0:1, 512:1024], in0=ST[0:1, 512:1024],
                                            in1=ST[0:1, 1024:1536]), "dve", 1)
        wt("vector", "dve", bld.cnt["dve"])
        st("vector", lambda e: e.tensor_scalar_add(out=ST[0:1, 512:1024],
                                                   in0=ST[0:1, 512:1024], scalar1=1e-5),
           "dve", 1)
        dve_v = bld.cnt["dve"]
        wt("scalar", "dve", dve_v)
        st("scalar", lambda e: e.activation(out=ST[0:1, 512:1024], in_=ST[0:1, 512:1024],
                                            func=AF.Ln), "act", 1)
        wt("scalar", "act", bld.cnt["act"])
        st("scalar", lambda e: e.activation(out=ST[0:1, 512:1024], in_=ST[0:1, 512:1024],
                                            func=AF.Exp, scale=-0.5), "act", 1)
        act_rs = bld.cnt["act"]
        wt("vector", "act", act_rs)
        st("vector", lambda e: e.tensor_mul(out=ST[0:1, 0:512], in0=ST[0:1, 0:512],
                                            in1=ST[0:1, 512:1024]), "dve", 1)
        dve_mr = bld.cnt["dve"]
        wt("tensor", "dve", dve_mr)
        st("tensor", lambda e: e.matmul(PS[:, 512 * 6:512 * 6 + 512],
                                        CONf[0:1, FO["onesr"]:FO["onesr"] + 128],
                                        ST[0:1, 512:1024], start=True, stop=True), "pe", 1)
        st("tensor", lambda e: e.matmul(PS[:, 512 * 5:512 * 5 + 512],
                                        CONf[0:1, FO["onesr"]:FO["onesr"] + 128],
                                        ST[0:1, 0:512], start=True, stop=True), "pe", 1)
        pe_bc = bld.cnt["pe"]
        wt("vector", "pe", pe_bc)
        st("vector", lambda e: e.tensor_copy(out=SC2[:, 1024:1536],
                                             in_=PS[:, 512 * 6:512 * 6 + 512]), "dve", 1)
        st("vector", lambda e: e.tensor_copy(out=SC2[:, 1536:2048],
                                             in_=PS[:, 512 * 5:512 * 5 + 512]), "dve", 1)
        wt("vector", "dve", bld.cnt["dve"])
        for oc in range(8):
            st("vector", lambda e, oc=oc, t=t: e.tensor_mul(
                out=XR[:, TOK * oc + QT * t:TOK * oc + QT * t + 512],
                in0=XR[:, TOK * oc + QT * t:TOK * oc + QT * t + 512],
                in1=SC2[:, 1024:1536]), "dve", 1)
            wt("vector", "dve", bld.cnt["dve"])
            st("vector", lambda e, oc=oc, t=t: e.tensor_sub(
                out=XR[:, TOK * oc + QT * t:TOK * oc + QT * t + 512],
                in0=XR[:, TOK * oc + QT * t:TOK * oc + QT * t + 512],
                in1=SC2[:, 1536:2048]), "dve", 1)
    else:
        st("scalar", lambda e: e.activation(out=ST[0:1, 512:1024],
                                            in_=PS[0:1, 512 * 5:512 * 5 + 512],
                                            func=AF.Sqrt, scale=1.0 / 1024.0), "act", 1)
        act_rt = bld.cnt["act"]
        wt("vector", "act", act_rt)
        st("vector", lambda e: e.tensor_scalar_add(out=ST[0:1, 512:1024],
                                                   in0=ST[0:1, 512:1024], scalar1=1e-5),
           "dve", 1)
        dve_e = bld.cnt["dve"]
        wt("scalar", "dve", dve_e)
        st("scalar", lambda e: e.activation(out=ST[0:1, 512:1024], in_=ST[0:1, 512:1024],
                                            func=AF.Ln), "act", 1)
        wt("scalar", "act", bld.cnt["act"])
        st("scalar", lambda e: e.activation(out=ST[0:1, 512:1024], in_=ST[0:1, 512:1024],
                                            func=AF.Exp, scale=-1.0), "act", 1)
        act_rc = bld.cnt["act"]
        wt("tensor", "act", act_rc)
        st("tensor", lambda e: e.matmul(PS[:, 512 * 6:512 * 6 + 512],
                                        CONf[0:1, FO["onesr"]:FO["onesr"] + 128],
                                        ST[0:1, 512:1024], start=True, stop=True), "pe", 1)
        pe_bc = bld.cnt["pe"]
        wt("vector", "pe", pe_bc)
        st("vector", lambda e: e.tensor_copy(out=SC2[:, 1024:1536],
                                             in_=PS[:, 512 * 6:512 * 6 + 512]), "dve", 1)
        wt("vector", "dve", bld.cnt["dve"])
        for oc in range(8):
            if mode == "rms":
                st("vector", lambda e, oc=oc, t=t, wcol=wcol: e.scalar_tensor_tensor(
                    out=XB[:, 512 * oc:512 * oc + 512],
                    in0=XR[:, TOK * oc + QT * t:TOK * oc + QT * t + 512],
                    scalar=CONf[:, wcol + oc:wcol + oc + 1], in1=SC2[:, 1024:1536],
                    op0=ALU.mult, op1=ALU.mult), "dve", 1)
            else:
                st("vector", lambda e, oc=oc, wcol=wcol: e.scalar_tensor_tensor(
                    out=XB[:, 512 * oc:512 * oc + 512], in0=XB[:, 512 * oc:512 * oc + 512],
                    scalar=CONf[:, wcol + oc:wcol + oc + 1], in1=SC2[:, 1024:1536],
                    op0=ALU.mult, op1=ALU.mult), "dve", 1)


def build_program(nc, bld, I, y_out, M, ag):
    U1, XB, QF, WR = M["U1"], M["XB"], M["QF"], M["WR"]
    SCb, CONb = M["SCb"], M["CONb"]
    XR, TAB, ST, SC2, CONf = M["XR"], M["TAB"], M["ST"], M["SC2"], M["CONf"]
    STG = M["STG"]
    PS, PSB = M["PS"], M["PSB"]
    st, wt = bld.step, bld.wait

    stg = {"pp": 0, "pool_done": {0: 0, 1: 0}}

    def wload(dst, src2d, ncols, kcn, wait_pe=None):
        pp = stg["pp"]; stg["pp"] ^= 1
        wt("sync", "pool", stg["pool_done"][pp])
        st("sync", lambda e, pp=pp, src2d=src2d, ncols=ncols, kcn=kcn: e.dma_start(
            out=STG[:, 1024 * pp:1024 * pp + ncols].rearrange("p (k f) -> p k f", k=kcn),
            in_=src2d.rearrange("(k p) f -> p k f", p=128)), "dws%d" % pp, 16)
        v = bld.cnt["dws%d" % pp]
        wt("gpsimd", "dws%d" % pp, v)
        if wait_pe is not None:
            wt("gpsimd", "pe", wait_pe)
        st("gpsimd", lambda e, pp=pp, dst=dst, ncols=ncols: e.tensor_copy(
            out=dst, in_=STG[:, 1024 * pp:1024 * pp + ncols]), "pool", 1)
        stg["pool_done"][pp] = bld.cnt["pool"]
        return bld.cnt["pool"]

    for cc in range(8):
        st("sync", lambda e, cc=cc: e.dma_start(
            out=XR[:, TOK * cc:TOK * (cc + 1)], in_=I["x"][128 * cc:128 * (cc + 1), :]),
            "dx", 16)
    for nm, sl in (("identb", (CO["ident"], 128, 128)), ("ones1", (CO["ones1"], 4, 128)),
                   ("ones2", (CO["ones2"], 2, 128)), ("onesc", (CO["onesc"], 1, 128)),
                   ("oneh1", (CO["oneh1"], 16, 128)), ("oneh2", (CO["oneh2"], 4, 128))):
        off, w, p = sl
        st("sync", lambda e, nm=nm, off=off, w=w: e.dma_start(
            out=CONb[:, off:off + w], in_=I[nm]), "dx", 16)
    st("sync", lambda e: e.dma_start(out=CONb[0:4, CO["esel1T"]:CO["esel1T"] + 128],
                                     in_=I["esel1T"]), "dx", 16)
    st("sync", lambda e: e.dma_start(out=CONb[0:2, CO["esel2T"]:CO["esel2T"] + 128],
                                     in_=I["esel2T"]), "dx", 16)
    st("sync", lambda e: e.dma_start(out=CONb[0:1, CO["onesrb"]:CO["onesrb"] + 128],
                                     in_=I["onesrb"]), "dx", 16)
    st("sync", lambda e: e.dma_start(out=CONb[0:ER, CO["sele"]:CO["sele"] + ER * 128],
                                     in_=I["sele"]), "dx", 16)
    for nm, off, w in (("r1T", FO["r1T"], 128), ("r2T", FO["r2T"], 128),
                       ("wqkT", FO["wqkT"], 32), ("identf", FO["identf"], 128),
                       ("l2oT", FO["l2oT"], 64), ("q2lT", FO["q2lT"], 32),
                       ("kv2lT", FO["kv2lT"], 32)):
        st("sync", lambda e, nm=nm, off=off, w=w: e.dma_start(
            out=CONf[:, off:off + w], in_=I[nm]), "dx", 16)
    st("sync", lambda e: e.dma_start(out=CONf[0:1, FO["onesr"]:FO["onesr"] + 128],
                                     in_=I["onesr"]), "dx", 16)
    for nm, off, w in (("bqkv", BO["bqkv"], 24), ("bq2l", BO["bq2l"], 1),
                       ("bkv2l", BO["bkv2l"], 1), ("bwqk", BO["bwqk"], 1),
                       ("bl2o", BO["bl2o"], 1), ("bcproj", BO["bcproj"], 8),
                       ("wln1", BO["wln1"], 8), ("wln2", BO["wln2"], 8)):
        st("sync", lambda e, nm=nm, off=off, w=w: e.dma_start(
            out=CONf[:, off:off + w], in_=I[nm]), "dx", 16)
    st("sync", lambda e: e.dma_start(out=CONf[0:ER, BO["brout"]:BO["brout"] + 1],
                                     in_=I["brout"]), "dx", 16)
    dx_pre = bld.cnt["dx"]

    wt("gpsimd", "dx", dx_pre)
    st("gpsimd", lambda e: e.tensor_copy(out=CONb[:, CO["l2oT"]:CO["l2oT"] + 64],
                                         in_=CONf[:, FO["l2oT"]:FO["l2oT"] + 64]), "pool", 1)
    st("gpsimd", lambda e: e.tensor_copy(out=CONb[:, CO["q2lT"]:CO["q2lT"] + 32],
                                         in_=CONf[:, FO["q2lT"]:FO["q2lT"] + 32]), "pool", 1)
    st("gpsimd", lambda e: e.tensor_copy(out=CONb[:, CO["kv2lT"]:CO["kv2lT"] + 32],
                                         in_=CONf[:, FO["kv2lT"]:FO["kv2lT"] + 32]), "pool", 1)
    pool_consts = bld.cnt["pool"]

    for oc in range(24):
        wload(U1[:, 1024 * oc:1024 * oc + 1024],
              I["wqkv"][:, 128 * oc:128 * oc + 128], 1024, 8)
    dw_qkv = bld.cnt["pool"]

    for t in range(NT):
        wt("sync", "dve", bld.cnt["dve"])
        st("sync", lambda e, t=t: e.dma_start(out=TAB[:, 0:512],
                                              in_=I["cos1"][:, QT * t:QT * t + 512]), "dt", 16)
        st("sync", lambda e, t=t: e.dma_start(out=TAB[:, 512:1024],
                                              in_=I["sin1"][:, QT * t:QT * t + 512]), "dt", 16)
        dx_tab = bld.cnt["dt"]
        wt("gpsimd", "dx", dx_pre)
        wt("gpsimd", "pe", bld.cnt["pe"])
        for cc in range(8):
            st("gpsimd", lambda e, t=t, cc=cc: e.tensor_copy(
                out=XB[:, QT * cc:QT * (cc + 1)],
                in_=XR[:, TOK * cc + QT * t:TOK * cc + QT * t + QT]), "pool", 1)
        pool_xb = bld.cnt["pool"]
        wt("tensor", "pool", pool_xb)
        if t == 0:
            wt("tensor", "pool", dw_qkv)
            wt("tensor", "dx", dx_pre)
        for hg in range(4):
            ocs = [2 * hg, 2 * hg + 1, 8 + 2 * hg, 8 + 2 * hg + 1,
                   16 + 2 * hg, 16 + 2 * hg + 1]
            wt("vector", "da", bld.cnt["da"])
            for i6, oc in enumerate(ocs):
                bk = i6 % 2
                wt("tensor", "dve", bld.cnt["dve"])
                for kc in range(8):
                    st("tensor", lambda e, bk=bk, oc=oc, kc=kc: e.matmul(
                        bank(PS, bk), U1[:, 1024 * oc + 128 * kc:1024 * oc + 128 * (kc + 1)],
                        XB[:, QT * kc:QT * (kc + 1)],
                        start=(kc == 0), stop=(kc == 7)), "pe", 1)
                pe_mm = bld.cnt["pe"]
                wt("vector", "pe", pe_mm)
                st("vector", lambda e, bk=bk, i6=i6, oc=oc: e.tensor_scalar_add(
                    out=SCb[:, 512 * i6:512 * (i6 + 1)], in0=bank(PS, bk),
                    scalar1=CONf[:, BO["bqkv"] + oc:BO["bqkv"] + oc + 1]),
                    "dve", 1)
            dve_qkv = bld.cnt["dve"]
            wt("tensor", "dve", dve_qkv)
            if t == 0 and hg == 0:
                wt("tensor", "pool", pool_consts)
            for src, wname, bnk in ((0, "q2lT", 2), (2, "kv2lT", 3), (4, "kv2lT", 4)):
                for hh in range(4):
                    rb = 64 * (hh % 2)
                    cb = 32 * hh
                    ch = src + hh // 2
                    st("tensor", lambda e, rb=rb, cb=cb, ch=ch, wname=wname, bnk=bnk, hh=hh: e.matmul(
                        PS[cb:cb + 32, 512 * bnk:512 * bnk + 512],
                        CONb[rb:rb + 64, CO[wname]:CO[wname] + 32],
                        SCb[rb:rb + 64, 512 * ch:512 * ch + 512],
                        start=True, stop=True, tile_position=(rb, cb)), "pe", 1)
            pe_lat = bld.cnt["pe"]
            wt("vector", "pe", pe_lat)
            wt("vector", "pool", bld.cnt["pool"])
            st("vector", lambda e: e.tensor_scalar_add(out=SC2[:, 0:512], in0=bank(PS, 2),
                scalar1=CONf[:, BO["bq2l"]:BO["bq2l"] + 1]), "dve", 1)
            st("vector", lambda e: e.tensor_scalar_add(out=SC2[:, 512:1024], in0=bank(PS, 3),
                scalar1=CONf[:, BO["bkv2l"]:BO["bkv2l"] + 1]), "dve", 1)
            st("vector", lambda e: e.tensor_scalar_add(out=SCb[:, 0:512], in0=bank(PS, 4),
                scalar1=CONf[:, BO["bkv2l"]:BO["bkv2l"] + 1]), "dve", 1)
            act_lat = bld.cnt["dve"]
            for off in (0, 512):
                wt("tensor", "dve", bld.cnt["dve"])
                st("tensor", lambda e, off=off: e.matmul(
                    bank(PS, 5), CONf[:, FO["r1T"]:FO["r1T"] + 128], SC2[:, off:off + 512],
                    start=True, stop=True), "pe", 1)
                pe_rot = bld.cnt["pe"]
                wt("vector", "pe", pe_rot)
                if hg == 0 and off == 0:
                    wt("vector", "dt", dx_tab)
                st("vector", lambda e, off=off: e.tensor_mul(
                    out=SC2[:, 1024 + off:1024 + off + 512], in0=SC2[:, off:off + 512],
                    in1=TAB[:, 0:512]), "dve", 1)
                wt("vector", "dve", bld.cnt["dve"])
                st("vector", lambda e, off=off: e.tensor_mul(
                    out=SC2[:, off:off + 512], in0=bank(PS, 5), in1=TAB[:, 512:1024]), "dve", 1)
                wt("vector", "dve", bld.cnt["dve"])
                st("vector", lambda e, off=off: e.tensor_add(
                    out=SC2[:, off:off + 512], in0=SC2[:, off:off + 512],
                    in1=SC2[:, 1024 + off:1024 + off + 512]), "dve", 1)
            dve_rope = bld.cnt["dve"]
            wt("tensor", "dve", dve_rope)
            for hh in range(4):
                st("tensor", lambda e, hh=hh: e.matmul(
                    PS[32 * hh:32 * hh + 32, 512 * 6:512 * 6 + 512],
                    CONf[32 * hh:32 * hh + 32, FO["wqkT"]:FO["wqkT"] + 32],
                    SC2[32 * hh:32 * hh + 32, 0:512],
                    start=True, stop=True, tile_position=(32 * hh, 32 * hh)), "pe", 1)
            pe_qf = bld.cnt["pe"]
            wt("vector", "pe", pe_qf)
            st("vector", lambda e, hg=hg, t=t: e.tensor_scalar_add(
                out=QF[:, 2048 * hg + 512 * t:2048 * hg + 512 * t + 512], in0=bank(PS, 6),
                scalar1=CONf[:, BO["bwqk"]:BO["bwqk"] + 1]), "dve", 1)
            act_qf = bld.cnt["dve"]
            wt("gpsimd", "dve", dve_rope)
            wt("gpsimd", "da", bld.cnt["da"])
            st("gpsimd", lambda e: e.tensor_copy(out=SCb[:, 512:1024],
                                                 in_=SC2[:, 512:1024]), "pool", 1)
            pool_k = bld.cnt["pool"]
            wt("sync", "pool", pool_k)
            st("sync", lambda e, hg=hg, t=t: e.dma_start(
                out=ag["ag1k_i"].ap()[128 * hg:128 * hg + 128, QT * t:QT * (t + 1)],
                in_=SCb[:, 512:1024]), "da", 16)
            for s in range(4):
                wt("tensor", "dve", act_lat)
                st("tensor", lambda e, s=s: e.transpose(
                    PSB[:, 128 * s:128 * s + 128], SCb[:, 128 * s:128 * s + 128],
                    CONb[:, CO["ident"]:CO["ident"] + 128]), "pe", 1)
            pe_tp = bld.cnt["pe"]
            wt("vector", "pe", pe_tp)
            wt("vector", "da", bld.cnt["da"])
            st("vector", lambda e: e.tensor_copy(out=SCb[:, 1024:1536],
                                                 in_=PSB[:, 0:512]), "dve", 1)
            dve_vt = bld.cnt["dve"]
            wt("sync", "dve", dve_vt)
            st("sync", lambda e, hg=hg, t=t: e.dma_start(
                out=ag["ag1v_i"].ap()[QT * t:QT * t + 512, 128 * hg:128 * hg + 128]
                .rearrange("(s p) f -> p s f", p=128),
                in_=SCb[:, 1024:1536].rearrange("p (s f) -> p s f", s=4)), "da", 16)
            wt("scalar", "dve", act_qf)
            st("scalar", lambda e, hg=hg, t=t: e.activation(
                out=SCb[:, 2048:2560],
                in_=QF[:, 2048 * hg + 512 * t:2048 * hg + 512 * t + 512],
                func=AF.Square), "act", 1)
            act_sq = bld.cnt["act"]
            wt("tensor", "act", act_sq)
            wt("tensor", "dve", bld.cnt["dve"])
            st("tensor", lambda e: e.matmul(
                PS[0:4, 512 * 6:512 * 6 + 512], CONb[:, CO["ones1"]:CO["ones1"] + 4],
                SCb[:, 2048:2560], start=True, stop=True), "pe", 1)
            pe_q2 = bld.cnt["pe"]
            wt("vector", "pe", pe_q2)
            st("vector", lambda e, hg=hg, t=t: e.reduce_max(
                ST[0:4, 64 + 16 * hg + t:64 + 16 * hg + t + 1],
                PS[0:4, 512 * 6:512 * 6 + 512], axis=AX.X), "dve", 1)
    da_A = bld.cnt["da"]

    wt("gpsimd", "da", da_A)
    st("gpsimd", lambda e: e.collective_compute(
        "AllGather", ALU.bypass, replica_groups=ag["RG"],
        ins=[ag["ag1k_i"].ap()], outs=[ag["ag1k_o"].ap()]), "cc", 1)
    wt("gpsimd", "cc", bld.cnt["cc"])
    st("gpsimd", lambda e: e.collective_compute(
        "AllGather", ALU.bypass, replica_groups=ag["RG"],
        ins=[ag["ag1v_i"].ap()], outs=[ag["ag1v_o"].ap()]), "cc", 1)
    cc1 = bld.cnt["cc"]

    wt("gpsimd", "cc", cc1)
    wt("sync", "cc", cc1)
    last_pe_kst = {0: 0, 1: 0}
    for hg in range(4):
        for rr in range(4):
            pp = rr % 2
            wt("sync", "pe", last_pe_kst[pp])
            st("sync", lambda e, hg=hg, rr=rr, pp=pp: e.dma_start(
                out=U1[:, KST0 + 2048 * pp:KST0 + 2048 * pp + 2048],
                in_=ag["ag1k_o"].ap()[512 * rr + 128 * hg:512 * rr + 128 * hg + 128, :]),
                "dk%d" % pp, 16)
            dk_n = bld.cnt["dk%d" % pp]
            wt("scalar", "dk%d" % pp, dk_n)
            for q4 in range(4):
                wt("scalar", "pe", bld.cnt["pe"])
                st("scalar", lambda e, pp=pp, q4=q4: e.activation(
                    out=SCb[:, 512 * (q4 % 2):512 * (q4 % 2) + 512],
                    in_=U1[:, KST0 + 2048 * pp + 512 * q4:KST0 + 2048 * pp + 512 * q4 + 512],
                    func=AF.Square), "act", 1)
                a_n = bld.cnt["act"]
                wt("tensor", "act", a_n)
                wt("tensor", "dve", bld.cnt["dve"])
                st("tensor", lambda e, q4=q4: e.matmul(
                    PS[0:4, 512 * 6:512 * 6 + 512], CONb[:, CO["ones1"]:CO["ones1"] + 4],
                    SCb[:, 512 * (q4 % 2):512 * (q4 % 2) + 512], start=True, stop=True),
                    "pe", 1)
                p_n = bld.cnt["pe"]
                wt("vector", "pe", p_n)
                st("vector", lambda e, hg=hg, rr=rr, q4=q4: e.reduce_max(
                    ST[0:4, 80 + 16 * hg + 4 * rr + q4:81 + 16 * hg + 4 * rr + q4],
                    PS[0:4, 512 * 6:512 * 6 + 512], axis=AX.X), "dve", 1)
            last_pe_kst[pp] = bld.cnt["pe"]
    wt("vector", "dve", bld.cnt["dve"])
    for hg in range(4):
        st("vector", lambda e, hg=hg: e.reduce_max(
            ST[0:4, 32 + hg:33 + hg], ST[0:4, 64 + 16 * hg:64 + 16 * hg + 4], axis=AX.X),
            "dve", 1)
        st("vector", lambda e, hg=hg: e.reduce_max(
            ST[0:4, 40 + hg:41 + hg], ST[0:4, 80 + 16 * hg:80 + 16 * hg + 16], axis=AX.X),
            "dve", 1)
    dve_mx = bld.cnt["dve"]
    wt("scalar", "dve", dve_mx)
    st("scalar", lambda e: e.activation(out=ST[0:4, 48:52], in_=ST[0:4, 32:36],
                                        func=AF.Sqrt), "act", 1)
    st("scalar", lambda e: e.activation(out=ST[0:4, 52:56], in_=ST[0:4, 40:44],
                                        func=AF.Sqrt), "act", 1)
    act_sq2 = bld.cnt["act"]
    wt("vector", "act", act_sq2)
    st("vector", lambda e: e.tensor_mul(out=ST[0:4, 56:60], in0=ST[0:4, 48:52],
                                        in1=ST[0:4, 52:56]), "dve", 1)
    wt("vector", "dve", bld.cnt["dve"])
    st("vector", lambda e: e.tensor_scalar_mul(out=ST[0:4, 56:60], in0=ST[0:4, 56:60],
                                               scalar1=-float(SC1)), "dve", 1)
    wt("vector", "dve", bld.cnt["dve"])
    st("vector", lambda e: e.tensor_copy(out=SCb[0:4, 0:4], in_=ST[0:4, 56:60]), "dve", 1)
    dve_bias = bld.cnt["dve"]
    wt("tensor", "dve", dve_bias)
    for hg in range(4):
        wt("tensor", "dve", bld.cnt["dve"])
        st("tensor", lambda e, hg=hg: e.matmul(
            PS[:, 512 * 6:512 * 6 + 1], CONb[0:4, CO["esel1T"]:CO["esel1T"] + 128],
            SCb[0:4, hg:hg + 1], start=True, stop=True), "pe", 1)
        p_b = bld.cnt["pe"]
        wt("vector", "pe", p_b)
        st("vector", lambda e, hg=hg: e.tensor_copy(
            out=ST[:, 8 + hg:9 + hg], in_=PS[:, 512 * 6:512 * 6 + 1]), "dve", 1)
    dve_bx = bld.cnt["dve"]

    for t in range(NT):
        wt("sync", "dve", bld.cnt["dve"])
        for mi in range(4):
            st("sync", lambda e, mi=mi: e.dma_start(
                out=U1[:, MSK0 + 2048 * mi:MSK0 + 2048 * mi + 2048]
                .rearrange("p (a f) -> p a f", a=4),
                in_=I["mask"][512 * mi:512 * mi + 512, :]
                .rearrange("(a p) f -> p a f", p=128)), "dm", 16)
        dk_msk = bld.cnt["dm"]
        for hg in range(4):
            first = True
            wt("tensor", "dve", bld.cnt["dve"])
            for m2 in range(t + 1):
                pp = m2 % 2
                wt("sync", "pe", last_pe_kst[pp])
                for rr in range(4):
                    st("sync", lambda e, rr=rr, hg=hg, pp=pp, m2=m2: e.dma_start(
                        out=U1[:, KST0 + 2048 * pp + 512 * rr:KST0 + 2048 * pp + 512 * rr + 512],
                        in_=ag["ag1k_o"].ap()[512 * rr + 128 * hg:512 * rr + 128 * hg + 128,
                                              QT * m2:QT * m2 + 512]), "dk%d" % pp, 16)
                    st("sync", lambda e, rr=rr, hg=hg, pp=pp, m2=m2: e.dma_start(
                        out=U1[:, VST0 + 2048 * pp + 512 * rr:VST0 + 2048 * pp + 512 * rr + 512]
                        .rearrange("p (s f) -> p s f", s=4),
                        in_=ag["ag1v_o"].ap()[TOK * rr + QT * m2:TOK * rr + QT * m2 + 512,
                                              128 * hg:128 * hg + 128]
                        .rearrange("(s p) f -> p s f", p=128)), "dk%d" % pp, 16)
                dk_kv = bld.cnt["dk%d" % pp]
                for rr in range(4):
                    for kb in range(4):
                        is_last = (m2 == t and rr == 3 and kb == 3)
                        wt("tensor", "dk%d" % pp, dk_kv)
                        wt("tensor", "act", bld.cnt["act"])
                        for hh in range(4):
                            st("tensor", lambda e, hh=hh, pp=pp, rr=rr, kb=kb, hg=hg, t=t: e.matmul(
                                bank(PS, hh),
                                U1[32 * hh:32 * hh + 32,
                                   KST0 + 2048 * pp + 512 * rr + 128 * kb:
                                   KST0 + 2048 * pp + 512 * rr + 128 * kb + 128],
                                QF[32 * hh:32 * hh + 32,
                                   2048 * hg + 512 * t:2048 * hg + 512 * t + 512],
                                start=True, stop=True, skip_group_check=True,
                                tile_position=(32 * hh, 0)), "pe", 1)
                        pe_s = bld.cnt["pe"]
                        wt("scalar", "pe", pe_s)
                        st("scalar", lambda e, hg=hg: e.activation(
                            out=U1[:, PT0:PT0 + 2048], in_=PS[:, 0:2048], func=AF.Exp,
                            bias=ST[:, 8 + hg:9 + hg], scale=float(SC1)), "act", 1)
                        act_e = bld.cnt["act"]
                        if m2 == t:
                            wt("vector", "act", act_e)
                            if rr == 0 and kb == 0:
                                wt("vector", "dm", dk_msk)
                            for hh in range(4):
                                st("vector", lambda e, hh=hh, rr=rr, kb=kb: e.tensor_mul(
                                    out=U1[:, PT0 + 512 * hh:PT0 + 512 * hh + 512],
                                    in0=U1[:, PT0 + 512 * hh:PT0 + 512 * hh + 512],
                                    in1=U1[:, MSK0 + 512 * (4 * rr + kb):
                                           MSK0 + 512 * (4 * rr + kb) + 512]), "dve", 1)
                            wt("tensor", "dve", bld.cnt["dve"])
                        else:
                            wt("tensor", "act", act_e)
                        for hh in range(4):
                            st("tensor", lambda e, hh=hh, pp=pp, rr=rr, kb=kb, first=first,
                               is_last=is_last: e.matmul(
                                PS[32 * hh:32 * hh + 32, 512 * 4:512 * 4 + 512],
                                U1[:, VST0 + 2048 * pp + 512 * rr + 128 * kb + 32 * hh:
                                   VST0 + 2048 * pp + 512 * rr + 128 * kb + 32 * hh + 32],
                                U1[:, PT0 + 512 * hh:PT0 + 512 * hh + 512],
                                start=first, stop=is_last, skip_group_check=True,
                                tile_position=(0, 32 * hh)), "pe", 1)
                        for hh in range(4):
                            st("tensor", lambda e, hh=hh, first=first, is_last=is_last: e.matmul(
                                PS[0:4, 512 * 5:512 * 5 + 512],
                                CONb[:, CO["oneh1"] + 4 * hh:CO["oneh1"] + 4 * hh + 4],
                                U1[:, PT0 + 512 * hh:PT0 + 512 * hh + 512],
                                start=(first and hh == 0), stop=(is_last and hh == 3),
                                skip_group_check=True), "pe", 1)
                        first = False
                last_pe_kst[pp] = bld.cnt["pe"]
            pe_attn = bld.cnt["pe"]
            wt("scalar", "pe", pe_attn)
            st("scalar", lambda e: e.activation(out=ST[0:4, 1024:1536],
                                                in_=PS[0:4, 512 * 5:512 * 5 + 512],
                                                func=AF.Ln), "act", 1)
            wt("scalar", "act", bld.cnt["act"])
            st("scalar", lambda e: e.activation(out=ST[0:4, 1024:1536], in_=ST[0:4, 1024:1536],
                                                func=AF.Exp, scale=-1.0), "act", 1)
            act_rc = bld.cnt["act"]
            wt("vector", "act", act_rc)
            st("vector", lambda e: e.tensor_copy(out=SCb[0:4, 1024:1536],
                                                 in_=ST[0:4, 1024:1536]), "dve", 1)
            dve_rcb = bld.cnt["dve"]
            wt("tensor", "dve", dve_rcb)
            st("tensor", lambda e: e.matmul(
                bank(PS, 6), CONb[0:4, CO["esel1T"]:CO["esel1T"] + 128], SCb[0:4, 1024:1536],
                start=True, stop=True), "pe", 1)
            pe_rb = bld.cnt["pe"]
            wt("vector", "pe", pe_rb)
            st("vector", lambda e: e.tensor_copy(out=SC2[:, 0:512], in_=bank(PS, 6)), "dve", 1)
            dve_rb2 = bld.cnt["dve"]
            wt("vector", "dve", dve_rb2)
            st("vector", lambda e: e.tensor_mul(out=SCb[:, 2048:2560], in0=bank(PS, 4),
                                                in1=SC2[:, 0:512]), "dve", 1)
            dve_yn = bld.cnt["dve"]
            wt("tensor", "dve", dve_yn)
            for hh in range(4):
                bnk = 5 + hh // 2
                st("tensor", lambda e, hh=hh, bnk=bnk: e.matmul(
                    PS[64 * (hh % 2):64 * (hh % 2) + 64, 512 * bnk:512 * bnk + 512],
                    CONb[32 * hh:32 * hh + 32, CO["l2oT"]:CO["l2oT"] + 64],
                    SCb[32 * hh:32 * hh + 32, 2048:2560],
                    start=True, stop=True, tile_position=(32 * hh, 64 * (hh % 2))), "pe", 1)
            pe_l2o = bld.cnt["pe"]
            wt("vector", "pe", pe_l2o)
            for half in range(2):
                st("vector", lambda e, hg=hg, half=half: e.tensor_scalar_add(
                    out=U1[:, YH0 + 1024 * hg + 512 * half:YH0 + 1024 * hg + 512 * half + 512],
                    in0=bank(PS, 5 + half),
                    scalar1=CONf[:, BO["bl2o"]:BO["bl2o"] + 1]), "dve", 1)
        act_yh = bld.cnt["dve"]
        last_pe_cp = {0: bld.cnt["pe"], 1: bld.cnt["pe"]}
        for oc in range(8):
            dw_c = wload(WR[:, 1024 * (oc % 2):1024 * (oc % 2) + 1024],
                         I["cprojT"][:, 128 * oc:128 * oc + 128], 1024, 8,
                         wait_pe=last_pe_cp[oc % 2])
            wt("tensor", "pool", dw_c)
            wt("tensor", "dve", bld.cnt["dve"])
            for kc in range(8):
                st("tensor", lambda e, oc=oc, kc=kc: e.matmul(
                    bank(PS, oc % 2),
                    WR[:, 1024 * (oc % 2) + 128 * kc:1024 * (oc % 2) + 128 * kc + 128],
                    U1[:, YH0 + 512 * kc:YH0 + 512 * kc + 512],
                    start=(kc == 0), stop=(kc == 7)), "pe", 1)
            last_pe_cp[oc % 2] = bld.cnt["pe"]
            pe_cp = bld.cnt["pe"]
            wt("vector", "pe", pe_cp)
            st("vector", lambda e, oc=oc, t=t: e.scalar_tensor_tensor(
                out=XR[:, TOK * oc + QT * t:TOK * oc + QT * t + 512], in0=bank(PS, oc % 2),
                scalar=CONf[:, BO["bcproj"] + oc:BO["bcproj"] + oc + 1],
                in1=XR[:, TOK * oc + QT * t:TOK * oc + QT * t + 512],
                op0=ALU.add, op1=ALU.add), "dve", 1)
        emit_stats_norm(bld, M, t, None, "ln")

    wt("gpsimd", "dve", bld.cnt["dve"])
    wt("gpsimd", "act", bld.cnt["act"])
    wt("gpsimd", "pe", bld.cnt["pe"])
    for wname, kcn, mcn in (("wkvd", 8, 2), ("wqd", 8, 2), ("wku", 2, 4),
                            ("wropek", 8, 4), ("wvu", 2, 8)):
        for mc in range(mcn):
            wload(U1[:, WOFF[wname] + 128 * kcn * mc:WOFF[wname] + 128 * kcn * (mc + 1)],
                  I[wname][:, 128 * mc:128 * mc + 128], 128 * kcn, kcn)
    for mc in range(8):
        wload(WR[:, WQQ0 + 256 * mc:WQQ0 + 256 * (mc + 1)],
              I["wqq"][:, 128 * mc:128 * mc + 128], 256, 2)
    dw_C = bld.cnt["pool"]

    for t in range(NT):
        wt("sync", "dve", bld.cnt["dve"])
        st("sync", lambda e, t=t: e.dma_start(out=TAB[:, 0:512],
                                              in_=I["cos2"][:, QT * t:QT * t + 512]), "dt", 16)
        st("sync", lambda e, t=t: e.dma_start(out=TAB[:, 512:1024],
                                              in_=I["sin2"][:, QT * t:QT * t + 512]), "dt", 16)
        dx_tab = bld.cnt["dt"]
        emit_stats_norm(bld, M, t, BO["wln1"], "rms")
        wt("tensor", "dve", bld.cnt["dve"])
        if t == 0:
            wt("tensor", "pool", dw_C)
        for mc in range(2):
            for src, bnk in (("wkvd", 0), ("wqd", 2)):
                for kc in range(8):
                    st("tensor", lambda e, src=src, mc=mc, kc=kc, bnk=bnk: e.matmul(
                        bank(PS, bnk + mc),
                        U1[:, WOFF[src] + 128 * (8 * mc + kc):WOFF[src] + 128 * (8 * mc + kc) + 128],
                        XB[:, QT * kc:QT * (kc + 1)], start=(kc == 0), stop=(kc == 7)), "pe", 1)
        pe_lat2 = bld.cnt["pe"]
        wt("scalar", "pe", pe_lat2)
        for mc in range(2):
            st("scalar", lambda e, mc=mc: e.activation(
                out=SCb[:, 512 * mc:512 * mc + 512], in_=bank(PS, mc), func=AF.Copy), "act", 1)
        for mc in range(2):
            st("scalar", lambda e, mc=mc, t=t: e.activation(
                out=QF[:, 512 * (2 * t + mc):512 * (2 * t + mc) + 512],
                in_=bank(PS, 2 + mc), func=AF.Copy), "act", 1)
        act_lat2 = bld.cnt["act"]
        for g in range(8):
            bnk = 4 + g % 2
            wt("tensor", "act", bld.cnt["act"])
            wt("tensor", "dve", bld.cnt["dve"])
            for hh in range(2):
                h = 2 * g + hh
                for kc in range(2):
                    st("tensor", lambda e, h=h, hh=hh, kc=kc, bnk=bnk: e.matmul(
                        PS[64 * hh:64 * hh + 32, 512 * bnk:512 * bnk + 512],
                        U1[:, WOFF["wku"] + 128 * (2 * (h // 4) + kc) + 32 * (h % 4):
                           WOFF["wku"] + 128 * (2 * (h // 4) + kc) + 32 * (h % 4) + 32],
                        SCb[:, 512 * kc:512 * kc + 512],
                        start=(kc == 0), stop=(kc == 1),
                        tile_position=(0, 64 * hh)), "pe", 1)
                for kc in range(8):
                    st("tensor", lambda e, h=h, hh=hh, kc=kc, bnk=bnk: e.matmul(
                        PS[64 * hh + 32:64 * hh + 64, 512 * bnk:512 * bnk + 512],
                        U1[:, WOFF["wropek"] + 128 * (8 * (h // 4) + kc) + 32 * (h % 4):
                           WOFF["wropek"] + 128 * (8 * (h // 4) + kc) + 32 * (h % 4) + 32],
                        XB[:, QT * kc:QT * (kc + 1)],
                        start=(kc == 0), stop=(kc == 7),
                        tile_position=(0, 64 * hh + 32)), "pe", 1)
            pe_kk = bld.cnt["pe"]
            wt("scalar", "pe", pe_kk)
            st("scalar", lambda e, bnk=bnk: e.activation(out=SC2[:, 0:512], in_=bank(PS, bnk),
                                                         func=AF.Copy), "act", 1)
            act_kk = bld.cnt["act"]
            wt("tensor", "act", act_kk)
            st("tensor", lambda e: e.matmul(bank(PS, 6), CONf[:, FO["r2T"]:FO["r2T"] + 128],
                                            SC2[:, 0:512], start=True, stop=True), "pe", 1)
            pe_rot2 = bld.cnt["pe"]
            wt("vector", "pe", pe_rot2)
            wt("vector", "da", bld.cnt["da"])
            if g == 0:
                wt("vector", "dt", dx_tab)
            st("vector", lambda e: e.tensor_mul(out=SC2[:, 512:1024], in0=SC2[:, 0:512],
                                                in1=TAB[:, 0:512]), "dve", 1)
            wt("vector", "dve", bld.cnt["dve"])
            st("vector", lambda e: e.tensor_mul(out=SC2[:, 0:512], in0=bank(PS, 6),
                                                in1=TAB[:, 512:1024]), "dve", 1)
            wt("vector", "dve", bld.cnt["dve"])
            st("vector", lambda e: e.tensor_add(out=SCb[:, 1024:1536], in0=SC2[:, 0:512],
                                                in1=SC2[:, 512:1024]), "dve", 1)
            dve_kk = bld.cnt["dve"]
            wt("sync", "dve", dve_kk)
            st("sync", lambda e, g=g, t=t: e.dma_start(
                out=ag["ag2k_i"].ap()[128 * g:128 * g + 128, QT * t:QT * (t + 1)],
                in_=SCb[:, 1024:1536]), "da", 16)
            vb = 4 + (g + 1) % 2
            for kc in range(2):
                st("tensor", lambda e, g=g, kc=kc, vb=vb: e.matmul(
                    bank(PS, vb),
                    U1[:, WOFF["wvu"] + 128 * (2 * g + kc):WOFF["wvu"] + 128 * (2 * g + kc) + 128],
                    SCb[:, 512 * kc:512 * kc + 512], start=(kc == 0), stop=(kc == 1)), "pe", 1)
            pe_v2 = bld.cnt["pe"]
            wt("scalar", "pe", pe_v2)
            st("scalar", lambda e, vb=vb: e.activation(out=SCb[:, 1536:2048],
                                                       in_=bank(PS, vb), func=AF.Copy), "act", 1)
            act_v2 = bld.cnt["act"]
            for s in range(4):
                wt("tensor", "act", act_v2)
                st("tensor", lambda e, s=s: e.transpose(
                    PSB[:, 128 * s:128 * s + 128], SCb[:, 1536 + 128 * s:1536 + 128 * s + 128],
                    CONb[:, CO["ident"]:CO["ident"] + 128]), "pe", 1)
            pe_vt2 = bld.cnt["pe"]
            wt("vector", "pe", pe_vt2)
            wt("vector", "da", bld.cnt["da"])
            st("vector", lambda e: e.tensor_copy(out=SCb[:, 2048:2560], in_=PSB[:, 0:512]),
               "dve", 1)
            dve_vt2 = bld.cnt["dve"]
            wt("sync", "dve", dve_vt2)
            st("sync", lambda e, g=g, t=t: e.dma_start(
                out=ag["ag2v_i"].ap()[QT * t:QT * t + 512, 128 * g:128 * g + 128]
                .rearrange("(s p) f -> p s f", p=128),
                in_=SCb[:, 2048:2560].rearrange("p (s f) -> p s f", s=4)), "da", 16)
    da_C = bld.cnt["da"]
    wt("gpsimd", "da", da_C)
    st("gpsimd", lambda e: e.collective_compute(
        "AllGather", ALU.bypass, replica_groups=ag["RG"],
        ins=[ag["ag2k_i"].ap()], outs=[ag["ag2k_o"].ap()]), "cc", 1)
    wt("gpsimd", "cc", bld.cnt["cc"])
    st("gpsimd", lambda e: e.collective_compute(
        "AllGather", ALU.bypass, replica_groups=ag["RG"],
        ins=[ag["ag2v_i"].ap()], outs=[ag["ag2v_o"].ap()]), "cc", 1)
    cc2 = bld.cnt["cc"]

    wt("gpsimd", "cc", cc2)
    wt("sync", "cc", cc2)
    last_pe_kst = {0: bld.cnt["pe"], 1: bld.cnt["pe"]}
    for g in range(8):
        for rr in range(4):
            pp = rr % 2
            wt("sync", "pe", last_pe_kst[pp])
            st("sync", lambda e, g=g, rr=rr, pp=pp: e.dma_start(
                out=U1[:, KST0 + 2048 * pp:KST0 + 2048 * pp + 2048],
                in_=ag["ag2k_o"].ap()[1024 * rr + 128 * g:1024 * rr + 128 * g + 128, :]),
                "dk%d" % pp, 16)
            dk_n = bld.cnt["dk%d" % pp]
            wt("scalar", "dk%d" % pp, dk_n)
            for q4 in range(4):
                wt("scalar", "pe", bld.cnt["pe"])
                st("scalar", lambda e, pp=pp, q4=q4: e.activation(
                    out=SCb[:, 512 * (q4 % 2):512 * (q4 % 2) + 512],
                    in_=U1[:, KST0 + 2048 * pp + 512 * q4:KST0 + 2048 * pp + 512 * q4 + 512],
                    func=AF.Square), "act", 1)
                a_n = bld.cnt["act"]
                wt("tensor", "act", a_n)
                wt("tensor", "dve", bld.cnt["dve"])
                st("tensor", lambda e, q4=q4: e.matmul(
                    PS[0:2, 512 * 6:512 * 6 + 512], CONb[:, CO["ones2"]:CO["ones2"] + 2],
                    SCb[:, 512 * (q4 % 2):512 * (q4 % 2) + 512], start=True, stop=True),
                    "pe", 1)
                p_n = bld.cnt["pe"]
                wt("vector", "pe", p_n)
                st("vector", lambda e, g=g, rr=rr, q4=q4: e.reduce_max(
                    ST[0:2, 128 + 16 * g + 4 * rr + q4:129 + 16 * g + 4 * rr + q4],
                    PS[0:2, 512 * 6:512 * 6 + 512], axis=AX.X), "dve", 1)
            last_pe_kst[pp] = bld.cnt["pe"]
    wt("vector", "dve", bld.cnt["dve"])
    for g in range(8):
        st("vector", lambda e, g=g: e.reduce_max(
            ST[0:2, 256 + g:257 + g], ST[0:2, 128 + 16 * g:128 + 16 * g + 16], axis=AX.X),
            "dve", 1)
    dve_kn2 = bld.cnt["dve"]
    wt("scalar", "dve", dve_kn2)
    st("scalar", lambda e: e.activation(out=ST[0:2, 264:272], in_=ST[0:2, 256:264],
                                        func=AF.Sqrt), "act", 1)
    act_kn2 = bld.cnt["act"]
    pe_gu_prev = bld.cnt["pe"]
    pe_dn_prev = bld.cnt["pe"]
    last_pe_half = {0: bld.cnt["pe"], 1: bld.cnt["pe"]}
    last_act_gu = {0: 0, 1: 0}
    last_dve_gu = {0: 0, 1: 0}
    last_dve_dn = {0: 0, 1: 0}

    for t in range(NT):
        wt("sync", "act", act_kn2)
        wt("sync", "dve", bld.cnt["dve"])
        st("sync", lambda e, t=t: e.dma_start(out=TAB[:, 0:512],
                                              in_=I["cos2"][:, QT * t:QT * t + 512]), "dt", 16)
        st("sync", lambda e, t=t: e.dma_start(out=TAB[:, 512:1024],
                                              in_=I["sin2"][:, QT * t:QT * t + 512]), "dt", 16)
        dx_tab = bld.cnt["dt"]
        for mi in range(4):
            st("sync", lambda e, mi=mi: e.dma_start(
                out=U1[:, MSK0 + 2048 * mi:MSK0 + 2048 * mi + 2048]
                .rearrange("p (a f) -> p a f", a=4),
                in_=I["mask"][512 * mi:512 * mi + 512, :]
                .rearrange("(a p) f -> p a f", p=128)), "dm", 16)
        dk_msk = bld.cnt["dm"]
        for g in range(8):
            bnk = g % 2
            wt("tensor", "act", bld.cnt["act"])
            wt("tensor", "dve", bld.cnt["dve"])
            if t == 0 and g == 0:
                wt("tensor", "dt", dx_tab)
            for kc in range(2):
                st("tensor", lambda e, g=g, kc=kc, t=t, bnk=bnk: e.matmul(
                    bank(PS, bnk),
                    WR[:, WQQ0 + 128 * (2 * g + kc):WQQ0 + 128 * (2 * g + kc) + 128],
                    QF[:, 512 * (2 * t + kc):512 * (2 * t + kc) + 512],
                    start=(kc == 0), stop=(kc == 1)), "pe", 1)
            pe_qq = bld.cnt["pe"]
            wt("scalar", "pe", pe_qq)
            st("scalar", lambda e, bnk=bnk: e.activation(out=SC2[:, 0:512], in_=bank(PS, bnk),
                                                         func=AF.Copy), "act", 1)
            act_q2 = bld.cnt["act"]
            wt("tensor", "act", act_q2)
            st("tensor", lambda e: e.matmul(bank(PS, 6), CONf[:, FO["r2T"]:FO["r2T"] + 128],
                                            SC2[:, 0:512], start=True, stop=True), "pe", 1)
            pe_rot = bld.cnt["pe"]
            wt("vector", "pe", pe_rot)
            if g == 0:
                wt("vector", "dt", dx_tab)
            st("vector", lambda e: e.tensor_mul(out=SC2[:, 512:1024], in0=SC2[:, 0:512],
                                                in1=TAB[:, 0:512]), "dve", 1)
            wt("vector", "dve", bld.cnt["dve"])
            st("vector", lambda e: e.tensor_mul(out=SC2[:, 0:512], in0=bank(PS, 6),
                                                in1=TAB[:, 512:1024]), "dve", 1)
            wt("vector", "dve", bld.cnt["dve"])
            st("vector", lambda e, g=g: e.tensor_add(
                out=QF[:, 4096 + 512 * g:4096 + 512 * g + 512],
                in0=SC2[:, 0:512], in1=SC2[:, 512:1024]), "dve", 1)
            dve_qq = bld.cnt["dve"]
            wt("scalar", "dve", dve_qq)
            st("scalar", lambda e, g=g: e.activation(
                out=SCb[:, 2048:2560], in_=QF[:, 4096 + 512 * g:4096 + 512 * g + 512],
                func=AF.Square), "act", 1)
            act_sq = bld.cnt["act"]
            wt("tensor", "act", act_sq)
            wt("tensor", "dve", bld.cnt["dve"])
            st("tensor", lambda e: e.matmul(PS[0:2, 512 * 6:512 * 6 + 512],
                                            CONb[:, CO["ones2"]:CO["ones2"] + 2],
                                            SCb[:, 2048:2560], start=True, stop=True), "pe", 1)
            pe_q2m = bld.cnt["pe"]
            wt("vector", "pe", pe_q2m)
            st("vector", lambda e: e.reduce_max(ST[0:2, 272:273],
                                                PS[0:2, 512 * 6:512 * 6 + 512], axis=AX.X),
               "dve", 1)
            wt("scalar", "dve", bld.cnt["dve"])
            st("scalar", lambda e: e.activation(out=ST[0:2, 272:273], in_=ST[0:2, 272:273],
                                                func=AF.Sqrt), "act", 1)
            act_qs = bld.cnt["act"]
            wt("vector", "act", act_qs)
            st("vector", lambda e, g=g: e.tensor_mul(out=ST[0:2, 273:274], in0=ST[0:2, 272:273],
                                                     in1=ST[0:2, 264 + g:265 + g]), "dve", 1)
            wt("vector", "dve", bld.cnt["dve"])
            st("vector", lambda e: e.tensor_scalar_mul(out=ST[0:2, 273:274],
                                                       in0=ST[0:2, 273:274],
                                                       scalar1=-float(SC2s)), "dve", 1)
            wt("vector", "dve", bld.cnt["dve"])
            st("vector", lambda e: e.tensor_copy(out=SCb[0:2, 8:9], in_=ST[0:2, 273:274]),
               "dve", 1)
            dve_bias2 = bld.cnt["dve"]
            wt("tensor", "dve", dve_bias2)
            st("tensor", lambda e: e.matmul(PS[:, 512 * 6:512 * 6 + 1],
                                            CONb[0:2, CO["esel2T"]:CO["esel2T"] + 128],
                                            SCb[0:2, 8:9], start=True, stop=True), "pe", 1)
            pe_b2 = bld.cnt["pe"]
            wt("vector", "pe", pe_b2)
            st("vector", lambda e: e.tensor_copy(out=ST[:, 12:13],
                                                 in_=PS[:, 512 * 6:512 * 6 + 1]), "dve", 1)
            dve_b2 = bld.cnt["dve"]
            first = True
            for m2 in range(t + 1):
                pp = m2 % 2
                wt("sync", "pe", last_pe_kst[pp])
                wt("sync", "dve", dve_b2) if m2 == 0 else None
                for rr in range(4):
                    st("sync", lambda e, rr=rr, g=g, pp=pp, m2=m2: e.dma_start(
                        out=U1[:, KST0 + 2048 * pp + 512 * rr:KST0 + 2048 * pp + 512 * rr + 512],
                        in_=ag["ag2k_o"].ap()[1024 * rr + 128 * g:1024 * rr + 128 * g + 128,
                                              QT * m2:QT * m2 + 512]), "dk%d" % pp, 16)
                    st("sync", lambda e, rr=rr, g=g, pp=pp, m2=m2: e.dma_start(
                        out=U1[:, VST0 + 2048 * pp + 512 * rr:VST0 + 2048 * pp + 512 * rr + 512]
                        .rearrange("p (s f) -> p s f", s=4),
                        in_=ag["ag2v_o"].ap()[TOK * rr + QT * m2:TOK * rr + QT * m2 + 512,
                                              128 * g:128 * g + 128]
                        .rearrange("(s p) f -> p s f", p=128)), "dk%d" % pp, 16)
                dk_kv = bld.cnt["dk%d" % pp]
                for rr in range(4):
                    for kb in range(4):
                        is_last = (m2 == t and rr == 3 and kb == 3)
                        wt("tensor", "dk%d" % pp, dk_kv)
                        wt("tensor", "act", bld.cnt["act"])
                        wt("tensor", "dve", dve_b2)
                        for hh in range(2):
                            st("tensor", lambda e, hh=hh, pp=pp, rr=rr, kb=kb, g=g: e.matmul(
                                bank(PS, hh),
                                U1[64 * hh:64 * hh + 64,
                                   KST0 + 2048 * pp + 512 * rr + 128 * kb:
                                   KST0 + 2048 * pp + 512 * rr + 128 * kb + 128],
                                QF[64 * hh:64 * hh + 64, 4096 + 512 * g:4096 + 512 * g + 512],
                                start=True, stop=True, skip_group_check=True,
                                tile_position=(64 * hh, 0)), "pe", 1)
                        pe_s = bld.cnt["pe"]
                        wt("scalar", "pe", pe_s)
                        st("scalar", lambda e: e.activation(
                            out=U1[:, PT0:PT0 + 1024], in_=PS[:, 0:1024], func=AF.Exp,
                            bias=ST[:, 12:13], scale=float(SC2s)), "act", 1)
                        act_e = bld.cnt["act"]
                        if m2 == t:
                            wt("vector", "act", act_e)
                            if rr == 0 and kb == 0:
                                wt("vector", "dm", dk_msk)
                            for hh in range(2):
                                st("vector", lambda e, hh=hh, rr=rr, kb=kb: e.tensor_mul(
                                    out=U1[:, PT0 + 512 * hh:PT0 + 512 * hh + 512],
                                    in0=U1[:, PT0 + 512 * hh:PT0 + 512 * hh + 512],
                                    in1=U1[:, MSK0 + 512 * (4 * rr + kb):
                                           MSK0 + 512 * (4 * rr + kb) + 512]), "dve", 1)
                            wt("tensor", "dve", bld.cnt["dve"])
                        else:
                            wt("tensor", "act", act_e)
                        for hh in range(2):
                            st("tensor", lambda e, hh=hh, pp=pp, rr=rr, kb=kb, first=first,
                               is_last=is_last: e.matmul(
                                PS[64 * hh:64 * hh + 64, 512 * 4:512 * 4 + 512],
                                U1[:, VST0 + 2048 * pp + 512 * rr + 128 * kb + 64 * hh:
                                   VST0 + 2048 * pp + 512 * rr + 128 * kb + 64 * hh + 64],
                                U1[:, PT0 + 512 * hh:PT0 + 512 * hh + 512],
                                start=first, stop=is_last, skip_group_check=True,
                                tile_position=(0, 64 * hh)), "pe", 1)
                        for hh in range(2):
                            st("tensor", lambda e, hh=hh, first=first, is_last=is_last: e.matmul(
                                PS[0:2, 512 * 5:512 * 5 + 512],
                                CONb[:, CO["oneh2"] + 2 * hh:CO["oneh2"] + 2 * hh + 2],
                                U1[:, PT0 + 512 * hh:PT0 + 512 * hh + 512],
                                start=(first and hh == 0), stop=(is_last and hh == 1),
                                skip_group_check=True), "pe", 1)
                        first = False
                last_pe_kst[pp] = bld.cnt["pe"]
            pe_attn = bld.cnt["pe"]
            wt("scalar", "pe", pe_attn)
            st("scalar", lambda e: e.activation(out=ST[0:2, 1024:1536],
                                                in_=PS[0:2, 512 * 5:512 * 5 + 512],
                                                func=AF.Ln), "act", 1)
            wt("scalar", "act", bld.cnt["act"])
            st("scalar", lambda e: e.activation(out=ST[0:2, 1024:1536], in_=ST[0:2, 1024:1536],
                                                func=AF.Exp, scale=-1.0), "act", 1)
            act_rc = bld.cnt["act"]
            wt("vector", "act", act_rc)
            st("vector", lambda e: e.tensor_copy(out=SCb[0:2, 1024:1536],
                                                 in_=ST[0:2, 1024:1536]), "dve", 1)
            dve_rcb = bld.cnt["dve"]
            wt("tensor", "dve", dve_rcb)
            st("tensor", lambda e: e.matmul(bank(PS, 6),
                                            CONb[0:2, CO["esel2T"]:CO["esel2T"] + 128],
                                            SCb[0:2, 1024:1536], start=True, stop=True), "pe", 1)
            pe_rb = bld.cnt["pe"]
            wt("vector", "pe", pe_rb)
            st("vector", lambda e: e.tensor_copy(out=SC2[:, 0:512], in_=bank(PS, 6)), "dve", 1)
            wt("vector", "dve", bld.cnt["dve"])
            st("vector", lambda e, g=g: e.tensor_mul(
                out=U1[:, YH0 + 512 * g:YH0 + 512 * g + 512], in0=bank(PS, 4),
                in1=SC2[:, 0:512]), "dve", 1)
        dve_y2 = bld.cnt["dve"]
        last_pe_wo = {0: bld.cnt["pe"], 1: bld.cnt["pe"]}
        for oc in range(8):
            dw_o = wload(WR[:, 1024 * (oc % 2):1024 * (oc % 2) + 1024],
                         I["wo"][:, 128 * oc:128 * oc + 128], 1024, 8,
                         wait_pe=last_pe_wo[oc % 2])
            wt("tensor", "pool", dw_o)
            wt("tensor", "dve", bld.cnt["dve"])
            for kc in range(8):
                st("tensor", lambda e, oc=oc, kc=kc: e.matmul(
                    bank(PS, oc % 2),
                    WR[:, 1024 * (oc % 2) + 128 * kc:1024 * (oc % 2) + 128 * kc + 128],
                    U1[:, YH0 + 512 * kc:YH0 + 512 * kc + 512],
                    start=(kc == 0), stop=(kc == 7)), "pe", 1)
            last_pe_wo[oc % 2] = bld.cnt["pe"]
            pe_wo = bld.cnt["pe"]
            wt("vector", "pe", pe_wo)
            st("vector", lambda e, oc=oc, t=t: e.tensor_add(
                out=SC2[:, 512 * (oc % 4):512 * (oc % 4) + 512], in0=bank(PS, oc % 2),
                in1=XR[:, TOK * oc + QT * t:TOK * oc + QT * t + 512]), "dve", 1)
            wt("vector", "dve", bld.cnt["dve"])
            st("vector", lambda e, oc=oc, t=t: e.tensor_add(
                out=XR[:, TOK * oc + QT * t:TOK * oc + QT * t + 512],
                in0=XR[:, TOK * oc + QT * t:TOK * oc + QT * t + 512],
                in1=SC2[:, 512 * (oc % 4):512 * (oc % 4) + 512]), "dve", 1)
            st("vector", lambda e, oc=oc: e.tensor_copy(
                out=XB[:, QT * oc:QT * oc + 512],
                in_=SC2[:, 512 * (oc % 4):512 * (oc % 4) + 512]), "dve", 1)
        emit_stats_norm(bld, M, t, BO["wln2"], "rmsxb")

        wt("gpsimd", "dve", bld.cnt["dve"])
        for kc in range(8):
            st("gpsimd", lambda e, kc=kc: e.dma_start(
                out=U1[:, 22528 + 8 * kc:22528 + 8 * kc + ER],
                in_=I["wrout"][128 * kc:128 * kc + 128, :]), "dwr", 16)
        dw_r = bld.cnt["dwr"]
        wt("tensor", "dwr", dw_r)
        wt("tensor", "dve", bld.cnt["dve"])
        for kc in range(8):
            st("tensor", lambda e, kc=kc: e.matmul(
                PS[0:ER, 512 * 6:512 * 6 + 512], U1[:, 22528 + 8 * kc:22528 + 8 * kc + ER],
                XB[:, QT * kc:QT * (kc + 1)], start=(kc == 0), stop=(kc == 7)), "pe", 1)
        pe_lg = bld.cnt["pe"]
        wt("scalar", "pe", pe_lg)
        st("scalar", lambda e: e.activation(out=SC2[0:ER, 0:512],
                                            in_=PS[0:ER, 512 * 6:512 * 6 + 512],
                                            func=AF.Sigmoid,
                                            bias=CONf[0:ER, BO["brout"]:BO["brout"] + 1]), "act", 1)
        act_pr = bld.cnt["act"]
        wt("tensor", "act", act_pr)
        for s in range(4):
            st("tensor", lambda e, s=s: e.transpose(
                PS[:, 512 * 5 + 128 * s:512 * 5 + 128 * s + 128],
                SC2[:, 128 * s:128 * s + 128],
                CONf[:, FO["identf"]:FO["identf"] + 128]), "pe", 1)
        pe_tp = bld.cnt["pe"]
        wt("vector", "pe", pe_tp)
        for s in range(4):
            st("vector", lambda e, s=s: e.tensor_copy(
                out=SC2[:, 1024 + 8 * s:1024 + 8 * s + ER],
                in_=PS[:, 512 * 5 + 128 * s:512 * 5 + 128 * s + ER]), "dve", 1)
        for s in range(4):
            pb = 1024 + 8 * s
            wt("vector", "dve", bld.cnt["dve"])
            st("vector", lambda e, pb=pb: e.reduce_max(ST[:, 0:1], SC2[:, pb:pb + ER],
                                                       axis=AX.X), "dve", 1)
            wt("vector", "dve", bld.cnt["dve"])
            st("vector", lambda e, pb=pb: e.tensor_scalar(
                out=ST[:, 8:8 + ER], in0=SC2[:, pb:pb + ER], scalar1=ST[:, 0:1],
                scalar2=None, op0=ALU.is_equal), "dve", 1)
            wt("vector", "dve", bld.cnt["dve"])
            st("vector", lambda e, pb=pb: e.scalar_tensor_tensor(
                out=ST[:, 8:8 + ER], in0=ST[:, 8:8 + ER], scalar=-1e30,
                in1=SC2[:, pb:pb + ER], op0=ALU.mult, op1=ALU.add), "dve", 1)
            wt("vector", "dve", bld.cnt["dve"])
            st("vector", lambda e: e.reduce_max(ST[:, 1:2], ST[:, 8:8 + ER], axis=AX.X),
               "dve", 1)
            wt("vector", "dve", bld.cnt["dve"])
            st("vector", lambda e, pb=pb: e.tensor_scalar(
                out=ST[:, 8:8 + ER], in0=SC2[:, pb:pb + ER], scalar1=ST[:, 1:2],
                scalar2=None, op0=ALU.is_ge), "dve", 1)
            wt("vector", "dve", bld.cnt["dve"])
            st("vector", lambda e, pb=pb: e.tensor_mul(
                out=ST[:, 8:8 + ER], in0=ST[:, 8:8 + ER], in1=SC2[:, pb:pb + ER]), "dve", 1)
            wt("vector", "dve", bld.cnt["dve"])
            st("vector", lambda e: e.tensor_add(out=ST[:, 0:1], in0=ST[:, 0:1],
                                                in1=ST[:, 1:2]), "dve", 1)
            dve_d = bld.cnt["dve"]
            wt("scalar", "dve", dve_d)
            st("scalar", lambda e: e.activation(out=ST[:, 1:2], in_=ST[:, 0:1], func=AF.Ln),
               "act", 1)
            wt("scalar", "act", bld.cnt["act"])
            st("scalar", lambda e: e.activation(out=ST[:, 1:2], in_=ST[:, 1:2], func=AF.Exp,
                                                scale=-1.0), "act", 1)
            act_rr2 = bld.cnt["act"]
            wt("vector", "act", act_rr2)
            wt("vector", "dve", bld.cnt["dve"])
            st("vector", lambda e, s=s: e.tensor_scalar(
                out=SC2[:, 1280 + 8 * s:1280 + 8 * s + ER], in0=ST[:, 8:8 + ER],
                scalar1=ST[:, 1:2], scalar2=None, op0=ALU.mult), "dve", 1)
        dve_wts = bld.cnt["dve"]
        wt("vector", "dve", dve_wts)
        for s in range(4):
            st("vector", lambda e, s=s: e.tensor_copy(
                out=SCb[:, 2048 + 8 * s:2048 + 8 * s + ER],
                in_=SC2[:, 1280 + 8 * s:1280 + 8 * s + ER]), "dve", 1)
        dve_wtb = bld.cnt["dve"]
        wt("tensor", "dve", dve_wtb)
        for s in range(4):
            st("tensor", lambda e, s=s: e.transpose(
                PSB[0:ER, 128 * s:128 * s + 128], SCb[:, 2048 + 8 * s:2048 + 8 * s + ER],
                CONb[:, CO["ident"]:CO["ident"] + 128]), "pe", 1)
        pe_wtt = bld.cnt["pe"]
        wt("vector", "pe", pe_wtt)
        st("vector", lambda e: e.tensor_copy(out=SCb[0:ER, 1024:1536], in_=PSB[0:ER, 0:512]),
           "dve", 1)
        dve_wfm = bld.cnt["dve"]

        for net in range(1 + ER):
            wn = ("wgs", "wus", "wds") if net == 0 else ("wgr", "wur", "wdr")
            e_idx = max(0, net - 1)
            if net > 0:
                wt("tensor", "dve", bld.cnt["dve"])
                st("tensor", lambda e, e_idx=e_idx: e.matmul(
                    bank(PS, 6), CONb[0:ER, CO["sele"] + 128 * e_idx:CO["sele"] + 128 * e_idx + 128],
                    SCb[0:ER, 1024:1536], start=True, stop=True), "pe", 1)
                pe_wb = bld.cnt["pe"]
                wt("vector", "pe", pe_wb)
                st("vector", lambda e: e.tensor_copy(out=SCb[:, 2560:3072], in_=bank(PS, 6)),
                   "dve", 1)
            gu_sc = {0: (0, 512), 1: (1536, 2048)}
            for ic in range(8):
                half = ic % 2
                hb = 4096 + 2048 * half
                g_src = (I[wn[0]][:, 128 * ic:128 * ic + 128] if net == 0
                         else I[wn[0]][e_idx, :, 128 * ic:128 * ic + 128])
                u_src = (I[wn[1]][:, 128 * ic:128 * ic + 128] if net == 0
                         else I[wn[1]][e_idx, :, 128 * ic:128 * ic + 128])
                wload(WR[:, hb:hb + 1024], g_src, 1024, 8, wait_pe=last_pe_half[half])
                dw_gu = wload(WR[:, hb + 1024:hb + 2048], u_src, 1024, 8,
                              wait_pe=last_pe_half[half])
                wt("tensor", "pool", dw_gu)
                wt("tensor", "act", last_act_gu[half])
                wt("tensor", "dve", last_dve_gu[half])
                for kc in range(8):
                    st("tensor", lambda e, kc=kc, hb=hb, half=half: e.matmul(
                        bank(PS, 2 * half), WR[:, hb + 128 * kc:hb + 128 * kc + 128],
                        XB[:, QT * kc:QT * (kc + 1)], start=(kc == 0), stop=(kc == 7)), "pe", 1)
                for kc in range(8):
                    st("tensor", lambda e, kc=kc, hb=hb, half=half: e.matmul(
                        bank(PS, 2 * half + 1), WR[:, hb + 1024 + 128 * kc:hb + 1152 + 128 * kc],
                        XB[:, QT * kc:QT * (kc + 1)], start=(kc == 0), stop=(kc == 7)), "pe", 1)
                pe_gu = bld.cnt["pe"]
                pe_gu_prev = pe_gu
                last_pe_half[half] = pe_gu
                wt("scalar", "pe", pe_gu)
                if ic >= 2:
                    wt("scalar", "dve", last_dve_gu[half])
                ss, us = gu_sc[half]
                st("scalar", lambda e, half=half, ss=ss: e.activation(
                    out=SCb[:, ss:ss + 512], in_=bank(PS, 2 * half), func=AF.Silu), "act", 1)
                st("scalar", lambda e, half=half, us=us: e.activation(
                    out=SCb[:, us:us + 512], in_=bank(PS, 2 * half + 1), func=AF.Copy), "act", 1)
                act_gu = bld.cnt["act"]
                last_act_gu[half] = act_gu
                wt("vector", "act", act_gu)
                st("vector", lambda e, ss=ss, us=us: e.tensor_mul(
                    out=SCb[:, ss:ss + 512], in0=SCb[:, ss:ss + 512],
                    in1=SCb[:, us:us + 512]), "dve", 1)
                wt("vector", "dve", bld.cnt["dve"])
                if net > 0:
                    st("vector", lambda e, ic=ic, ss=ss: e.tensor_mul(
                        out=U1[:, 512 * ic:512 * ic + 512], in0=SCb[:, ss:ss + 512],
                        in1=SCb[:, 2560:3072]), "dve", 1)
                else:
                    st("vector", lambda e, ic=ic, ss=ss: e.tensor_copy(
                        out=U1[:, 512 * ic:512 * ic + 512], in_=SCb[:, ss:ss + 512]), "dve", 1)
                last_dve_gu[half] = bld.cnt["dve"]
            dve_hm = bld.cnt["dve"]
            for oc in range(8):
                half = oc % 2
                hb = 4096 + 2048 * half
                d_src = (I[wn[2]][:, 128 * oc:128 * oc + 128] if net == 0
                         else I[wn[2]][e_idx, :, 128 * oc:128 * oc + 128])
                dw_d = wload(WR[:, hb:hb + 1024], d_src, 1024, 8,
                             wait_pe=last_pe_half[half])
                wt("tensor", "pool", dw_d)
                wt("tensor", "dve", max(dve_hm, last_dve_dn[half]))
                for kc in range(8):
                    st("tensor", lambda e, oc=oc, kc=kc, hb=hb: e.matmul(
                        bank(PS, 4 + oc % 2), WR[:, hb + 128 * kc:hb + 128 * kc + 128],
                        U1[:, 512 * kc:512 * kc + 512], start=(kc == 0), stop=(kc == 7)), "pe", 1)
                pe_dn = bld.cnt["pe"]
                pe_dn_prev = pe_dn
                last_pe_half[half] = pe_dn
                wt("vector", "pe", pe_dn)
                if net == 0:
                    st("vector", lambda e, oc=oc: e.tensor_copy(
                        out=TAB[:, ACC0 + 512 * oc:ACC0 + 512 * oc + 512],
                        in_=bank(PS, 4 + oc % 2)), "dve", 1)
                else:
                    st("vector", lambda e, oc=oc: e.tensor_add(
                        out=TAB[:, ACC0 + 512 * oc:ACC0 + 512 * oc + 512],
                        in0=TAB[:, ACC0 + 512 * oc:ACC0 + 512 * oc + 512],
                        in1=bank(PS, 4 + oc % 2)), "dve", 1)
                last_dve_dn[half] = bld.cnt["dve"]
        last_pe_kst = {0: bld.cnt["pe"], 1: bld.cnt["pe"]}
        wt("vector", "dve", bld.cnt["dve"])
        for oc in range(8):
            st("vector", lambda e, oc=oc, t=t: e.tensor_add(
                out=XR[:, TOK * oc + QT * t:TOK * oc + QT * t + 512],
                in0=XR[:, TOK * oc + QT * t:TOK * oc + QT * t + 512],
                in1=TAB[:, ACC0 + 512 * oc:ACC0 + 512 * oc + 512]), "dve", 1)
        emit_stats_norm(bld, M, t, None, "ln")
        dve_fin = bld.cnt["dve"]
        wt("sync", "dve", dve_fin)
        for oc in range(8):
            st("sync", lambda e, oc=oc, t=t: e.dma_start(
                out=y_out[128 * oc:128 * oc + 128, QT * t:QT * t + 512],
                in_=XR[:, TOK * oc + QT * t:TOK * oc + QT * t + 512]), "do", 16)
    wt("sync", "do", bld.cnt["do"])


def _host_inputs(kw):
    f32 = np.float32
    x = np.ascontiguousarray(np.asarray(kw["L0_attn_x"], f32))

    t = np.arange(T, dtype=np.float64)
    inv1 = 1.0 / 10000.0 ** (np.arange(0, L1, 2, dtype=np.float64) / L1)
    fr1 = np.outer(t, inv1)
    cos1 = np.repeat(np.cos(fr1), 2, axis=-1)[:, :L1].astype(f32)
    sin1 = np.repeat(np.sin(fr1), 2, axis=-1)[:, :L1].astype(f32)
    inv2 = 1.0 / 10000.0 ** (np.arange(0, HH2, 2, dtype=np.float64) / HH2)
    fr2 = np.outer(t, inv2)
    emb2 = np.concatenate([fr2, fr2], axis=-1)
    cos2 = np.cos(emb2).astype(f32)
    sin2 = np.sin(emb2).astype(f32)

    R1 = np.zeros((L1, L1), f32)
    for i in range(L1 // 2):
        R1[2 * i, 2 * i + 1] = -1.0
        R1[2 * i + 1, 2 * i] = 1.0
    R1T = np.kron(np.eye(4, dtype=f32), R1.T)
    R2 = np.zeros((HH2, HH2), f32)
    for i in range(HH2 // 2):
        R2[i, HH2 // 2 + i] = -1.0
        R2[HH2 // 2 + i, i] = 1.0
    R2T = np.zeros((128, 128), f32)
    R2T[32:64, 32:64] = R2.T
    R2T[96:128, 96:128] = R2.T

    sh = {}
    sh["wqkv"] = np.ascontiguousarray(np.asarray(kw["L0_attn_c_attn_w"], f32).T)
    sh["bqkv"] = np.ascontiguousarray(np.asarray(kw["L0_attn_c_attn_b"], f32).reshape(24, 128).T)
    sh["q2lT"] = np.ascontiguousarray(np.tile(np.asarray(kw["L0_attn_q2l_w"], f32).T, (2, 1)))
    sh["kv2lT"] = np.ascontiguousarray(np.tile(np.asarray(kw["L0_attn_kv2l_w"], f32).T, (2, 1)))
    sh["bq2l"] = np.ascontiguousarray(np.tile(np.asarray(kw["L0_attn_q2l_b"], f32), 4).reshape(128, 1))
    sh["bkv2l"] = np.ascontiguousarray(np.tile(np.asarray(kw["L0_attn_kv2l_b"], f32), 4).reshape(128, 1))
    sh["wqkT"] = np.ascontiguousarray(np.tile(np.asarray(kw["L0_attn_wqk_w"], f32).T, (4, 1)))
    sh["bwqk"] = np.ascontiguousarray(np.tile(np.asarray(kw["L0_attn_wqk_b"], f32), 4).reshape(128, 1))
    sh["l2oT"] = np.ascontiguousarray(np.tile(np.asarray(kw["L0_attn_l2o_w"], f32).T, (4, 1)))
    sh["bl2o"] = np.ascontiguousarray(np.tile(np.asarray(kw["L0_attn_l2o_b"], f32), 2).reshape(128, 1))
    sh["cprojT"] = np.ascontiguousarray(np.asarray(kw["L0_attn_cproj_w"], f32).T)
    sh["bcproj"] = np.ascontiguousarray(np.asarray(kw["L0_attn_cproj_b"], f32).reshape(8, 128).T)
    sh["wln1"] = np.ascontiguousarray(np.asarray(kw["L0_ffn_w_ln1"], f32).reshape(8, 128).T)
    sh["wln2"] = np.ascontiguousarray(np.asarray(kw["L0_ffn_w_ln2"], f32).reshape(8, 128).T)
    sh["wkvd"] = np.ascontiguousarray(np.asarray(kw["L0_ffn_W_kvd"], f32))
    sh["wqd"] = np.ascontiguousarray(np.asarray(kw["L0_ffn_W_qd"], f32))
    sh["wku"] = np.ascontiguousarray(np.asarray(kw["L0_ffn_W_ku"], f32))
    wqu = np.asarray(kw["L0_ffn_W_qu"], f32)
    wrq = np.asarray(kw["L0_ffn_W_ropeq"], f32)
    wqq = np.zeros((256, 1024), f32)
    for h in range(16):
        wqq[:, 64 * h:64 * h + 32] = wqu[:, 32 * h:32 * h + 32]
        wqq[:, 64 * h + 32:64 * h + 64] = wrq[:, 32 * h:32 * h + 32]
    sh["wqq"] = np.ascontiguousarray(wqq)
    sh["wropek"] = np.ascontiguousarray(np.asarray(kw["L0_ffn_W_ropek"], f32))
    sh["wvu"] = np.ascontiguousarray(np.asarray(kw["L0_ffn_W_vu"], f32))
    sh["wo"] = np.ascontiguousarray(np.asarray(kw["L0_ffn_W_o"], f32))
    sh["wgs"] = np.ascontiguousarray(np.asarray(kw["L0_ffn_Wg_s"], f32))
    sh["wus"] = np.ascontiguousarray(np.asarray(kw["L0_ffn_Wu_s"], f32))
    sh["wds"] = np.ascontiguousarray(np.asarray(kw["L0_ffn_Wd_s"], f32))
    sh["wgr"] = np.ascontiguousarray(np.asarray(kw["L0_ffn_Wg_r"], f32))
    sh["wur"] = np.ascontiguousarray(np.asarray(kw["L0_ffn_Wu_r"], f32))
    sh["wdr"] = np.ascontiguousarray(np.asarray(kw["L0_ffn_Wd_r"], f32))
    sh["wrout"] = np.ascontiguousarray(np.asarray(kw["L0_ffn_W_router"], f32))
    sh["brout"] = np.ascontiguousarray(np.asarray(kw["L0_ffn_routing_bias"], f32).reshape(ER, 1))
    sh["identb"] = np.eye(128).astype(bfdt)
    sh["identf"] = np.eye(128, dtype=f32)
    sh["r1T"] = R1T
    sh["r2T"] = R2T
    o1 = np.zeros((128, 4), f32); e1 = np.zeros((4, 128), f32)
    for h in range(4):
        o1[32 * h:32 * h + 32, h] = 1.0
        e1[h, 32 * h:32 * h + 32] = 1.0
    sh["ones1"] = o1.astype(bfdt); sh["esel1T"] = e1.astype(bfdt)
    o2 = np.zeros((128, 2), f32); e2 = np.zeros((2, 128), f32)
    for h in range(2):
        o2[64 * h:64 * h + 64, h] = 1.0
        e2[h, 64 * h:64 * h + 64] = 1.0
    sh["ones2"] = o2.astype(bfdt); sh["esel2T"] = e2.astype(bfdt)
    sh["onesc"] = np.ones((128, 1), f32).astype(bfdt)
    sh["onesr"] = np.ones((1, 128), f32)
    sh["onesrb"] = np.ones((1, 128), f32).astype(bfdt)
    oh1 = np.zeros((128, 16), f32)
    for h in range(4):
        oh1[:, 4 * h + h] = 1.0
    sh["oneh1"] = oh1.astype(bfdt)
    oh2 = np.zeros((128, 4), f32)
    for h in range(2):
        oh2[:, 2 * h + h] = 1.0
    sh["oneh2"] = oh2.astype(bfdt)
    sele = np.zeros((ER, ER * 128), f32)
    for e in range(ER):
        sele[e, 128 * e:128 * e + 128] = 1.0
    sh["sele"] = sele.astype(bfdt)

    in_maps = []
    row_list = []
    for c in range(8):
        b, rr = c // 4, c % 4
        rows = np.concatenate([np.arange(QT * (4 * j + rr), QT * (4 * j + rr + 1))
                               for j in range(NT)])
        row_list.append((b, rows))
        m = dict(sh)
        m["x"] = np.ascontiguousarray(x[b][rows].T)
        m["cos1"] = np.ascontiguousarray(np.tile(cos1[rows].T, (4, 1)))
        m["sin1"] = np.ascontiguousarray(np.tile(sin1[rows].T, (4, 1)))
        c2t = np.ones((128, TOK), f32); s2t = np.zeros((128, TOK), f32)
        c2t[32:64] = cos2[rows].T; s2t[32:64] = sin2[rows].T
        c2t[96:128] = cos2[rows].T; s2t[96:128] = sin2[rows].T
        m["cos2"] = np.ascontiguousarray(c2t)
        m["sin2"] = np.ascontiguousarray(s2t)
        msk = np.zeros((16, 128, QT), f32)
        for rp in range(4):
            for s in range(4):
                kpos = 512 * rp + 128 * s + np.arange(128)[:, None]
                qpos = 512 * rr + np.arange(QT)[None, :]
                msk[4 * rp + s] = (kpos <= qpos).astype(f32)
        m["mask"] = np.ascontiguousarray(msk.reshape(16 * 128, QT).astype(bfdt))
        in_maps.append(m)
    return in_maps, row_list


_NC_CACHE = None


def kernel(**kw):
    global _NC_CACHE
    in_maps, row_list = _host_inputs(kw)
    if _NC_CACHE is None:
        _NC_CACHE = build_nc()
    nc = _NC_CACHE
    res = run_bass_kernel_spmd(nc, in_maps, core_ids=list(range(8)))
    out = np.zeros((B, T, C), np.float32)
    for c in range(8):
        b, rows = row_list[c]
        out[b][rows] = res.results[c]["y"].T
    return out


if __name__ == "__main__":
    nc = build_nc()
    n = sum(len(bb.instructions) for bb in nc.m.basic_blocks) if hasattr(nc, "m") else -1
    print("built ok")
